# version 13
# speedup vs baseline: 1.0838x; 1.0327x over previous
import numpy as np
from contextlib import ExitStack

import concourse.bass as bass
import concourse.mybir as mybir
from concourse.bass_utils import run_bass_kernel_spmd
from concourse.alu_op_type import AluOpType as ALU

F32 = mybir.dt.float32
BF16 = mybir.dt.bfloat16
I32 = mybir.dt.int32
AF = mybir.ActivationFunctionType
AX = mybir.AxisListType

B, T, C = 4, 8192, 1024
H, D, CS = 16, 64, 128
NT, NCH, HID, SLAB, NSLAB = 4096, 32, 4096, 512, 8

DEBUG = []
_dbg_out = None
STAGE = 99
SUB = 9
FEAT = {'xt','proj','knorm','chunks','quads','quad4','sevac','masktt','x0','solve','post','qhat'}


class P:
    def __init__(self, nc, es):
        self.nc, self.es, self.c = nc, es, {}

    def sem(self, name):
        h = self.es.enter_context(self.nc.semaphore(name))
        self.c[name] = 0
        setattr(self, name, h)

    def inc(self, name, instr, n=1):
        instr.then_inc(getattr(self, name), n)
        self.c[name] += n

    def wait(self, eng, name, val=None):
        eng.wait_ge(getattr(self, name), self.c[name] if val is None else val)


def build(dbg=()):
    nc = bass.Bass("TRN2", target_bir_lowering=False)
    es = ExitStack()
    p = P(nc, es)

    xin = nc.dram_tensor("xin", [NT, C], F32, kind="ExternalInput").ap()
    wq_d = nc.dram_tensor("wq", [C, C], F32, kind="ExternalInput").ap()
    wk_d = nc.dram_tensor("wk", [C, C], F32, kind="ExternalInput").ap()
    wv_d = nc.dram_tensor("wv", [C, C], F32, kind="ExternalInput").ap()
    wo_d = nc.dram_tensor("wo", [C, C], F32, kind="ExternalInput").ap()
    w1t_d = nc.dram_tensor("w1t", [C, HID], F32, kind="ExternalInput").ap()
    w2t_d = nc.dram_tensor("w2t", [HID, C], F32, kind="ExternalInput").ap()
    alpha_d = nc.dram_tensor("alpha", [1, H], F32, kind="ExternalInput").ap()
    beta_d = nc.dram_tensor("beta", [1, H], F32, kind="ExternalInput").ap()
    lng_d = nc.dram_tensor("lng", [1, C], F32, kind="ExternalInput").ap()
    lnb_d = nc.dram_tensor("lnb", [1, C], F32, kind="ExternalInput").ap()
    ident_d = nc.dram_tensor("identin", [128, 128], F32, kind="ExternalInput").ap()
    ltm_d = nc.dram_tensor("ltmin", [128, 128], F32, kind="ExternalInput").ap()
    utm4_d = nc.dram_tensor("utm4in", [128, 512], F32, kind="ExternalInput").ap()
    hselT_d = nc.dram_tensor("hselTin", [16, C], F32, kind="ExternalInput").ap()
    hsel_d = nc.dram_tensor("hselin", [C, 16], F32, kind="ExternalInput").ap()
    ones_d = nc.dram_tensor("onesin", [128, 128], F32, kind="ExternalInput").ap()
    selc_d = nc.dram_tensor("selcin", [64, 1], F32, kind="ExternalInput").ap()
    eo_d = nc.dram_tensor("eoin", [128, 2], F32, kind="ExternalInput").ap()
    qtarg_d = nc.dram_tensor("qtargin", [1, 16], F32, kind="ExternalInput").ap()
    out_d = nc.dram_tensor("out", [NT, C], F32, kind="ExternalOutput").ap()
    dbg_d = {nm: nc.dram_tensor(nm, list(sh), dt, kind="ExternalOutput").ap()
             for nm, sh, dt in dbg}

    qhat_sp = nc.dram_tensor("qhat_sp", [NCH, 64, H * CS], BF16).ap()
    yint_sp = nc.dram_tensor("yint_sp", [NCH, 128, H * D], F32).ap()
    gh_sp = nc.dram_tensor("gh_sp", [NCH, 64, H * D], BF16).ap()
    bh_sp = nc.dram_tensor("bh_sp", [NCH, 64, H * D], BF16).ap()
    n_sp = nc.dram_tensor("n_sp", [NCH, 64, H * D], BF16).ap()
    x2_sp = nc.dram_tensor("x2_sp", [NT, C], F32).ap()
    h1_sp = nc.dram_tensor("h1_sp", [NT, C], F32).ap()
    g_sp = nc.dram_tensor("g_sp", [NSLAB, HID, SLAB], F32).ap()
    pool2_d = nc.dram_tensor("pool2_d", [128, 8192], F32).ap()

    cc_nf_in = nc.dram_tensor("cc_nf_in", [64, H * D], F32).ap()
    cc_nf_out = nc.dram_tensor("cc_nf_out", [128, H * D], F32).ap()

    def cpair(nm, rows, cols, out_rows=None):
        return (nc.dram_tensor(nm + "_i", [rows, cols], F32).ap(),
                nc.dram_tensor(nm + "_o", [out_rows or rows, cols], F32).ap())

    cc_th = [cpair(f"cc_th{q}", 1, 8) for q in range(2)]
    cc_ct = [cpair(f"cc_ct{q}", 1, 8) for q in range(2)]
    cc_e = [[cpair(f"cc_e{q}_{i}", 1, 16) for i in range(4)] for q in range(2)]
    cc_gc = [cpair(f"cc_gc{q}", 128, 16, out_rows=1024) for q in range(2)]
    RG_ALL = [list(range(8))]
    RG_PAIR = [[0, 1], [2, 3], [4, 5], [6, 7]]

    for s in ["s_ld", "s_w", "s_mm", "s_ev", "s_sc", "s_sp", "s_cc", "s_out", "s_gp",
              "s_gs0", "s_gs1", "s_ps0", "s_ps1", "s_cr"]:
        p.sem(s)

    sbl = lambda name, shape, dt: es.enter_context(nc.sbuf_tensor(name, shape, dt))

    ident = sbl("ident", [128, 128], BF16)
    ones32 = sbl("ones32", [128, 128], F32)
    ident32 = sbl("ident32", [128, 128], F32)
    selcol = sbl("selcol", [64, 1], F32)
    lngb = sbl("lngb", [128, 2 * C], F32)
    stats = sbl("stats", [128, 64], F32)
    qrow = sbl("qrow", [1, 256], F32)
    qcol = sbl("qcol", [128, 16], F32)
    zbuf = sbl("zbuf", [128, C], F32) if STAGE < 5 else None
    LSattc = ExitStack()
    sbac = lambda name, shape, dt: LSattc.enter_context(nc.sbuf_tensor(name, shape, dt))
    ltm_all = sbac("ltm_all", [128, H * 128], BF16)
    utm_all = sbac("utm_all", [128, H * 128], BF16)
    sup_mask = sbac("sup_mask", [128, 512], F32)
    hselT = sbac("hselT", [16, C], F32)
    hsel = sbac("hsel", [128, 8 * 16], F32)
    ab_row = sbac("ab_row", [1, 4 * H], F32)
    acol = sbac("acol", [128, 4 * H], F32)
    ltm32 = sbac("ltm32", [128, 128], F32)
    eomask = sbac("eomask", [128, 2], F32)

    ps = [es.enter_context(nc.psum_tensor(f"ps{i}", [128, 512], F32)) for i in range(6)]
    pt = es.enter_context(nc.psum_tensor("pt", [128, 1024], BF16))
    BLK = es.enter_context(nc.Block())

    def fence(v):
        p.wait(v, "s_ev")

    if True:
        blk = BLK
        @blk.sync
        def _(s):
            for dst, srcd in [(ones32, ones_d), (sup_mask, utm4_d), (hselT, hselT_d),
                             (ltm32, ltm_d), (selcol, selc_d)]:
                p.inc("s_w", s.dma_start(out=dst[:, :], in_=srcd[:, :]), 16)
            p.inc("s_w", s.dma_start(out=hsel.rearrange("p (k h) -> p k h", h=16),
                                     in_=hsel_d.rearrange("(k p) h -> p k h", p=128)), 16)
            p.inc("s_w", s.dma_start(out=ab_row[0:1, 0:H], in_=alpha_d[:, :]), 16)
            p.inc("s_w", s.dma_start(out=ab_row[0:1, H:2 * H], in_=beta_d[:, :]), 16)
            p.inc("s_w", s.dma_start(out=qrow[0:1, 0:16], in_=qtarg_d[:, :]), 16)
            p.inc("s_w", s.dma_start(out=lngb[0:1, 0:C], in_=lng_d[:, :]), 16)
            p.inc("s_w", s.dma_start(out=lngb[0:1, C:2 * C], in_=lnb_d[:, :]), 16)
            p.inc("s_w", s.dma_start(out=eomask[:, :], in_=eo_d[:, :]), 16)
            p.inc("s_w", s.dma_start(out=ident32[:, :], in_=ident_d[:, :]), 16)

        @blk.gpsimd
        def _(g):
            p.inc("s_gp", g.dma_start(out=ident[:, :], in_=ident_d[:, :]), 16)

        @blk.vector
        def _(v):
            p.inc("s_ev", v.memset(qrow[0:1, 16:256], 0.0))

    if True:

        blk = BLK
        @blk.scalar
        def _(s):
            p.wait(s, "s_w")
            p.inc("s_sc", s.activation(ab_row[0:1, 0:2 * H], ab_row[0:1, 0:2 * H],
                                       AF.Sigmoid))
            p.wait(s, "s_sc")
            p.inc("s_sc", s.mul(ab_row[0:1, 0:2 * H], ab_row[0:1, 0:2 * H], 0.5))
            p.wait(s, "s_sc")
            p.inc("s_sc", s.mul(ab_row[0:1, 2 * H:4 * H], ab_row[0:1, 0:2 * H], -1.0))

        @blk.tensor
        def _(t):
            p.wait(t, "s_sc")
            p.inc("s_mm", t.matmul(ps[0][:, 0:4 * H], ones32[0:1, :], ab_row[0:1, :],
                                   start=True, stop=True))

        @blk.vector
        def _(v):
            p.wait(v, "s_mm")
            p.inc("s_ev", v.tensor_copy(acol[:, :], ps[0][:, 0:4 * H]))
            p.wait(v, "s_ev")
            for hh in range(H):
                na = acol[:, 2 * H + hh:2 * H + hh + 1]
                v.tensor_scalar(ltm_all[:, hh * 128:(hh + 1) * 128], ltm32[:, :],
                                na, None, op0=ALU.mult)
            last = None
            for hh in range(H):
                na = acol[:, 2 * H + hh:2 * H + hh + 1]
                last = v.tensor_scalar(utm_all[:, hh * 128:(hh + 1) * 128],
                                       sup_mask[:, 0:128], na, None, op0=ALU.mult)
            p.inc("s_ev", last)

    for j in range(4):
        if True:
            blk = BLK
            @blk.tensor
            def _(t, j=j):
                p.wait(t, "s_w")
                p.wait(t, "s_ev")
                p.inc("s_mm", t.matmul(ps[1][:, :], ones32[0:1, :],
                                       lngb[0:1, j * 512:(j + 1) * 512],
                                       start=True, stop=True))

            @blk.vector
            def _(v, j=j):
                p.wait(v, "s_mm")
                p.inc("s_ev", v.tensor_copy(lngb[:, j * 512:(j + 1) * 512], ps[1][:, :]))

    LSw = ExitStack()
    sbw = lambda name, shape, dt: LSw.enter_context(nc.sbuf_tensor(name, shape, dt))
    LS3h = ExitStack()
    habsA = LS3h.enter_context(nc.sbuf_tensor("habsA", [128, 4 * C], BF16))
    LSwo = ExitStack()
    wo_sb = LSwo.enter_context(nc.sbuf_tensor("wo_sb", [128, 8 * C], BF16))
    wq_sb = sbw("wq_sb", [128, 8 * C], BF16)
    wk_sb = sbw("wk_sb", [128, 8 * C], BF16)
    wv_sb = sbw("wv_sb", [128, 8 * C], BF16)
    if True:
        blk = BLK
        @blk.gpsimd
        def _(g):
            for w_sb, w_d in [(wq_sb, wq_d), (wk_sb, wk_d), (wv_sb, wv_d), (wo_sb, wo_d)]:
                p.inc("s_gp", g.dma_start(out=w_sb.rearrange("p (k o) -> p k o", o=C),
                                         in_=w_d.rearrange("(k p) o -> p k o", p=128)), 16)
    W_READY = p.c["s_gp"]

    LS1 = ExitStack()
    sb1 = lambda name, shape, dt: LS1.enter_context(nc.sbuf_tensor(name, shape, dt))
    xsl = sb1("xsl", [128, 4 * C], F32)
    scr32 = sb1("scr32", [128, 2 * C], F32)
    xlnb = sb1("xlnb", [128, 4 * C], BF16)
    xlnT = sb1("xlnT", [128, 8 * SLAB], BF16)
    q_fm = sb1("q_fm", [128, 8 * SLAB], BF16)
    k_fm = sb1("k_fm", [128, 8 * SLAB], BF16)
    kt_fm = sb1("kt_fm", [128, 8 * SLAB], BF16)
    kto = sb1("kto", [128, 8 * SLAB], BF16)
    v_tok = sb1("v_tok", [128, 4 * C], BF16)
    ktok = sb1("ktok", [128, C], BF16)
    xq = sb1("xq", [128, SLAB], BF16)
    pl_a = sb1("pl_a", [128, SLAB], BF16)
    pu_a = sb1("pu_a", [128, SLAB], BF16)
    pl_b = sb1("pl_b", [128, SLAB], BF16)
    pu_b = sb1("pu_b", [128, SLAB], BF16)
    naw = sb1("naw", [128, 256], BF16)
    rbq = sb1("rbq", [128, 256], BF16)
    sup_q = sb1("sup_q", [128, SLAB], BF16)
    pl_c = sb1("pl_c", [128, SLAB], BF16)
    pu_c = sb1("pu_c", [128, SLAB], BF16)
    pl_d = sb1("pl_d", [128, SLAB], BF16)
    pu_d = sb1("pu_d", [128, SLAB], BF16)
    xq2 = sb1("xq2", [128, SLAB], BF16)
    naw2 = sb1("naw2", [128, 256], BF16)
    rbq2 = sb1("rbq2", [128, 256], BF16)
    sup_q2 = sb1("sup_q2", [128, SLAB], BF16)
    ghq = sb1("ghq", [64, H * D], BF16)
    bhq = sb1("bhq", [64, H * D], BF16)
    qhats = sb1("qhats", [64, H * CS], BF16)
    yints = sb1("yints", [128, H * D], F32)

    h1sp_sn = {}

    def p1_chunk(s, ccq, cc):
        for half in range(2):
            if True:
                blk = BLK
                @blk.tensor
                def _(t, half=half):
                    p.wait(t, "s_sc")
                    last = None
                    for j in range(4):
                        kk = half * 4 + j
                        last = t.transpose(pt[:, j * 256:j * 256 + 128],
                                           kt_fm[:, kk * SLAB + ccq * 128:
                                                 kk * SLAB + (ccq + 1) * 128],
                                           ident[:, :])
                    p.inc("s_mm", last)

                @blk.vector
                def _(v, half=half):
                    p.wait(v, "s_mm")
                    last = None
                    for j in range(4):
                        kk = half * 4 + j
                        last = v.tensor_copy(ktok[:, kk * 128:(kk + 1) * 128],
                                       pt[:, j * 256:j * 256 + 128])
                    p.inc("s_sc", last)
        KT_DONE = p.c["s_sc"]
        if 'quads' not in FEAT:
            return

        for qp in range(2):
            qa, qb = 2 * qp, 2 * qp + 1
            SETS = [
                dict(qd=qa, heads=[4 * qa + j for j in range(4)],
                     pl0=pl_a, pu0=pu_a, pl1=pl_b, pu1=pu_b, xqt=xq,
                     nawt=naw, rbqt=rbq, supt=sup_q, so=0, po=512,
                     bX=ps[1], bL=ps[2], bU=ps[3], bGh=ps[4], bPq=ps[0]),
                dict(qd=qb, heads=[4 * qb + j for j in range(4)],
                     pl0=pl_c, pu0=pu_c, pl1=pl_d, pu1=pu_d, xqt=xq2,
                     nawt=naw2, rbqt=rbq2, supt=sup_q2, so=1024, po=1536,
                     bX=ps[4], bL=ps[5], bU=ps[0], bGh=ps[5], bPq=ps[2]),
            ]
            blk = BLK

            def st_mms(t, S):
                last = None
                for j, hh in enumerate(S['heads']):
                    kk = hh // 2
                    ksel = (k_fm if hh % 2 == 0 else kto)[
                        :, kk * SLAB + ccq * 128:kk * SLAB + (ccq + 1) * 128]
                    kful = kt_fm[:, kk * SLAB + ccq * 128:kk * SLAB + (ccq + 1) * 128]
                    last = t.matmul(ps[0][:, j * 128:(j + 1) * 128], ksel, kful,
                                    start=True, stop=True)
                p.inc("s_mm", last)

            def masks_x0(v, S):
                qd = S['qd']
                v.tensor_tensor(S['pl0'][:, :], scr32[:, S['so']:S['so'] + 512],
                                ltm_all[:, qd * 512:(qd + 1) * 512], op=ALU.mult)
                p.inc("s_ev", v.tensor_tensor(
                    S['pu0'][:, :], scr32[:, S['so']:S['so'] + 512],
                    utm_all[:, qd * 512:(qd + 1) * 512], op=ALU.mult))
                v.tensor_copy(
                    S['xqt'].rearrange("p (j two d) -> p j two d", two=2, d=64)[:, :, 0, :],
                    ktok.rearrange("p (j d) -> p j d", d=64)[:, qd * 4:qd * 4 + 4, :])
                p.inc("s_ev", v.tensor_copy(
                    S['xqt'].rearrange("p (j two d) -> p j two d", two=2, d=64)[:, :, 1, :],
                    v_tok.rearrange("p (cq j d) -> p cq j d", cq=4, d=64)
                    [:, ccq, qd * 4:qd * 4 + 4, :]))

            @blk.tensor
            def _(t):
                p.wait(t, "s_sc")
                p.wait(t, "s_ev")
                st_mms(t, SETS[0])
            stA = p.c["s_mm"]

            @blk.vector
            def _(v):
                p.wait(v, "s_mm", stA)
                p.inc("s_ev", v.tensor_copy(scr32[:, 0:512], ps[0][:, :]))
            stAe = p.c["s_ev"]

            @blk.tensor
            def _(t):
                p.wait(t, "s_ev", stAe)
                st_mms(t, SETS[1])
            stB = p.c["s_mm"]

            @blk.vector
            def _(v):
                fence(v)
                masks_x0(v, SETS[0])
            xA = p.c["s_ev"]

            @blk.vector
            def _(v):
                p.wait(v, "s_mm", stB)
                p.inc("s_ev", v.tensor_copy(scr32[:, 1024:1536], ps[0][:, :]))
                fence(v)
                masks_x0(v, SETS[1])
            xB = p.c["s_ev"]

            nA = nB = None
            for lvl in range(7):
                for si, S in enumerate(SETS):
                    cl, cu = (S['pl0'], S['pu0']) if lvl % 2 == 0 else (S['pl1'], S['pu1'])
                    nl, nu = (S['pl1'], S['pu1']) if lvl % 2 == 0 else (S['pl0'], S['pu0'])

                    @blk.tensor
                    def _(t, S=S, cl=cl, cu=cu, lvl=lvl,
                          xs=(xA if si == 0 else xB),
                          ns=(nA if si == 0 else nB)):
                        p.wait(t, "s_ev", xs)
                        if ns is not None:
                            p.wait(t, "s_sc", ns)
                        bX, bL, bU = S['bX'], S['bL'], S['bU']
                        last = None
                        for j in range(4):
                            last = t.matmul(bX[:, j * 128:(j + 1) * 128],
                                            cu[:, j * 128:(j + 1) * 128],
                                            S['xqt'][:, j * 128:(j + 1) * 128],
                                            start=True, stop=True)
                        if lvl < 6:
                            for j in range(4):
                                t.matmul(bL[:, j * 128:(j + 1) * 128],
                                         cu[:, j * 128:(j + 1) * 128],
                                         cl[:, j * 128:(j + 1) * 128],
                                         start=True, stop=True)
                            for j in range(4):
                                last = t.matmul(bU[:, j * 128:(j + 1) * 128],
                                                cl[:, j * 128:(j + 1) * 128],
                                                cu[:, j * 128:(j + 1) * 128],
                                                start=True, stop=True)
                        p.inc("s_mm", last)
                    msnap = p.c["s_mm"]

                    @blk.vector
                    def _(v, S=S, msnap=msnap, nu=nu, lvl=lvl):
                        p.wait(v, "s_mm", msnap)
                        v.tensor_tensor(S['xqt'][:, :], S['xqt'][:, :],
                                        S['bX'][:, :], op=ALU.add)
                        if lvl < 6:
                            p.inc("s_ev", v.tensor_copy(nu[:, :], S['bU'][:, :]))
                        else:
                            p.inc("s_ev", v.engine_nop())
                    if si == 0:
                        xA = p.c["s_ev"]
                    else:
                        xB = p.c["s_ev"]

                    if lvl < 6:
                        @blk.scalar
                        def _(sc, S=S, msnap=msnap, nl=nl):
                            p.wait(sc, "s_mm", msnap)
                            p.inc("s_sc", sc.copy(nl[:, :], S['bL'][:, :]))
                        if si == 0:
                            nA = p.c["s_sc"]
                        else:
                            nB = p.c["s_sc"]

            nw = [None, None]
            for si, S in enumerate(SETS):
                @blk.vector
                def _(v, S=S):
                    fence(v)
                    for j, hh in enumerate(S['heads']):
                        v.tensor_scalar(S['nawt'][:, j * 64:(j + 1) * 64],
                                        S['xqt'][:, j * 128:j * 128 + 64],
                                        acol[:, 2 * H + hh:2 * H + hh + 1],
                                        None, op0=ALU.mult)
                    last = None
                    for j, hh in enumerate(S['heads']):
                        last = v.tensor_scalar(S['rbqt'][:, j * 64:(j + 1) * 64],
                                               S['xqt'][:, j * 128 + 64:(j + 1) * 128],
                                               acol[:, H + hh:H + hh + 1],
                                               None, op0=ALU.mult)
                    p.inc("s_ev", last)
                nw[si] = p.c["s_ev"]

            gsn = [None, None]
            sup_sn = [None, None]
            for si, S in enumerate(SETS):
                @blk.tensor
                def _(t, S=S, si=si):
                    p.wait(t, "s_ev", nw[si])
                    p.wait(t, "s_sc", nB)
                    bGh, bPq = S['bGh'], S['bPq']
                    for j, hh in enumerate(S['heads']):
                        t.matmul(bGh[0:64, j * 64:(j + 1) * 64],
                                 S['nawt'][:, j * 64:(j + 1) * 64],
                                 ktok[:, hh * 64:(hh + 1) * 64],
                                 start=True, stop=True)
                    for j, hh in enumerate(S['heads']):
                        t.matmul(bGh[0:64, 256 + j * 64:256 + (j + 1) * 64],
                                 ktok[:, hh * 64:(hh + 1) * 64],
                                 S['rbqt'][:, j * 64:(j + 1) * 64],
                                 start=True, stop=True)
                    last = None
                    for j, hh in enumerate(S['heads']):
                        kk = hh // 2
                        kslice = (k_fm if hh % 2 == 0 else kto)[
                            :, kk * SLAB + ccq * 128:kk * SLAB + (ccq + 1) * 128]
                        qslice = q_fm[:, kk * SLAB + ccq * 128:kk * SLAB + (ccq + 1) * 128]
                        last = t.matmul(bPq[:, j * 128:(j + 1) * 128],
                                        kslice, qslice, start=True, stop=True)
                    p.inc("s_mm", last)
                gsn[si] = p.c["s_mm"]

                @blk.vector
                def _(v, S=S, si=si):
                    p.wait(v, "s_mm", gsn[si])
                    p.wait(v, "s_sp")
                    qd = S['qd']
                    v.tensor_copy(ghq[:, qd * 256:(qd + 1) * 256], S['bGh'][0:64, 0:256])
                    p.inc("s_sc", v.tensor_copy(bhq[:, qd * 256:(qd + 1) * 256],
                                                S['bGh'][0:64, 256:512]))

                @blk.vector
                def _(v, S=S, si=si):
                    p.wait(v, "s_mm", gsn[si])
                    p.inc("s_ev", v.tensor_copy(scr32[:, S['po']:S['po'] + 512],
                                                S['bPq'][:, :]))
                    fence(v)
                    p.inc("s_ev", v.tensor_tensor(S['supt'][:, :],
                                                  scr32[:, S['po']:S['po'] + 512],
                                                  sup_mask[:, :], op=ALU.mult))
                sup_sn[si] = p.c["s_ev"]

            prev_q = None
            for si, S in enumerate(SETS):
                @blk.tensor
                def _(t, S=S, si=si, prev_q=prev_q):
                    p.wait(t, "s_ev", sup_sn[si] if si == 0 else p.c["s_ev"])
                    p.wait(t, "s_sc", nB if si == 0 else prev_q)
                    for j, hh in enumerate(S['heads']):
                        kk, hf = hh // 2, (hh % 2) * 64
                        qtile = q_fm[:, kk * SLAB + ccq * 128:kk * SLAB + (ccq + 1) * 128]
                        t.matmul(ps[1][0:64, j * 128:(j + 1) * 128],
                                 ident[:, hf:hf + 64], qtile, start=True, stop=False)
                        t.matmul(ps[1][0:64, j * 128:(j + 1) * 128],
                                 S['nawt'][:, j * 64:(j + 1) * 64],
                                 S['supt'][:, j * 128:(j + 1) * 128],
                                 start=False, stop=True)
                    last = None
                    for j in range(4):
                        last = t.matmul(ps[3][:, j * 64:(j + 1) * 64],
                                        S['supt'][:, j * 128:(j + 1) * 128],
                                        S['rbqt'][:, j * 64:(j + 1) * 64],
                                        start=True, stop=True)
                    p.inc("s_mm", last)
                qsn = p.c["s_mm"]

                @blk.vector
                def _(v, S=S, qsn=qsn):
                    p.wait(v, "s_mm", qsn)
                    p.wait(v, "s_sp")
                    qd = S['qd']
                    p.inc("s_sc", v.tensor_copy(qhats[:, qd * 512:(qd + 1) * 512],
                                                ps[1][0:64, :]))
                    p.inc("s_ev", v.tensor_copy(yints[:, qd * 256:(qd + 1) * 256],
                                                ps[3][:, 0:256]))
                prev_q = p.c["s_sc"]

        SP_READY_SC = p.c["s_sc"]
        SP_READY_EV = p.c["s_ev"]
        if True:
            blk = BLK
            @blk.sync
            def _(sy):
                p.wait(sy, "s_sc", SP_READY_SC)
                p.wait(sy, "s_ev", SP_READY_EV)
                p.inc("s_sp", sy.dma_start(out=qhat_sp[cc], in_=qhats[:, :]), 16)
                p.inc("s_sp", sy.dma_start(out=yint_sp[cc], in_=yints[:, :]), 16)
                p.inc("s_sp", sy.dma_start(out=gh_sp[cc], in_=ghq[:, :]), 16)
                p.inc("s_sp", sy.dma_start(out=bh_sp[cc], in_=bhq[:, :]), 16)

    def p1_slab(s):
        if True:
            blk = BLK
            @blk.sync
            def _(sy):
                p.wait(sy, "s_ev")
                for ccq in range(4):
                    p.inc("s_ld", sy.dma_start(
                        out=xsl[:, ccq * C:(ccq + 1) * C],
                        in_=xin[(s * 4 + ccq) * 128:(s * 4 + ccq + 1) * 128, :]), 16)

        if True:
            blk = BLK
            @blk.vector
            def _(v):
                p.wait(v, "s_ld")
                if s == 0:
                    p.wait(v, "s_ev")
                for ccq in range(4):
                    xt = xsl[:, ccq * C:(ccq + 1) * C]
                    mcol = stats[:, ccq:ccq + 1]
                    p.inc("s_ev", v.reduce_sum(mcol, xt, axis=AX.X))
                    p.wait(v, "s_ev")
                    p.inc("s_ev", v.tensor_scalar(mcol, mcol, 1.0 / C, None,
                                                  op0=ALU.mult))
                    p.wait(v, "s_ev")
                    p.inc("s_ev", v.tensor_scalar(xt, xt, mcol, None,
                                                  op0=ALU.subtract))

            @blk.vector
            def _(v):
                p.wait(v, "s_ev")
                p.inc("s_ev", v.tensor_tensor(scr32[:, :], xsl[:, 0:2 * C],
                                              xsl[:, 0:2 * C], op=ALU.mult))
                fence(v)
                last = None
                for ccq in range(2):
                    last = v.reduce_sum(stats[:, 4 + ccq:5 + ccq],
                                 scr32[:, ccq * C:(ccq + 1) * C], axis=AX.X)
                p.inc("s_ev", last)

            @blk.vector
            def _(v):
                p.wait(v, "s_ev")
                p.inc("s_ev", v.tensor_tensor(scr32[:, :], xsl[:, 2 * C:4 * C],
                                              xsl[:, 2 * C:4 * C], op=ALU.mult))
                fence(v)
                last = None
                for ccq in range(2, 4):
                    last = v.reduce_sum(stats[:, 4 + ccq:5 + ccq],
                                 scr32[:, (ccq - 2) * C:(ccq - 1) * C], axis=AX.X)
                p.inc("s_ev", last)
                p.wait(v, "s_ev")
                p.inc("s_ev", v.tensor_scalar(stats[:, 8:12], stats[:, 4:8],
                                              1.0 / C, 1e-5, op0=ALU.mult, op1=ALU.add))

            @blk.vector
            def _(v):
                p.wait(v, "s_ev")
                p.inc("s_ev", v.reciprocal(stats[:, 8:12], stats[:, 8:12]))

            @blk.scalar
            def _(sc):
                p.wait(sc, "s_ev")
                p.inc("s_sc", sc.activation(stats[:, 8:12], stats[:, 8:12], AF.Sqrt))

            @blk.vector
            def _(v):
                p.wait(v, "s_sc")
                last = None
                for ccq in range(4):
                    last = v.tensor_scalar(xlnb[:, ccq * C:(ccq + 1) * C],
                                           xsl[:, ccq * C:(ccq + 1) * C],
                                           stats[:, 8 + ccq:9 + ccq], None,
                                           op0=ALU.mult)
                p.inc("s_ev", last)
        LN_DONE = p.c["s_ev"]

        if 'xt' not in FEAT:
            return
        blocks = [(ccq, kk) for ccq in range(4) for kk in range(8)]
        for bi in range(0, 32, 4):
            if True:
                blk = BLK
                @blk.tensor
                def _(t, bi=bi):
                    p.wait(t, "s_ev", LN_DONE)
                    if s == 0 and bi == 0:
                        p.wait(t, "s_gp", W_READY)
                    p.wait(t, "s_sc")
                    last = None
                    for j in range(4):
                        ccq, kk = blocks[bi + j]
                        last = t.transpose(
                            pt[:, j * 256:j * 256 + 128],
                            xlnb[:, ccq * C + kk * 128:ccq * C + (kk + 1) * 128],
                            ident[:, :])
                    p.inc("s_mm", last)

                @blk.vector
                def _(v, bi=bi):
                    p.wait(v, "s_mm")
                    last = None
                    for j in range(4):
                        ccq, kk = blocks[bi + j]
                        last = v.tensor_copy(xlnT[:, kk * SLAB + ccq * 128:
                                            kk * SLAB + (ccq + 1) * 128],
                                       pt[:, j * 256:j * 256 + 128])
                    p.inc("s_sc", last)
        XT_DONE = p.c["s_sc"]

        if 'proj' not in FEAT:
            return
        for wsb, dst in [(wq_sb, q_fm), (wk_sb, k_fm)]:
            for mo in range(8):
                bank = ps[mo % 2]
                if True:
                    blk = BLK
                    @blk.tensor
                    def _(t, mo=mo, wsb=wsb, bank=bank):
                        p.wait(t, "s_sc", XT_DONE)
                        p.wait(t, "s_ev")
                        last = None
                        for kk in range(8):
                            last = t.matmul(
                                bank[:, :],
                                wsb[:, kk * C + mo * 128:kk * C + (mo + 1) * 128],
                                xlnT[:, kk * SLAB:(kk + 1) * SLAB],
                                start=(kk == 0), stop=(kk == 7))
                        p.inc("s_mm", last)

                    @blk.vector
                    def _(v, mo=mo, dst=dst, bank=bank):
                        p.wait(v, "s_mm")
                        p.inc("s_ev", v.tensor_copy(dst[:, mo * SLAB:(mo + 1) * SLAB],
                                                    bank[:, :]))

        for ccq in range(4):
            for ob in range(2):
                bank = ps[2 + (ob % 2)]
                if True:
                    blk = BLK
                    @blk.tensor
                    def _(t, ccq=ccq, ob=ob, bank=bank):
                        p.wait(t, "s_ev")
                        p.wait(t, "s_sc")
                        last = None
                        for kk in range(8):
                            last = t.matmul(
                                bank[:, :],
                                xlnT[:, kk * SLAB + ccq * 128:kk * SLAB + (ccq + 1) * 128],
                                wv_sb[:, kk * C + ob * 512:kk * C + (ob + 1) * 512],
                                start=(kk == 0), stop=(kk == 7))
                        p.inc("s_mm", last)

                    @blk.vector
                    def _(v, ccq=ccq, ob=ob, bank=bank):
                        p.wait(v, "s_mm")
                        p.inc("s_sc", v.tensor_copy(
                            v_tok[:, ccq * C + ob * 512:ccq * C + (ob + 1) * 512],
                            bank[:, :]))

        if 'knorm' not in FEAT:
            return
        if True:
            blk = BLK
            @blk.vector
            def _(v):
                p.wait(v, "s_ev")
                p.inc("s_ev", v.tensor_tensor(scr32[:, :], k_fm[:, 0:4 * SLAB],
                                              k_fm[:, 0:4 * SLAB], op=ALU.mult))

            @blk.tensor
            def _(t):
                p.wait(t, "s_ev")
                last = None
                for kk in range(4):
                    last = t.matmul(ps[4][0:16, :], hsel[:, kk * 16:(kk + 1) * 16],
                                    scr32[:, kk * SLAB:(kk + 1) * SLAB],
                                    start=(kk == 0), stop=False)
                p.inc("s_mm", last)

            @blk.vector
            def _(v):
                p.wait(v, "s_mm")
                p.inc("s_ev", v.tensor_tensor(scr32[:, :], k_fm[:, 4 * SLAB:8 * SLAB],
                                              k_fm[:, 4 * SLAB:8 * SLAB], op=ALU.mult))

            @blk.tensor
            def _(t):
                p.wait(t, "s_ev")
                last = None
                for kk in range(4):
                    last = t.matmul(ps[4][0:16, :], hsel[:, (4 + kk) * 16:(5 + kk) * 16],
                                    scr32[:, kk * SLAB:(kk + 1) * SLAB],
                                    start=False, stop=(kk == 3))
                p.inc("s_mm", last)

            @blk.vector
            def _(v):
                p.wait(v, "s_mm")
                p.inc("s_ev", v.tensor_scalar(scr32[0:16, 0:SLAB],
                                              ps[4][0:16, :], 1e-24, None, op0=ALU.max))

            @blk.vector
            def _(v):
                p.wait(v, "s_ev")
                p.inc("s_ev", v.reciprocal(scr32[0:16, 0:SLAB], scr32[0:16, 0:SLAB]))

            @blk.scalar
            def _(sc):
                p.wait(sc, "s_ev")
                p.inc("s_sc", sc.activation(scr32[0:16, 0:SLAB], scr32[0:16, 0:SLAB], AF.Sqrt))

        for kk in range(8):
            bank = ps[2 + (kk % 2)]
            if True:
                blk = BLK
                @blk.tensor
                def _(t, kk=kk, bank=bank):
                    p.wait(t, "s_sc")
                    p.wait(t, "s_ev")
                    p.inc("s_mm", t.matmul(bank[:, :], hselT[:, kk * 128:(kk + 1) * 128],
                                           scr32[0:16, 0:SLAB], start=True, stop=True))

                @blk.vector
                def _(v, kk=kk, bank=bank):
                    p.wait(v, "s_mm")
                    p.inc("s_ev", v.tensor_tensor(kt_fm[:, kk * SLAB:(kk + 1) * SLAB],
                                                  k_fm[:, kk * SLAB:(kk + 1) * SLAB],
                                                  bank[:, :], op=ALU.mult))
                    p.wait(v, "s_ev")
                    v.tensor_scalar(k_fm[:, kk * SLAB:(kk + 1) * SLAB],
                                    kt_fm[:, kk * SLAB:(kk + 1) * SLAB],
                                    eomask[:, 0:1], None, op0=ALU.mult)
                    p.inc("s_ev", v.tensor_scalar(kto[:, kk * SLAB:(kk + 1) * SLAB],
                                                  kt_fm[:, kk * SLAB:(kk + 1) * SLAB],
                                                  eomask[:, 1:2], None, op0=ALU.mult))

        if 'chunks' in FEAT:
            for ccq in range(4):
                p1_chunk(s, ccq, s * 4 + ccq)

    nslab_run = NSLAB if STAGE >= 2 else (1 if STAGE >= 1 else 0)
    for s in range(nslab_run):
        p1_slab(s)

    def dbg_store(name, src_ap, rows=None):
        if name in dbg_d:
            if True:
                blk = BLK
                @blk.gpsimd
                def _(sy):
                    p.wait(sy, "s_ev")
                    p.wait(sy, "s_sc")
                    p.wait(sy, "s_mm")
                    p.inc("s_gp", sy.dma_start(out=dbg_d[name][:, :], in_=src_ap), 16)

    dbg_store("d_qfm", q_fm[:, :])
    dbg_store("d_ktfm", kt_fm[:, :])
    dbg_store("d_vtok", v_tok[:, :])
    dbg_store("d_xq", xq[:, :])
    dbg_store("d_ghq", ghq[:, :])
    dbg_store("d_bhq", bhq[:, :])
    dbg_store("d_qhats", qhats[:, :])
    dbg_store("d_yints", yints[:, :])
    LS1.close()
    LSw.close()

    if STAGE >= 3:
        LS2 = ExitStack()
        sb2 = lambda name, shape, dt: LS2.enter_context(nc.sbuf_tensor(name, shape, dt))
        ghb = sb2("ghb", [64, 2 * H * D], BF16)
        bhb = sb2("bhb", [64, 2 * H * D], BF16)
        nst = sb2("nst", [64, 2 * H * D], BF16)
        nb_s = sb2("nb_s", [64, H * D], BF16)
        nfin = sb2("nfin", [64, H * D], F32)
        ninit = sb2("ninit", [64, H * D], F32)

        def scan_pass(save):
            if True:
                blk = BLK
                @blk.vector
                def _(v):
                    p.wait(v, "s_cc")
                    p.wait(v, "s_ld")
                    p.wait(v, "s_cr")
                    if not save:
                        p.inc("s_ev", v.memset(nst[:, 0:H * D], 0.0))
                    else:
                        p.inc("s_ev", v.tensor_scalar(nst[:, 0:H * D], ninit[:, :],
                                                      selcol[:, 0:1], None, op0=ALU.mult))
                @blk.sync
                def _(sy):
                    p.wait(sy, "s_ev")
                    if save:
                        p.inc("s_sp", sy.dma_start(out=n_sp[0], in_=nst[:, 0:H * D]), 16)
            for cc in range(NCH):
                cur, nxt = cc % 2, (cc + 1) % 2
                if True:
                    blk = BLK
                    @blk.sync
                    def _(sy):
                        p.wait(sy, "s_sc")
                        p.wait(sy, "s_sp")
                        p.inc("s_ld", sy.dma_start(out=ghb[:, cur * H * D:(cur + 1) * H * D],
                                                   in_=gh_sp[cc]), 16)
                        p.inc("s_ld", sy.dma_start(out=bhb[:, cur * H * D:(cur + 1) * H * D],
                                                   in_=bh_sp[cc]), 16)

                    @blk.vector
                    def _(v):
                        p.wait(v, "s_ld")
                        p.wait(v, "s_ev")
                        p.inc("s_ev", v.tensor_tensor(nb_s[:, :], nst[:, cur * H * D:(cur + 1) * H * D],
                                                      bhb[:, cur * H * D:(cur + 1) * H * D],
                                                      op=ALU.add))

                    @blk.tensor
                    def _(t):
                        p.wait(t, "s_ev")
                        p.wait(t, "s_sc")
                        last = None
                        for hh in range(H):
                            reg = ps[4 if hh < 8 else 5][0:64,
                                  (hh % 8) * 64:(hh % 8 + 1) * 64]
                            t.matmul(reg, ghb[:, cur * H * D + hh * D:cur * H * D + (hh + 1) * D],
                                     nst[:, cur * H * D + hh * D:cur * H * D + (hh + 1) * D],
                                     start=True, stop=False)
                            last = t.matmul(reg, ident[0:64, 0:64],
                                            nb_s[:, hh * D:(hh + 1) * D],
                                            start=False, stop=True)
                        p.inc("s_mm", last)

                    @blk.vector
                    def _(v):
                        p.wait(v, "s_mm")
                        v.tensor_copy(nst[:, nxt * H * D:nxt * H * D + 512], ps[4][0:64, :])
                        p.inc("s_sc", v.tensor_copy(nst[:, nxt * H * D + 512:(nxt + 1) * H * D],
                                              ps[5][0:64, :]))

                    @blk.sync
                    def _(sy, cc=cc):
                        p.wait(sy, "s_sc")
                        if save and cc < NCH - 1:
                            p.inc("s_sp", sy.dma_start(out=n_sp[cc + 1],
                                                       in_=nst[:, nxt * H * D:(nxt + 1) * H * D]), 16)

            fin = (NCH % 2)
            return fin

        fin = scan_pass(save=False)
        if True:
            blk = BLK
            @blk.vector
            def _(v):
                p.wait(v, "s_sc")
                p.inc("s_ev", v.tensor_copy(nfin[:, :], nst[:, fin * H * D:(fin + 1) * H * D]))

            @blk.sync
            def _(sy):
                p.wait(sy, "s_ev")
                p.inc("s_sp", sy.dma_start(out=cc_nf_in[:, :], in_=nfin[:, :]), 16)

            @blk.gpsimd
            def _(g):
                p.wait(g, "s_sp")
                p.inc("s_cc", g.collective_compute(
                    "AllGather", ALU.bypass, replica_groups=RG_PAIR,
                    ins=[cc_nf_in[:, :]], outs=[cc_nf_out[:, :]]), 1)

            @blk.sync
            def _(sy):
                p.wait(sy, "s_cc")
                p.inc("s_cr", sy.dma_start(out=ninit[:, :], in_=cc_nf_out[0:64, :]), 16)

        if True:

            blk = BLK
            @blk.vector
            def _(v):
                p.wait(v, "s_ld")
                p.inc("s_ev", v.engine_nop())
        scan_pass(save=True)
        LS2.close()

    if STAGE >= 4:
        LS3 = ExitStack()
        sb3 = lambda name, shape, dt: LS3.enter_context(nc.sbuf_tensor(name, shape, dt))
        qhb = sb3("qhb", [64, H * CS], BF16)
        yib = sb3("yib", [128, H * D], F32)
        n0b = sb3("n0b", [64, H * D], BF16)
        xrb = sb3("xrb", [128, C], F32)
        ych = sb3("ych", [128, C], BF16)
        yT = sb3("yT", [128, C], BF16)
        p3s = sb3("p3s", [128, 4 * C], F32)

        for cc in range(NCH):
            if True:
                blk = BLK
                @blk.sync
                def _(sy, cc=cc):
                    p.wait(sy, "s_ev")
                    p.wait(sy, "s_sc")
                    p.wait(sy, "s_sp")
                    p.inc("s_ld", sy.dma_start(out=qhb[:, :], in_=qhat_sp[cc]), 16)
                    p.inc("s_ld", sy.dma_start(out=yib[:, :], in_=yint_sp[cc]), 16)
                    p.inc("s_ld", sy.dma_start(out=n0b[:, :], in_=n_sp[cc]), 16)
                    p.inc("s_ld", sy.dma_start(out=xrb[:, :],
                                               in_=xin[cc * 128:(cc + 1) * 128, :]), 16)

                @blk.tensor
                def _(t):
                    p.wait(t, "s_ld")
                    p.wait(t, "s_ev")
                    last = None
                    for hh in range(H):
                        reg = ps[0 if hh < 8 else 1][:, (hh % 8) * 64:(hh % 8 + 1) * 64]
                        last = t.matmul(reg, qhb[:, hh * CS:(hh + 1) * CS],
                                        n0b[:, hh * D:(hh + 1) * D],
                                        start=True, stop=True)
                    p.inc("s_mm", last)

                @blk.vector
                def _(v):
                    p.wait(v, "s_mm")
                    v.tensor_tensor(ych[:, 0:512], ps[0][:, :], yib[:, 0:512], op=ALU.add)
                    p.inc("s_ev", v.tensor_tensor(ych[:, 512:1024], ps[1][:, :],
                                                  yib[:, 512:1024], op=ALU.add))
            Y_DONE = p.c["s_ev"]

            for half in range(2):
                if True:
                    blk = BLK
                    @blk.tensor
                    def _(t, half=half):
                        p.wait(t, "s_ev", Y_DONE)
                        p.wait(t, "s_sc")
                        last = None
                        for j in range(4):
                            kk = half * 4 + j
                            last = t.transpose(pt[:, j * 256:j * 256 + 128],
                                               ych[:, kk * 128:(kk + 1) * 128],
                                               ident[:, :])
                        p.inc("s_mm", last)

                    @blk.vector
                    def _(v, half=half):
                        p.wait(v, "s_mm")
                        last = None
                        for j in range(4):
                            kk = half * 4 + j
                            last = v.tensor_copy(yT[:, kk * 128:(kk + 1) * 128],
                                           pt[:, j * 256:j * 256 + 128])
                        p.inc("s_sc", last)
            YT_DONE = p.c["s_sc"]

            if True:
                blk = BLK
                @blk.tensor
                def _(t):
                    p.wait(t, "s_sc", YT_DONE)
                    p.wait(t, "s_ev")
                    last = None
                    for ob in range(2):
                        for kk in range(8):
                            last = t.matmul(ps[2 + ob][:, :],
                                            yT[:, kk * 128:(kk + 1) * 128],
                                            wo_sb[:, kk * C + ob * 512:kk * C + (ob + 1) * 512],
                                            start=(kk == 0), stop=(kk == 7))
                    p.inc("s_mm", last)

                @blk.vector
                def _(v):
                    p.wait(v, "s_mm")
                    v.tensor_copy(p3s[:, 0:512], ps[2][:, :])
                    p.inc("s_sc", v.tensor_copy(p3s[:, 512:1024], ps[3][:, :]))

            def ln_stats(v, sc, seg, mc, vc):
                pass

            if True:

                blk = BLK
                @blk.vector
                def _(v):
                    p.wait(v, "s_sc")
                    att = p3s[:, 0:1024]
                    p.inc("s_ev", v.reduce_sum(stats[:, 0:1], att, axis=AX.X))
                    p.wait(v, "s_ev")
                    p.inc("s_ev", v.tensor_scalar(stats[:, 0:1], stats[:, 0:1],
                                                  1.0 / C, None, op0=ALU.mult))
                    p.wait(v, "s_ev")
                    p.inc("s_ev", v.tensor_scalar(att, att, stats[:, 0:1], None,
                                                  op0=ALU.subtract))

                @blk.vector
                def _(v):
                    p.wait(v, "s_ev")
                    p.inc("s_ev", v.tensor_tensor(p3s[:, 3072:4096], p3s[:, 0:1024],
                                                  p3s[:, 0:1024], op=ALU.mult))
                    fence(v)
                    p.inc("s_ev", v.reduce_sum(stats[:, 1:2], p3s[:, 3072:4096], axis=AX.X))
                    p.wait(v, "s_ev")
                    p.inc("s_ev", v.tensor_scalar(stats[:, 1:2], stats[:, 1:2],
                                                  1.0 / C, 1e-5, op0=ALU.mult, op1=ALU.add))

                @blk.vector
                def _(v):
                    p.wait(v, "s_ev")
                    p.inc("s_ev", v.reciprocal(stats[:, 1:2], stats[:, 1:2]))

                @blk.scalar
                def _(sc):
                    p.wait(sc, "s_ev")
                    p.inc("s_sc", sc.activation(stats[:, 1:2], stats[:, 1:2], AF.Sqrt))

                @blk.vector
                def _(v):
                    p.wait(v, "s_sc")
                    p.wait(v, "s_sp")
                    p.inc("s_ev", v.tensor_scalar(p3s[:, 0:1024], p3s[:, 0:1024],
                                                  stats[:, 1:2], None, op0=ALU.mult))
                    p.wait(v, "s_ev")
                    p.inc("s_ev", v.tensor_tensor(p3s[:, 1024:2048], xrb[:, :],
                                                  p3s[:, 0:1024], op=ALU.add))

            if True:
                blk = BLK
                @blk.vector
                def _(v):
                    p.wait(v, "s_ev")
                    x2 = p3s[:, 1024:2048]
                    p.inc("s_ev", v.reduce_sum(stats[:, 2:3], x2, axis=AX.X))
                    p.wait(v, "s_ev")
                    p.inc("s_ev", v.tensor_scalar(stats[:, 2:3], stats[:, 2:3],
                                                  1.0 / C, None, op0=ALU.mult))
                    p.wait(v, "s_ev")
                    p.inc("s_ev", v.tensor_scalar(p3s[:, 2048:3072], x2, stats[:, 2:3],
                                                  None, op0=ALU.subtract))

                @blk.vector
                def _(v):
                    p.wait(v, "s_ev")
                    p.inc("s_ev", v.tensor_tensor(p3s[:, 3072:4096], p3s[:, 2048:3072],
                                                  p3s[:, 2048:3072], op=ALU.mult))
                    fence(v)
                    p.inc("s_ev", v.reduce_sum(stats[:, 3:4], p3s[:, 3072:4096], axis=AX.X))
                    p.wait(v, "s_ev")
                    p.inc("s_ev", v.tensor_scalar(stats[:, 3:4], stats[:, 3:4],
                                                  1.0 / C, 1e-5, op0=ALU.mult, op1=ALU.add))

                @blk.vector
                def _(v):
                    p.wait(v, "s_ev")
                    p.inc("s_ev", v.reciprocal(stats[:, 3:4], stats[:, 3:4]))

                @blk.scalar
                def _(sc):
                    p.wait(sc, "s_ev")
                    p.inc("s_sc", sc.activation(stats[:, 3:4], stats[:, 3:4], AF.Sqrt))

                @blk.vector
                def _(v):
                    p.wait(v, "s_sc")
                    z = p3s[:, 2048:3072]
                    p.inc("s_ev", v.tensor_scalar(z, z, stats[:, 3:4], None, op0=ALU.mult))
                    p.wait(v, "s_ev")
                    p.inc("s_ev", v.tensor_tensor(z, z, lngb[:, 0:C], op=ALU.mult))
                    p.wait(v, "s_ev")
                    p.inc("s_ev", v.tensor_tensor(z, z, lngb[:, C:2 * C], op=ALU.add))
                @blk.vector
                def _(v, cc=cc):
                    p.wait(v, "s_ev")
                    p.inc("s_ev", v.tensor_scalar(p3s[:, 3072:4096], p3s[:, 2048:3072],
                                                  -1.0, None, op0=ALU.mult))
                    fence(v)
                    p.inc("s_ev", v.tensor_tensor(p3s[:, 3072:4096], p3s[:, 3072:4096],
                                                  p3s[:, 2048:3072], op=ALU.max))
                    if cc < 4:
                        fence(v)
                        p.inc("s_ev", v.tensor_copy(habsA[:, cc * C:(cc + 1) * C],
                                                    p3s[:, 3072:4096]))
                    p.inc("s_sc", v.engine_nop())

                @blk.sync
                def _(sy, cc=cc):
                    p.wait(sy, "s_ev")
                    p.inc("s_sp", sy.dma_start(out=x2_sp[cc * 128:(cc + 1) * 128, :],
                                               in_=p3s[:, 1024:2048]), 16)
                    p.inc("s_sp", sy.dma_start(out=h1_sp[cc * 128:(cc + 1) * 128, :],
                                               in_=p3s[:, 2048:3072]), 16)
                h1sp_sn[cc] = p.c["s_sp"]
        if "d_x2c" in dbg_d:
            if True:
                blk = BLK
                @blk.gpsimd
                def _(g):
                    p.wait(g, "s_ev")
                    p.wait(g, "s_sp")
                    p.inc("s_gp", g.dma_start(out=dbg_d["d_x2c"][:, :],
                                              in_=p3s[:, 1024:2048]), 16)
                    p.inc("s_gp", g.dma_start(out=dbg_d["d_h1c"][:, :],
                                              in_=p3s[:, 2048:3072]), 16)
                    p.inc("s_gp", g.dma_start(out=dbg_d["d_n0"][:, :], in_=n_sp[8]), 16)
        if True:
            blk = BLK
            @blk.vector
            def _(v):
                p.wait(v, "s_sp")
                p.wait(v, "s_gp")
                p.inc("s_ev", v.engine_nop())
        LS3.close()
        LSwo.close()




    if STAGE < 4:
        LSwo.close()
    if STAGE < 5:
        LS3h.close()
        LSattc.close()
    if STAGE >= 5:
        LS4s = ExitStack()
        sb4s = lambda name, shape, dt: LS4s.enter_context(nc.sbuf_tensor(name, shape, dt))
        witmp = sb4s("witmp", [128, HID], I32)
        wacc = sb4s("wacc", [128, 40], F32)
        qtr = sb4s("qtr", [128, 4 * C], F32)
        cand = sb4s("cand", [128, 128], F32)
        negc = sb4s("negc", [128, 128], F32)
        ttmp = sb4s("ttmp", [128, 128], F32)
        LS4 = ExitStack()
        sb4 = lambda name, shape, dt: LS4.enter_context(nc.sbuf_tensor(name, shape, dt))
        t1_sb = sb4("t1_sb", [128, 8 * HID], BF16)
        gsA = sb4("gsA", [128, 16384], BF16)
        gt32 = sb4("gt32", [128, 2 * 512], F32)
        pstg = sb4("pstg", [128, 64], F32)
        n1T = sb4("n1T", [128, 4096], BF16)
        n1stage = sb4("n1stage", [128, C], BF16)

        hbuf = sb4("hbuf", [128, 2 * C], F32)
        pool1 = sb4("pool1", [128, 2048], F32)
        QS = 20

        def v_ss(v, dst, a, s1, s2, o0, o1=None):
            if o1 is None:
                p.inc("s_ev", v.tensor_scalar(dst, a, s1, None, op0=o0))
            else:
                p.inc("s_ev", v.tensor_scalar(dst, a, s1, s2, op0=o0, op1=o1))
            fence(v)

        def v_tt(v, dst, a, b, op):
            p.inc("s_ev", v.tensor_tensor(dst, a, b, op=op))
            fence(v)

        def bcast_row_to_cols(row_ap, col_ap, n):
            if True:
                blk = BLK
                @blk.tensor
                def _(t):
                    p.wait(t, "s_ev")
                    p.wait(t, "s_sc")
                    p.inc("s_mm", t.matmul(ps[5][:, 0:n], ones32[0:1, :], row_ap,
                                           start=True, stop=True))

                @blk.vector
                def _(v):
                    p.wait(v, "s_mm")
                    p.inc("s_ev", v.tensor_copy(col_ap, ps[5][:, 0:n]))

        def cross_sum_cols(col_ap, out_row, n):
            if True:
                blk = BLK
                @blk.tensor
                def _(t):
                    p.wait(t, "s_ev")
                    p.inc("s_mm", t.matmul(ps[5][0:1, 0:n], ones32[:, 0:1], col_ap,
                                           start=True, stop=True))

                @blk.vector
                def _(v):
                    p.wait(v, "s_mm")
                    p.inc("s_ev", v.tensor_copy(out_row, ps[5][0:1, 0:n]))

        def cross_max_col(col_ap, out_11):
            if True:
                blk = BLK
                @blk.vector
                def _(v):
                    p.wait(v, "s_ev")
                    p.inc("s_ev", v.memset(ttmp[:, :], 0.0))
                    fence(v)
                    p.inc("s_ev", v.tensor_copy(ttmp[:, 0:1], col_ap))

                @blk.tensor
                def _(t):
                    p.wait(t, "s_ev")
                    p.inc("s_mm", t.transpose(ps[5][:, 0:128], ttmp[:, :], ident32[:, :]))

                @blk.vector
                def _(v):
                    p.wait(v, "s_mm")
                    p.inc("s_ev", v.reduce_max(out_11, ps[5][0:1, 0:128], axis=AX.X))

        def count_gt(segs, thr_col, out_11):
            pieces = []
            for ap, fd in segs:
                off = 0
                while off < fd:
                    w = min(4096, fd - off)
                    pieces.append((ap[:, off:off + w] if (off or w < fd) else ap, w))
                    off += w
            if True:
                blk = BLK
                @blk.vector
                def _(v):
                    p.wait(v, "s_ev")
                    for i, (ap, w) in enumerate(pieces):
                        if i:
                            fence(v)
                        p.inc("s_ev", v.tensor_scalar(qtr[:, 0:w], ap, thr_col, 0.0,
                                                      op0=ALU.is_gt, op1=ALU.add,
                                                      accum_out=wacc[:, 32 + i:33 + i]))

                @blk.tensor
                def _(t):
                    p.wait(t, "s_ev")
                    p.inc("s_mm", t.matmul(ps[5][0:1, 0:len(pieces)], ones32[:, 0:1],
                                           wacc[:, 32:32 + len(pieces)],
                                           start=True, stop=True))

                @blk.vector
                def _(v):
                    p.wait(v, "s_mm")
                    p.inc("s_ev", v.reduce_sum(out_11, ps[5][0:1, 0:len(pieces)],
                                               axis=AX.X))

        def emit_bisect(segs, rounds, lo, hi, target, out_ap):
            for r in range(rounds):
                if True:
                    blk = BLK
                    @blk.vector
                    def _(v):
                        p.wait(v, "s_ev")
                        v_tt(v, qrow[0:1, QS:QS + 1], lo, hi, ALU.add)
                        v_ss(v, qrow[0:1, QS:QS + 1], qrow[0:1, QS:QS + 1], 0.5, None,
                             ALU.mult)
                bcast_row_to_cols(qrow[0:1, QS:QS + 1], qcol[:, 0:1], 1)
                count_gt(segs, qcol[:, 0:1], qrow[0:1, QS + 1:QS + 2])
                if True:
                    blk = BLK
                    @blk.vector
                    def _(v):
                        p.wait(v, "s_ev")
                        v_tt(v, qrow[0:1, QS + 2:QS + 3], qrow[0:1, QS + 1:QS + 2],
                             target, ALU.is_gt)
                        v_tt(v, qrow[0:1, QS + 3:QS + 4], qrow[0:1, QS:QS + 1], lo,
                             ALU.subtract)
                        v_tt(v, qrow[0:1, QS + 3:QS + 4], qrow[0:1, QS + 3:QS + 4],
                             qrow[0:1, QS + 2:QS + 3], ALU.mult)
                        v_tt(v, lo, lo, qrow[0:1, QS + 3:QS + 4], ALU.add)
                        v_tt(v, qrow[0:1, QS + 4:QS + 5], qrow[0:1, QS:QS + 1], hi,
                             ALU.subtract)
                        v_ss(v, qrow[0:1, QS + 5:QS + 6], qrow[0:1, QS + 2:QS + 3],
                             -1.0, 1.0, ALU.mult, ALU.add)
                        v_tt(v, qrow[0:1, QS + 4:QS + 5], qrow[0:1, QS + 4:QS + 5],
                             qrow[0:1, QS + 5:QS + 6], ALU.mult)
                        v_tt(v, hi, hi, qrow[0:1, QS + 4:QS + 5], ALU.add)
            if out_ap is not None:
                if True:
                    blk = BLK
                    @blk.vector
                    def _(v):
                        p.wait(v, "s_ev")
                        v_tt(v, out_ap, lo, hi, ALU.add)
                        v_ss(v, out_ap, out_ap, 0.5, None, ALU.mult)

        def allreduce_row(pair, width, scale=None, cols=8):
            cin, cout = pair
            if True:
                blk = BLK
                @blk.sync
                def _(sy):
                    p.wait(sy, "s_ev")
                    p.inc("s_sp", sy.dma_start(out=cin[0:1, 0:cols],
                                               in_=qrow[0:1, 40:40 + cols]), 16)

                @blk.gpsimd
                def _(g):
                    p.wait(g, "s_sp")
                    p.inc("s_cc", g.collective_compute(
                        "AllReduce", ALU.add, replica_groups=RG_ALL,
                        ins=[cin[:, :]], outs=[cout[:, :]]), 1)

                @blk.sync
                def _(sy):
                    p.wait(sy, "s_cc")
                    p.inc("s_cr", sy.dma_start(out=qrow[0:1, 40:40 + width],
                                               in_=cout[0:1, 0:width]), 16)

                @blk.vector
                def _(v):
                    p.wait(v, "s_cr")
                    if scale is not None:
                        v_ss(v, qrow[0:1, 40:40 + width], qrow[0:1, 40:40 + width],
                             scale, None, ALU.mult)
                    else:
                        p.inc("s_ev", v.engine_nop())

        def emit_quantile_tail(qidx, pool_ap, pool_fd, gather_pair, ka_slot,
                               frac_slot, s_out_col):
            if True:
                blk = BLK
                @blk.vector
                def _(v):
                    p.wait(v, "s_ev")
                    p.inc("s_ev", v.tensor_copy(qrow[0:1, 40:46], qrow[0:1, 72:78]))
            allreduce_row(cc_ct[qidx], 6)
            if True:
                blk = BLK
                @blk.vector
                def _(v):
                    p.wait(v, "s_ev")
                    p.inc("s_ev", v.tensor_copy(qrow[0:1, 80:86], qrow[0:1, 40:46]))
                    fence(v)
                    p.inc("s_ev", v.tensor_copy(qrow[0:1, 35:36], qrow[0:1, 85:86]))
                    fence(v)
                    v_tt(v, qrow[0:1, 36:37], qrow[0:1, ka_slot:ka_slot + 1],
                         qrow[0:1, 35:36], ALU.subtract)
                    v_ss(v, qrow[0:1, 20:21], qrow[0:1, ka_slot:ka_slot + 1],
                         1.0, None, ALU.add)
                    p.inc("s_ev", v.tensor_scalar(qrow[0:1, 96:102], qrow[0:1, 80:86],
                                                  qrow[0:1, 20:21], None, op0=ALU.is_ge))
                    fence(v)
                    v_tt(v, qrow[0:1, 104:110], qrow[0:1, 64:70], qrow[0:1, 96:102],
                         ALU.mult)
                    p.inc("s_ev", v.tensor_reduce(qrow[0:1, 30:31], qrow[0:1, 104:110],
                                                  axis=AX.X, op=ALU.max))
                    fence(v)
                    v_ss(v, qrow[0:1, 20:21], qrow[0:1, ka_slot:ka_slot + 1],
                         -1.0, None, ALU.add)
                    p.inc("s_ev", v.tensor_scalar(qrow[0:1, 96:102], qrow[0:1, 80:86],
                                                  qrow[0:1, 20:21], None, op0=ALU.is_le))
                    fence(v)
                    v_tt(v, qrow[0:1, 104:110], qrow[0:1, 64:70], qrow[0:1, 96:102],
                         ALU.mult)
                    v_ss(v, qrow[0:1, 96:102], qrow[0:1, 96:102], -1e9, 1e9,
                         ALU.mult, ALU.add)
                    v_tt(v, qrow[0:1, 104:110], qrow[0:1, 104:110], qrow[0:1, 96:102],
                         ALU.add)
                    p.inc("s_ev", v.tensor_reduce(qrow[0:1, 31:32], qrow[0:1, 104:110],
                                                  axis=AX.X, op=ALU.min))
                    fence(v)
            for er in range(3):
                if True:
                    blk = BLK
                    @blk.vector
                    def _(v):
                        p.wait(v, "s_ev")
                        v_tt(v, qrow[0:1, 20:21], qrow[0:1, 31:32], qrow[0:1, 30:31],
                             ALU.subtract)
                        for j in range(8):
                            v_ss(v, qrow[0:1, 88 + j:89 + j], qrow[0:1, 20:21],
                                 (j + 1) / 9.0, None, ALU.mult)
                            v_tt(v, qrow[0:1, 88 + j:89 + j], qrow[0:1, 88 + j:89 + j],
                                 qrow[0:1, 30:31], ALU.add)
                for j in range(8):
                    bcast_row_to_cols(qrow[0:1, 88 + j:89 + j], qcol[:, 1:2], 1)
                    count_gt([(pool_ap, pool_fd)], qcol[:, 1:2], qrow[0:1, 40 + j:41 + j])
                allreduce_row(cc_e[qidx][er], 8, cols=16)
                if True:
                    blk = BLK
                    @blk.vector
                    def _(v):
                        p.wait(v, "s_ev")
                        p.inc("s_ev", v.tensor_copy(qrow[0:1, 96:104], qrow[0:1, 40:48]))
                        fence(v)
                        v_ss(v, qrow[0:1, 20:21], qrow[0:1, 36:37], 1.0, None, ALU.add)
                        p.inc("s_ev", v.tensor_scalar(qrow[0:1, 104:112],
                                                      qrow[0:1, 96:104],
                                                      qrow[0:1, 20:21], None,
                                                      op0=ALU.is_ge))
                        fence(v)
                        v_tt(v, qrow[0:1, 112:120], qrow[0:1, 88:96], qrow[0:1, 104:112],
                             ALU.mult)
                        p.inc("s_ev", v.tensor_reduce(qrow[0:1, 21:22],
                                                      qrow[0:1, 112:120], axis=AX.X,
                                                      op=ALU.max))
                        fence(v)
                        v_tt(v, qrow[0:1, 30:31], qrow[0:1, 30:31], qrow[0:1, 21:22],
                             ALU.max)
                        v_ss(v, qrow[0:1, 20:21], qrow[0:1, 36:37], -65.0, None,
                             ALU.add)
                        p.inc("s_ev", v.tensor_scalar(qrow[0:1, 104:112],
                                                      qrow[0:1, 96:104],
                                                      qrow[0:1, 20:21], None,
                                                      op0=ALU.is_le))
                        fence(v)
                        v_tt(v, qrow[0:1, 112:120], qrow[0:1, 88:96], qrow[0:1, 104:112],
                             ALU.mult)
                        v_ss(v, qrow[0:1, 104:112], qrow[0:1, 104:112], -1e9, 1e9,
                             ALU.mult, ALU.add)
                        v_tt(v, qrow[0:1, 112:120], qrow[0:1, 112:120],
                             qrow[0:1, 104:112], ALU.add)
                        p.inc("s_ev", v.tensor_reduce(qrow[0:1, 21:22],
                                                      qrow[0:1, 112:120], axis=AX.X,
                                                      op=ALU.min))
                        fence(v)
                        v_tt(v, qrow[0:1, 31:32], qrow[0:1, 31:32], qrow[0:1, 21:22],
                             ALU.min)
            bcast_row_to_cols(qrow[0:1, 31:32], qcol[:, 2:3], 1)
            count_gt([(pool_ap, pool_fd)], qcol[:, 2:3], qrow[0:1, 40:41])
            allreduce_row(cc_e[qidx][3], 8, cols=16)
            if True:
                blk = BLK
                @blk.vector
                def _(v):
                    p.wait(v, "s_ev")
                    p.inc("s_ev", v.tensor_copy(qrow[0:1, 37:38], qrow[0:1, 40:41]))
                    fence(v)
                    v_tt(v, qrow[0:1, 38:39], qrow[0:1, 36:37], qrow[0:1, 37:38],
                         ALU.subtract)
            npc = max(1, pool_fd // 4096)
            bpp = 16 // npc
            if True:
                blk = BLK
                @blk.vector
                def _(v):
                    p.wait(v, "s_ev")
                    for h in range(npc):
                        w = pool_fd // npc
                        p.inc("s_ev", v.scalar_tensor_tensor(
                            qtr[:, 0:w], pool_ap[:, h * w:(h + 1) * w],
                            qcol[:, 2:3], pool_ap[:, h * w:(h + 1) * w],
                            op0=ALU.is_le, op1=ALU.mult))
                        fence(v)
                        p.inc("s_ev", v.tensor_reduce(
                            wacc[:, h * bpp:(h + 1) * bpp],
                            qtr[:, 0:w].rearrange("p (b f) -> p b f", b=bpp),
                            axis=AX.X, op=ALU.max))
                        fence(v)

                @blk.sync
                def _(sy):
                    p.wait(sy, "s_ev")
                    p.inc("s_sp", sy.dma_start(out=gather_pair[0][:, :],
                                               in_=wacc[:, 0:16]), 16)

                @blk.gpsimd
                def _(g):
                    p.wait(g, "s_sp")
                    p.inc("s_cc", g.collective_compute(
                        "AllGather", ALU.bypass, replica_groups=RG_ALL,
                        ins=[gather_pair[0][:, :]], outs=[gather_pair[1][:, :]]), 1)

                @blk.sync
                def _(sy):
                    p.wait(sy, "s_cc")
                    p.inc("s_cr", sy.dma_start(
                        out=cand.rearrange("p (r c) -> p r c", r=8),
                        in_=gather_pair[1].rearrange("(r p) c -> p r c", p=128)), 16)

                @blk.vector
                def _(v):
                    p.wait(v, "s_cr")
                    p.inc("s_ev", v.tensor_scalar(negc[:, :], cand[:, :], -1.0, None,
                                                  op0=ALU.mult))
            emit_bisect([(cand[:, :], 128)], 20, qrow[0:1, 30:31], qrow[0:1, 31:32],
                        qrow[0:1, 38:39], None)
            bcast_row_to_cols(qrow[0:1, 31:32], qcol[:, 3:4], 1)
            if True:
                blk = BLK
                @blk.vector
                def _(v):
                    p.wait(v, "s_ev")
                    p.inc("s_ev", v.scalar_tensor_tensor(
                        qtr[:, 0:128], cand[:, :], qcol[:, 3:4], cand[:, :],
                        op0=ALU.is_le, op1=ALU.mult))
                    fence(v)
                    p.inc("s_ev", v.tensor_reduce(wacc[:, 20:21], qtr[:, 0:128],
                                                  axis=AX.X, op=ALU.max))
            cross_max_col(wacc[:, 20:21], qrow[0:1, 26:27])
            if True:
                blk = BLK
                @blk.vector
                def _(v):
                    p.wait(v, "s_ev")
                    p.inc("s_ev", v.scalar_tensor_tensor(
                        qtr[:, 0:128], cand[:, :], qcol[:, 3:4], negc[:, :],
                        op0=ALU.is_gt, op1=ALU.mult))
                    fence(v)
                    p.inc("s_ev", v.tensor_reduce(wacc[:, 21:22], qtr[:, 0:128],
                                                  axis=AX.X, op=ALU.max))
            cross_max_col(wacc[:, 21:22], qrow[0:1, 27:28])
            if True:
                blk = BLK
                @blk.vector
                def _(v):
                    p.wait(v, "s_ev")
                    v_ss(v, qrow[0:1, 27:28], qrow[0:1, 27:28], -1.0, None, ALU.mult)
                    v_tt(v, qrow[0:1, 27:28], qrow[0:1, 27:28], qrow[0:1, 26:27],
                         ALU.max)
                    v_tt(v, qrow[0:1, 28:29], qrow[0:1, 27:28], qrow[0:1, 26:27],
                         ALU.subtract)
                    p.inc("s_ev", v.tensor_scalar(
                        qrow[0:1, 28:29], qrow[0:1, 28:29],
                        qrow[0:1, frac_slot:frac_slot + 1], None, op0=ALU.mult))
                    fence(v)
                    v_tt(v, qrow[0:1, 33:34], qrow[0:1, 26:27], qrow[0:1, 28:29],
                         ALU.add)
                    v_ss(v, qrow[0:1, 33:34], qrow[0:1, 33:34], 1e-5, None, ALU.max)
                    p.inc("s_ev", v.reciprocal(qrow[0:1, 34:35], qrow[0:1, 33:34]))
                    fence(v)
                    v_ss(v, qrow[0:1, 34:35], qrow[0:1, 34:35], 127.0, None, ALU.mult)
            bcast_row_to_cols(qrow[0:1, 34:35], s_out_col, 1)

        for kk in range(8):
            if True:
                blk = BLK
                @blk.sync
                def _(sy, kk=kk):
                    p.wait(sy, "s_ev")
                    p.inc("s_ld", sy.dma_start(
                        out=hbuf[:, 0:C] if False else qtr[:, 0:4096],
                        in_=w1t_d[kk * 128:(kk + 1) * 128, :]), 16)

                @blk.vector
                def _(v, kk=kk):
                    p.wait(v, "s_ld")
                    p.inc("s_ev", v.reduce_sum(wacc[:, kk:kk + 1], qtr[:, 0:4096],
                                               axis=AX.X, apply_absolute_value=True))
        cross_sum_cols(wacc[:, 0:8], qrow[0:1, 40:48], 8)
        if True:
            blk = BLK
            @blk.vector
            def _(v):
                p.wait(v, "s_ev")
                p.inc("s_ev", v.reduce_sum(qrow[0:1, 39:40], qrow[0:1, 40:48], axis=AX.X))
                fence(v)
                v_ss(v, qrow[0:1, 10:11], qrow[0:1, 39:40], 1.0 / (C * HID), 1e-5,
                     ALU.mult, ALU.add)
                p.inc("s_ev", v.reciprocal(qrow[0:1, 11:12], qrow[0:1, 10:11]))
        bcast_row_to_cols(qrow[0:1, 11:12], qcol[:, 4:5], 1)
        for kk in range(8):
            if True:
                blk = BLK
                @blk.sync
                def _(sy, kk=kk):
                    p.wait(sy, "s_ev")
                    p.inc("s_ld", sy.dma_start(out=qtr[:, 0:4096],
                                               in_=w1t_d[kk * 128:(kk + 1) * 128, :]), 16)

                @blk.vector
                def _(v, kk=kk):
                    p.wait(v, "s_ld")
                    p.inc("s_ev", v.tensor_scalar(qtr[:, 0:4096], qtr[:, 0:4096],
                                                  qcol[:, 4:5], 1.0, op0=ALU.mult,
                                                  op1=ALU.min))
                    fence(v)
                    p.inc("s_ev", v.tensor_scalar(qtr[:, 0:4096], qtr[:, 0:4096],
                                                  -1.0, None, op0=ALU.max))
                    fence(v)
                    p.inc("s_ev", v.tensor_scalar(witmp[:, :], qtr[:, 0:4096],
                                                  0.0, None, op0=ALU.add))
                    fence(v)
                    p.inc("s_ev", v.tensor_scalar(t1_sb[:, kk * HID:(kk + 1) * HID],
                                                  witmp[:, :], 0.0, None, op0=ALU.add))
        T1_DONE = p.c["s_ev"]

        for half in range(8):
            if True:
                blk = BLK
                @blk.sync
                def _(sy, half=half):
                    p.wait(sy, "s_ev")
                    p.inc("s_ld", sy.dma_start(
                        out=qtr[:, 0:4096].rearrange("p (a c) -> p a c", a=4),
                        in_=w2t_d[half * 512:(half + 1) * 512, :].rearrange(
                            "(a p) c -> p a c", p=128)), 16)

                @blk.vector
                def _(v, half=half):
                    p.wait(v, "s_ld")
                    p.inc("s_ev", v.reduce_sum(wacc[:, 16 + half:17 + half],
                                               qtr[:, 0:4096], axis=AX.X,
                                               apply_absolute_value=True))
        cross_sum_cols(wacc[:, 16:24], qrow[0:1, 40:48], 8)
        if True:
            blk = BLK
            @blk.vector
            def _(v):
                p.wait(v, "s_ev")
                p.inc("s_ev", v.reduce_sum(qrow[0:1, 44:45], qrow[0:1, 40:48], axis=AX.X))
                fence(v)
                v_ss(v, qrow[0:1, 12:13], qrow[0:1, 44:45], 1.0 / (C * HID), 1e-5,
                     ALU.mult, ALU.add)
                p.inc("s_ev", v.reciprocal(qrow[0:1, 13:14], qrow[0:1, 12:13]))
        bcast_row_to_cols(qrow[0:1, 13:14], qcol[:, 14:15], 1)

        if True:
            blk = BLK
            @blk.vector
            def _(v):
                p.wait(v, "s_ev")
                p.inc("s_ev", v.reduce_max(wacc[:, 24:25], habsA[:, :], axis=AX.X))
        cross_max_col(wacc[:, 24:25], qrow[0:1, 31:32])
        if True:
            blk = BLK
            @blk.vector
            def _(v):
                p.wait(v, "s_ev")
                p.inc("s_ev", v.memset(qrow[0:1, 30:31], 0.0))
                fence(v)
                p.inc("s_ev", v.memset(qrow[0:1, QS + 9:QS + 10], 0.005 * 4 * 128 * C))
        emit_bisect([(habsA[:, :], 4096)], 14, qrow[0:1, 30:31], qrow[0:1, 31:32],
                    qrow[0:1, QS + 9:QS + 10], qrow[0:1, 32:33])
        if True:
            blk = BLK
            @blk.vector
            def _(v):
                p.wait(v, "s_ev")
                p.inc("s_ev", v.tensor_copy(qrow[0:1, 40:41], qrow[0:1, 32:33]))
        allreduce_row(cc_th[0], 1, scale=1.0 / 8.0)
        if True:
            blk = BLK
            @blk.vector
            def _(v):
                p.wait(v, "s_ev")
                p.inc("s_ev", v.tensor_copy(qrow[0:1, 32:33], qrow[0:1, 40:41]))

        DLT = 0.012
        if True:
            blk = BLK
            @blk.vector
            def _(v):
                p.wait(v, "s_ev")
                for j in range(6):
                    f = -DLT + (2 * DLT) * j / 5.0
                    v_ss(v, qrow[0:1, 40 + j:41 + j], qrow[0:1, 32:33], 1.0 + f,
                         None, ALU.mult)
                p.inc("s_ev", v.memset(wacc[:, 8:14], 0.0))
        bcast_row_to_cols(qrow[0:1, 40:46], qcol[:, 8:14], 6)
        for cc in range(NCH):
            cur = cc % 2
            if True:
                blk = BLK
                @blk.sync
                def _(sy, cc=cc, cur=cur):
                    p.wait(sy, "s_ev")
                    p.wait(sy, "s_sp", h1sp_sn[cc])
                    p.inc("s_ld", sy.dma_start(out=hbuf[:, cur * C:(cur + 1) * C],
                                               in_=h1_sp[cc * 128:(cc + 1) * 128, :]), 16)

                @blk.vector
                def _(v, cc=cc, cur=cur):
                    p.wait(v, "s_ld")
                    hb0 = hbuf[:, cur * C:(cur + 1) * C]
                    hb = qtr[:, 0:C]
                    p.inc("s_ev", v.tensor_scalar(qtr[:, 1024:2048], hb0, -1.0, None,
                                                  op0=ALU.mult))
                    fence(v)
                    p.inc("s_ev", v.tensor_tensor(hb, hb0, qtr[:, 1024:2048],
                                                  op=ALU.max))
                    fence(v)
                    for j in range(6):
                        sco = 1024 + (j % 3) * 1024
                        if j == 3:
                            fence(v)
                        p.inc("s_ev", v.tensor_scalar(qtr[:, sco:sco + 1024], hb,
                                                      qcol[:, 8 + j:9 + j], 0.0,
                                                      op0=ALU.is_gt, op1=ALU.add,
                                                      accum_out=wacc[:, 30 + j:31 + j]))
                    fence(v)
                    p.inc("s_ev", v.tensor_tensor(wacc[:, 8:14], wacc[:, 8:14],
                                                  wacc[:, 30:36], op=ALU.add))
                    fence(v)
                    p.inc("s_ev", v.scalar_tensor_tensor(qtr[:, 0:C], hb,
                                                         qcol[:, 13:14], hb,
                                                         op0=ALU.is_le, op1=ALU.mult))
                    fence(v)
                    p.inc("s_ev", v.tensor_tensor(qtr[:, 0:512], qtr[:, 0:512],
                                                  qtr[:, 512:1024], op=ALU.max))
                    fence(v)
                    p.inc("s_ev", v.tensor_tensor(qtr[:, 0:256], qtr[:, 0:256],
                                                  qtr[:, 256:512], op=ALU.max))
                    fence(v)
                    p.inc("s_ev", v.tensor_tensor(qtr[:, 0:128], qtr[:, 0:128],
                                                  qtr[:, 128:256], op=ALU.max))
                    fence(v)
                    p.inc("s_ev", v.tensor_tensor(pool1[:, cc * 64:(cc + 1) * 64],
                                                  qtr[:, 0:64], qtr[:, 64:128],
                                                  op=ALU.max))
                    fence(v)
        cross_sum_cols(wacc[:, 8:14], qrow[0:1, 72:78], 6)
        if True:
            blk = BLK
            @blk.vector
            def _(v):
                p.wait(v, "s_ev")
                v_ss(v, qrow[0:1, 14:15], qrow[0:1, 0:1], -1.0, None, ALU.add)
                p.inc("s_ev", v.tensor_copy(qrow[0:1, 64:70], qrow[0:1, 40:46]))
        emit_quantile_tail(0, pool1[:, :], 2048, cc_gc[0], 14, 1, qcol[:, 5:6])
        if True:
            blk = BLK
            @blk.vector
            def _(v):
                p.wait(v, "s_ev")
                v_tt(v, qrow[0:1, 16:17], qrow[0:1, 10:11], qrow[0:1, 33:34], ALU.mult)
                v_ss(v, qrow[0:1, 16:17], qrow[0:1, 16:17], 1.0 / 127.0, None, ALU.mult)
        bcast_row_to_cols(qrow[0:1, 16:17], qcol[:, 6:7], 1)
        if "d_qrow1" in dbg_d:
            if True:
                blk = BLK
                @blk.gpsimd
                def _(g):
                    p.wait(g, "s_ev")
                    p.inc("s_gp", g.dma_start(out=dbg_d["d_qrow1"][:, :],
                                              in_=qrow[0:1, :]), 16)
                    p.inc("s_gp", g.dma_start(out=dbg_d["d_qcol"][:, :],
                                              in_=qcol[:, :]), 16)
                    p.inc("s_gp", g.dma_start(out=dbg_d["d_pool1"][:, :],
                                              in_=pool1[:, :]), 16)
                    p.inc("s_gp", g.dma_start(out=dbg_d["d_habsA"][:, :],
                                              in_=habsA[:, :]), 16)
        Q1_DONE = p.c["s_ev"]

        def q2_tile_counts(v, gb, ti):
            for j in range(6):
                sco = j * 512
                p.inc("s_ev", v.tensor_scalar(qtr[:, sco:sco + 512], gb,
                                              qcol[:, 8 + j:9 + j], 0.0,
                                              op0=ALU.is_gt, op1=ALU.add,
                                              accum_out=wacc[:, 30 + j:31 + j]))
            fence(v)
            p.inc("s_ev", v.tensor_tensor(wacc[:, 8:14], wacc[:, 8:14],
                                          wacc[:, 30:36], op=ALU.add))
            fence(v)
            p.inc("s_ev", v.scalar_tensor_tensor(qtr[:, 0:512], gb,
                                                 qcol[:, 13:14], gb,
                                                 op0=ALU.is_le, op1=ALU.mult))
            fence(v)
            p.inc("s_ev", v.tensor_tensor(qtr[:, 0:256], qtr[:, 0:256],
                                          qtr[:, 256:512], op=ALU.max))
            fence(v)
            p.inc("s_ev", v.tensor_tensor(qtr[:, 0:128], qtr[:, 0:128],
                                          qtr[:, 128:256], op=ALU.max))
            fence(v)
            p.inc("s_ev", v.tensor_tensor(qtr[:, 0:64], qtr[:, 0:64],
                                          qtr[:, 64:128], op=ALU.max))
            fence(v)
            slot = ti % 2
            if ti >= 2:
                p.wait(v, "s_ps%d" % slot, pspill_sn[ti - 2])
            p.inc("s_ev", v.tensor_tensor(pstg[:, slot * 32:(slot + 1) * 32],
                                          qtr[:, 0:32], qtr[:, 32:64],
                                          op=ALU.max))
            ptree_sn[ti] = p.c["s_ev"]

        def q2_tile_spill(ti):
            blk = BLK
            slot = ti % 2

            @blk.sync
            def _(sy, ti=ti, slot=slot):
                p.wait(sy, "s_ev", ptree_sn[ti])
                p.inc("s_ps%d" % slot, sy.dma_start(
                    out=pool2_d[:, ti * 32:(ti + 1) * 32],
                    in_=pstg[:, slot * 32:(slot + 1) * 32]), 16)
            pspill_sn[ti] = p.c["s_ps%d" % slot]

        scale_snap = {}
        pspill_sn = {}
        ptree_sn = {}
        gel_snap = {}
        gspill_sn = {}
        cnt_snap = {}
        for s in range(NSLAB):
            for ccq in range(4):
                cc = s * 4 + ccq
                cur = cc % 2
                if True:
                    blk = BLK
                    @blk.sync
                    def _(sy, cc=cc, cur=cur):
                        p.wait(sy, "s_ev")
                        p.wait(sy, "s_sp", h1sp_sn[NCH - 1])
                        p.inc("s_ld", sy.dma_start(
                            out=hbuf[:, cur * C:(cur + 1) * C],
                            in_=h1_sp[cc * 128:(cc + 1) * 128, :]), 16)

                    @blk.vector
                    def _(v, ccq=ccq, cur=cur):
                        p.wait(v, "s_ld")
                        p.wait(v, "s_mm")
                        hb = hbuf[:, cur * C:(cur + 1) * C]
                        p.inc("s_ev", v.tensor_scalar(qtr[:, 0:C], hb,
                                                      qcol[:, 5:6], 127.0,
                                                      op0=ALU.mult, op1=ALU.min))
                        fence(v)
                        p.inc("s_ev", v.tensor_scalar(qtr[:, 0:C], qtr[:, 0:C],
                                                      -128.0, None, op0=ALU.max))
                        fence(v)
                        p.inc("s_ev", v.tensor_scalar(witmp[:, 0:C], qtr[:, 0:C],
                                                      0.0, None, op0=ALU.add))
                        fence(v)
                        p.inc("s_ev", v.tensor_scalar(n1stage[:, :], witmp[:, 0:C],
                                                      0.0, None, op0=ALU.add))
                for half in range(2):
                    if True:
                        blk = BLK
                        @blk.tensor
                        def _(t, half=half, ccq=ccq):
                            p.wait(t, "s_ev")
                            p.wait(t, "s_sc")
                            last = None
                            for j in range(4):
                                kk = half * 4 + j
                                last = t.transpose(
                                    pt[:, j * 256:j * 256 + 128],
                                    n1stage[:, kk * 128:(kk + 1) * 128],
                                    ident[:, :])
                            p.inc("s_mm", last)

                        @blk.vector
                        def _(v, half=half, ccq=ccq):
                            p.wait(v, "s_mm")
                            last = None
                            for j in range(4):
                                kk = half * 4 + j
                                last = v.tensor_copy(n1T[:, kk * SLAB + ccq * 128:
                                                   kk * SLAB + (ccq + 1) * 128],
                                               pt[:, j * 256:j * 256 + 128])
                            p.inc("s_sc", last)
            N1T_DONE = p.c["s_sc"]
            for mh in range(32):
                ti = s * 32 + mh
                slot = ti % 2
                bank = ps[slot]
                ssl = gt32[:, slot * 512:(slot + 1) * 512]
                gsl = qtr[:, 3072 + slot * 512:3072 + (slot + 1) * 512]
                if True:
                    blk = BLK
                    @blk.tensor
                    def _(t, mh=mh, bank=bank, s=s,
                          bsn=gel_snap.get(ti - 2)):
                        if bsn is None:
                            p.wait(t, "s_sc", N1T_DONE)
                            p.wait(t, "s_ev")
                        else:
                            p.wait(t, "s_sc", max(N1T_DONE, bsn))
                        last = None
                        for kk in range(8):
                            last = t.matmul(
                                bank[:, :],
                                t1_sb[:, kk * HID + mh * 128:kk * HID + (mh + 1) * 128],
                                n1T[:, kk * SLAB:(kk + 1) * SLAB],
                                start=(kk == 0), stop=(kk == 7))
                        p.inc("s_mm", last)
                    mm_sn = p.c["s_mm"]

                    @blk.scalar
                    def _(sc, mh=mh, s=s, bank=bank, gsl=gsl, mm_sn=mm_sn,
                          csn=cnt_snap.get(ti - 2),
                          dsn=gspill_sn.get(ti - 2), slot=slot):
                        p.wait(sc, "s_mm", mm_sn)
                        if dsn is not None:
                            p.wait(sc, "s_gs%d" % slot, dsn)
                        if csn is not None:
                            p.wait(sc, "s_ev", csn)
                        p.inc("s_sc", sc.activation(gsl, bank[:, :], AF.Gelu,
                                                    scale=qcol[:, 6:7]))
                        if s == 0:
                            p.wait(sc, "s_sc")
                            p.inc("s_sc", sc.activation(gsA[:, mh * 512:(mh + 1) * 512],
                                                        gsl, AF.Abs))
                    gel_snap[ti] = p.c["s_sc"]

                    if s > 0:
                        @blk.vector
                        def _(v, gsl=gsl, ti=ti, gel_sn=gel_snap[ti]):
                            p.wait(v, "s_sc", gel_sn)
                            q2_tile_counts(v, gsl, ti)
                        cnt_snap[ti] = p.c["s_ev"]
                        q2_tile_spill(ti)

                    @blk.sync
                    def _(sy, mh=mh, s=s, gsl=gsl, slot=slot):
                        p.wait(sy, "s_sc")
                        p.inc("s_gs%d" % slot, sy.dma_start(
                            out=g_sp[s, mh * 128:(mh + 1) * 128, :],
                            in_=gsl), 16)
                    gspill_sn[ti] = p.c["s_gs%d" % slot]

            if s == 0:
                if True:
                    blk = BLK
                    @blk.vector
                    def _(v):
                        p.wait(v, "s_ev")
                        p.wait(v, "s_sc")
                        p.inc("s_ev", v.reduce_max(wacc[:, 24:25], gsA[:, :], axis=AX.X))
                cross_max_col(wacc[:, 24:25], qrow[0:1, 31:32])
                if True:
                    blk = BLK
                    @blk.vector
                    def _(v):
                        p.wait(v, "s_ev")
                        p.inc("s_ev", v.memset(qrow[0:1, 30:31], 0.0))
                        fence(v)
                        p.inc("s_ev", v.memset(qrow[0:1, 29:30], 0.005 * 128 * 16384))
                emit_bisect([(gsA[:, k * 2048:(k + 1) * 2048], 2048)
                             for k in range(8)],
                            14, qrow[0:1, 30:31], qrow[0:1, 31:32], qrow[0:1, 29:30],
                            qrow[0:1, 32:33])
                if True:
                    blk = BLK
                    @blk.vector
                    def _(v):
                        p.wait(v, "s_ev")
                        p.inc("s_ev", v.tensor_copy(qrow[0:1, 40:41], qrow[0:1, 32:33]))
                allreduce_row(cc_th[1], 1, scale=1.0 / 8.0)
                if True:
                    blk = BLK
                    @blk.vector
                    def _(v):
                        p.wait(v, "s_ev")
                        p.inc("s_ev", v.tensor_copy(qrow[0:1, 32:33], qrow[0:1, 40:41]))
                        fence(v)
                        for j in range(6):
                            f = -0.012 + 0.024 * j / 5.0
                            v_ss(v, qrow[0:1, 40 + j:41 + j], qrow[0:1, 32:33], 1.0 + f,
                                 None, ALU.mult)
                        p.inc("s_ev", v.memset(wacc[:, 8:14], 0.0))
                bcast_row_to_cols(qrow[0:1, 40:46], qcol[:, 8:14], 6)
                s0cnt_sn = {}
                for mh0 in range(32):
                    sl4 = mh0 % 4
                    if True:
                        blk = BLK
                        @blk.sync
                        def _(sy, mh0=mh0, sl4=sl4):
                            if mh0 < 4:
                                p.wait(sy, "s_gs0")
                                p.wait(sy, "s_gs1")
                            else:
                                p.wait(sy, "s_ev", s0cnt_sn[mh0 - 4])
                            p.inc("s_ld", sy.dma_start(
                                out=pool1[:, sl4 * 512:(sl4 + 1) * 512],
                                in_=g_sp[0, mh0 * 128:(mh0 + 1) * 128, :]), 16)
                        ld_sn = p.c["s_ld"]

                        @blk.vector
                        def _(v, mh0=mh0, sl4=sl4, ld_sn=ld_sn):
                            p.wait(v, "s_ld", ld_sn)
                            if mh0 == 0:
                                fence(v)
                            q2_tile_counts(v, pool1[:, sl4 * 512:(sl4 + 1) * 512], mh0)
                        q2_tile_spill(mh0)
                        s0cnt_sn[mh0] = p.c["s_ev"]
        MM1_DONE = p.c["s_sp"]
        if "d_t1" in dbg_d:
            if True:
                blk = BLK
                @blk.gpsimd
                def _(g):
                    p.wait(g, "s_ev")
                    p.wait(g, "s_sc")
                    p.wait(g, "s_sp")
                    p.inc("s_gp", g.dma_start(out=dbg_d["d_t1"][:, :],
                                              in_=t1_sb[:, 0:4096]), 16)
                    p.inc("s_gp", g.dma_start(out=dbg_d["d_n1T"][:, :],
                                              in_=n1T[:, :]), 16)
                    p.inc("s_gp", g.dma_start(out=dbg_d["d_gsA"][:, :],
                                              in_=gsA[:, 0:4096]), 16)


        cross_sum_cols(wacc[:, 8:14], qrow[0:1, 72:78], 6)
        if True:
            blk = BLK
            @blk.vector
            def _(v):
                p.wait(v, "s_ev")
                v_ss(v, qrow[0:1, 15:16], qrow[0:1, 2:3], -1.0, None, ALU.add)
                p.inc("s_ev", v.tensor_copy(qrow[0:1, 64:70], qrow[0:1, 40:46]))
        LS4.close()
        LS4t = ExitStack()
        pool2 = LS4t.enter_context(nc.sbuf_tensor("pool2", [128, 8192], F32))
        if True:
            blk = BLK
            @blk.sync
            def _(sy):
                p.wait(sy, "s_ps0")
                p.wait(sy, "s_ps1")
                p.wait(sy, "s_gs0")
                p.wait(sy, "s_gs1")
                p.inc("s_ld", sy.dma_start(out=pool2[:, 0:4096],
                                           in_=pool2_d[:, 0:4096]), 16)
                p.inc("s_ld", sy.dma_start(out=pool2[:, 4096:8192],
                                           in_=pool2_d[:, 4096:8192]), 16)

            @blk.vector
            def _(v):
                p.wait(v, "s_ld")
                p.inc("s_ev", v.engine_nop())
        emit_quantile_tail(1, pool2[:, :], 8192, cc_gc[1], 15, 3, qcol[:, 7:8])
        if True:
            blk = BLK
            @blk.vector
            def _(v):
                p.wait(v, "s_ev")
                p.inc("s_ev", v.reciprocal(qrow[0:1, 21:22], qrow[0:1, 34:35]))
                fence(v)
                v_tt(v, qrow[0:1, 22:23], qrow[0:1, 12:13], qrow[0:1, 21:22], ALU.mult)
        bcast_row_to_cols(qrow[0:1, 22:23], qcol[:, 9:10], 1)
        LS4t.close()

        LS4c = ExitStack()
        sb4c = lambda name, shape, dt: LS4c.enter_context(nc.sbuf_tensor(name, shape, dt))
        t2_sb = sb4c("t2_sb", [128, 32 * C], BF16)
        gbuf2 = sb4c("gbuf2", [128, 2 * 512], F32)
        n2s = sb4c("n2s", [128, 32 * SLAB], BF16)
        p4s = sb4c("p4s", [128, 2 * C], F32)
        x2b = sb4c("x2b", [128, C], F32)
        for ti in range(32):
            if True:
                blk = BLK
                @blk.sync
                def _(sy, ti=ti):
                    p.wait(sy, "s_ev")
                    p.inc("s_ld", sy.dma_start(out=qtr[:, 0:C],
                                               in_=w2t_d[ti * 128:(ti + 1) * 128, :]), 16)

                @blk.vector
                def _(v, ti=ti):
                    p.wait(v, "s_ld")
                    p.inc("s_ev", v.tensor_scalar(qtr[:, 0:C], qtr[:, 0:C],
                                                  qcol[:, 14:15], 1.0, op0=ALU.mult,
                                                  op1=ALU.min))
                    fence(v)
                    p.inc("s_ev", v.tensor_scalar(qtr[:, 0:C], qtr[:, 0:C],
                                                  -1.0, None, op0=ALU.max))
                    fence(v)
                    p.inc("s_ev", v.tensor_scalar(witmp[:, 0:C], qtr[:, 0:C],
                                                  0.0, None, op0=ALU.add))
                    fence(v)
                    p.inc("s_ev", v.tensor_scalar(t2_sb[:, ti * C:(ti + 1) * C],
                                                  witmp[:, 0:C], 0.0, None, op0=ALU.add))
        for s in range(NSLAB):
            for mh in range(32):
                cur = mh % 2
                if True:
                    blk = BLK
                    @blk.sync
                    def _(sy, s=s, mh=mh, cur=cur):
                        p.wait(sy, "s_ev")
                        p.wait(sy, "s_gs%d" % ((s * 32 + mh) % 2), gspill_sn[s * 32 + mh])
                        p.inc("s_ld", sy.dma_start(
                            out=gbuf2[:, cur * 512:(cur + 1) * 512],
                            in_=g_sp[s, mh * 128:(mh + 1) * 128, :]), 16)

                    @blk.vector
                    def _(v, mh=mh, cur=cur):
                        p.wait(v, "s_ld")
                        gb = gbuf2[:, cur * 512:(cur + 1) * 512]
                        p.inc("s_ev", v.tensor_scalar(qtr[:, 0:512], gb,
                                                      qcol[:, 7:8], 127.0,
                                                      op0=ALU.mult, op1=ALU.min))
                        fence(v)
                        p.inc("s_ev", v.tensor_scalar(qtr[:, 0:512], qtr[:, 0:512],
                                                      -128.0, None, op0=ALU.max))
                        fence(v)
                        p.inc("s_ev", v.tensor_scalar(witmp[:, 0:512], qtr[:, 0:512],
                                                      0.0, None, op0=ALU.add))
                        fence(v)
                        p.inc("s_ev", v.tensor_scalar(n2s[:, mh * SLAB:(mh + 1) * SLAB],
                                                      witmp[:, 0:512], 0.0, None,
                                                      op0=ALU.add))
            N2_DONE = p.c["s_ev"]
            for ccq in range(4):
                cc = s * 4 + ccq
                if True:
                    blk = BLK
                    @blk.tensor
                    def _(t, ccq=ccq):
                        p.wait(t, "s_ev")
                        p.wait(t, "s_sc")
                        last = None
                        for ob in range(2):
                            for hid in range(32):
                                last = t.matmul(
                                    ps[2 + ob][:, :],
                                    n2s[:, hid * SLAB + ccq * 128:
                                        hid * SLAB + (ccq + 1) * 128],
                                    t2_sb[:, hid * C + ob * 512:hid * C + (ob + 1) * 512],
                                    start=(hid == 0), stop=(hid == 31))
                        p.inc("s_mm", last)

                    @blk.sync
                    def _(sy, cc=cc):
                        p.wait(sy, "s_ev")
                        p.wait(sy, "s_sp", h1sp_sn[cc])
                        p.inc("s_ld", sy.dma_start(out=x2b[:, :],
                                                   in_=x2_sp[cc * 128:(cc + 1) * 128, :]), 16)

                    @blk.vector
                    def _(v):
                        p.wait(v, "s_mm")
                        p.wait(v, "s_ld")
                        p.inc("s_ev", v.tensor_scalar(p4s[:, 0:512], ps[2][:, :],
                                                      qcol[:, 9:10], None, op0=ALU.mult))
                        fence(v)
                        p.inc("s_ev", v.tensor_scalar(p4s[:, 512:1024], ps[3][:, :],
                                                      qcol[:, 9:10], None, op0=ALU.mult))
                        fence(v)

                    @blk.gpsimd
                    def _(g, cc=cc):
                        if cc == NCH - 1 and "d_raw" in dbg_d:
                            p.wait(g, "s_ev")
                            p.inc("s_gp", g.dma_start(out=dbg_d["d_raw"][:, :],
                                                      in_=p4s[:, 0:1024]), 16)

                    @blk.vector
                    def _(v):
                        p.wait(v, "s_gp")
                        p.inc("s_ev", v.engine_nop())
                        p.inc("s_ev", v.reduce_sum(stats[:, 0:1], p4s[:, 0:1024],
                                                   axis=AX.X))
                        fence(v)
                        p.inc("s_ev", v.tensor_scalar(stats[:, 0:1], stats[:, 0:1],
                                                      1.0 / C, None, op0=ALU.mult))
                        fence(v)
                        p.inc("s_ev", v.tensor_scalar(p4s[:, 0:1024], p4s[:, 0:1024],
                                                      stats[:, 0:1], None,
                                                      op0=ALU.subtract))

                    @blk.vector
                    def _(v):
                        p.wait(v, "s_ev")
                        p.wait(v, "s_out")
                        p.inc("s_ev", v.tensor_tensor(p4s[:, 1024:2048], p4s[:, 0:1024],
                                                      p4s[:, 0:1024], op=ALU.mult))
                        fence(v)
                        p.inc("s_ev", v.reduce_sum(stats[:, 1:2], p4s[:, 1024:2048],
                                                   axis=AX.X))
                        fence(v)
                        p.inc("s_ev", v.tensor_scalar(stats[:, 1:2], stats[:, 1:2],
                                                      1.0 / C, 1e-5, op0=ALU.mult,
                                                      op1=ALU.add))

                    @blk.vector
                    def _(v):
                        p.wait(v, "s_ev")
                        p.inc("s_ev", v.reciprocal(stats[:, 1:2], stats[:, 1:2]))

                    @blk.scalar
                    def _(sc):
                        p.wait(sc, "s_ev")
                        p.inc("s_sc", sc.activation(stats[:, 1:2], stats[:, 1:2], AF.Sqrt))

                    @blk.vector
                    def _(v):
                        p.wait(v, "s_sc")
                        p.wait(v, "s_out")
                        p.inc("s_ev", v.tensor_scalar(p4s[:, 0:1024], p4s[:, 0:1024],
                                                      stats[:, 1:2], None, op0=ALU.mult))
                        fence(v)
                        p.inc("s_ev", v.tensor_tensor(p4s[:, 1024:2048], x2b[:, :],
                                                      p4s[:, 0:1024], op=ALU.add))

                    @blk.sync
                    def _(sy, cc=cc):
                        p.wait(sy, "s_ev")
                        p.inc("s_out", sy.dma_start(out=out_d[cc * 128:(cc + 1) * 128, :],
                                                    in_=p4s[:, 1024:2048]), 16)
        if "d_qrow" in dbg_d:
            if True:
                blk = BLK
                @blk.gpsimd
                def _(g):
                    p.wait(g, "s_ev")
                    p.wait(g, "s_out")
                    p.inc("s_gp", g.dma_start(out=dbg_d["d_qrow"][:, :],
                                              in_=qrow[0:1, :]), 16)
                    p.inc("s_gp", g.dma_start(out=dbg_d["d_gt"][:, :],
                                              in_=g_sp[0, 0:128, :]), 16)
                    p.inc("s_gp", g.dma_start(out=dbg_d["d_n2"][:, :],
                                              in_=n2s[:, 0:512]), 16)
                    p.inc("s_gp", g.dma_start(out=dbg_d["d_t2"][:, :],
                                              in_=t2_sb[:, 0:1024]), 16)
                    p.inc("s_gp", g.dma_start(out=dbg_d["d_fout"][:, :],
                                              in_=p4s[:, 0:2048]), 16)

        LS4c.close()
        LS4s.close()
        LS3h.close()
        LSattc.close()


    if STAGE < 5:
        if True:
            blk = BLK
            @blk.vector
            def _(v):
                p.inc("s_ev", v.memset(zbuf[:, :], 0.0))

            @blk.sync
            def _(sy):
                p.wait(sy, "s_sp")
                p.wait(sy, "s_gp")
                p.wait(sy, "s_ev")
                for i in range(NT // 128):
                    p.inc("s_out", sy.dma_start(out=out_d[i * 128:(i + 1) * 128, :],
                                                in_=zbuf[:, 0:C]), 16)
                p.wait(sy, "s_out")
    else:
        if True:
            blk = BLK
            @blk.sync
            def _(sy):
                p.wait(sy, "s_sp")
                p.wait(sy, "s_gp")
                p.wait(sy, "s_out")

    es.close()
    return nc, [nm for nm, _, _ in dbg]


_cache = {}


def host_inputs(inputs, core):
    b, half = core // 2, core % 2
    x = inputs["L0_attn_x"]
    ut = np.triu(np.ones((128, 128), np.float32), 1)
    hselT = np.zeros((16, C), np.float32)
    for hh in range(H):
        hselT[hh, hh * D:(hh + 1) * D] = 1.0
    n1 = B * T * C
    n2 = B * T * HID
    r1, r2 = 0.995 * (n1 - 1), 0.995 * (n2 - 1)
    qtarg = np.zeros((1, 16), np.float32)
    qtarg[0, 0] = n1 - float(np.floor(r1))
    qtarg[0, 1] = r1 - np.floor(r1)
    qtarg[0, 2] = n2 - float(np.floor(r2))
    qtarg[0, 3] = r2 - np.floor(r2)
    qtarg[0, 4] = float(n1)
    qtarg[0, 5] = float(n2)
    return {
        "xin": np.ascontiguousarray(x[b, half * NT:(half + 1) * NT, :]),
        "wq": inputs["L0_attn_Wq"], "wk": inputs["L0_attn_Wk"],
        "wv": inputs["L0_attn_Wv"], "wo": inputs["L0_attn_Wo"],
        "w1t": np.ascontiguousarray(inputs["L0_ffn_w1"].T),
        "w2t": np.ascontiguousarray(inputs["L0_ffn_w2"].T),
        "alpha": np.ascontiguousarray(inputs["L0_attn_alpha_raw"].reshape(1, H)),
        "beta": np.ascontiguousarray(inputs["L0_attn_beta_raw"].reshape(1, H)),
        "lng": np.ascontiguousarray(inputs["L0_ffn_ln_g"].reshape(1, C)),
        "lnb": np.ascontiguousarray(inputs["L0_ffn_ln_b"].reshape(1, C)),
        "identin": np.eye(128, dtype=np.float32),
        "ltmin": np.tril(np.ones((128, 128), np.float32), -1),
        "utm4in": np.ascontiguousarray(np.tile(ut, (1, 4))),
        "hselTin": hselT,
        "hselin": np.ascontiguousarray(hselT.T),
        "onesin": np.ones((128, 128), np.float32),
        "selcin": np.full((64, 1), float(half), np.float32),
        "eoin": np.ascontiguousarray(np.concatenate([
            np.repeat([[1.0], [0.0]], 64, 0), np.repeat([[0.0], [1.0]], 64, 0)],
            1).astype(np.float32)),
        "qtargin": qtarg,
    }


def kernel(**inputs):
    global _dbg_out
    key = tuple(nm for nm, _, _ in DEBUG)
    if key not in _cache:
        _cache[key] = build(DEBUG)
    nc, dbg_names = _cache[key]
    in_maps = [host_inputs(inputs, c) for c in range(8)]
    res = run_bass_kernel_spmd(nc, in_maps, core_ids=list(range(8)))
    if dbg_names:
        _dbg_out = [{nm: r[nm] for nm in dbg_names} for r in res.results]
    out = np.empty((B, T, C), np.float32)
    for c in range(8):
        b, half = c // 2, c % 2
        out[b, half * NT:(half + 1) * NT, :] = res.results[c]["out"]
    return out


# version 14
# speedup vs baseline: 1.0922x; 1.0077x over previous
import numpy as np
from contextlib import ExitStack

import concourse.bass as bass
import concourse.mybir as mybir
from concourse.bass_utils import run_bass_kernel_spmd
from concourse.alu_op_type import AluOpType as ALU

F32 = mybir.dt.float32
BF16 = mybir.dt.bfloat16
I32 = mybir.dt.int32
AF = mybir.ActivationFunctionType
AX = mybir.AxisListType

B, T, C = 4, 8192, 1024
H, D, CS = 16, 64, 128
NT, NCH, HID, SLAB, NSLAB = 4096, 32, 4096, 512, 8

DEBUG = []
_dbg_out = None
STAGE = 99
SUB = 9
FEAT = {'xt','proj','knorm','chunks','quads','quad4','sevac','masktt','x0','solve','post','qhat'}


class P:
    def __init__(self, nc, es):
        self.nc, self.es, self.c = nc, es, {}

    def sem(self, name):
        h = self.es.enter_context(self.nc.semaphore(name))
        self.c[name] = 0
        setattr(self, name, h)

    def inc(self, name, instr, n=1):
        instr.then_inc(getattr(self, name), n)
        self.c[name] += n

    def wait(self, eng, name, val=None):
        eng.wait_ge(getattr(self, name), self.c[name] if val is None else val)


def build(dbg=()):
    nc = bass.Bass("TRN2", target_bir_lowering=False)
    es = ExitStack()
    p = P(nc, es)

    xin = nc.dram_tensor("xin", [NT, C], F32, kind="ExternalInput").ap()
    wq_d = nc.dram_tensor("wq", [C, C], F32, kind="ExternalInput").ap()
    wk_d = nc.dram_tensor("wk", [C, C], F32, kind="ExternalInput").ap()
    wv_d = nc.dram_tensor("wv", [C, C], F32, kind="ExternalInput").ap()
    wo_d = nc.dram_tensor("wo", [C, C], F32, kind="ExternalInput").ap()
    w1t_d = nc.dram_tensor("w1t", [C, HID], F32, kind="ExternalInput").ap()
    w2t_d = nc.dram_tensor("w2t", [HID, C], F32, kind="ExternalInput").ap()
    alpha_d = nc.dram_tensor("alpha", [1, H], F32, kind="ExternalInput").ap()
    beta_d = nc.dram_tensor("beta", [1, H], F32, kind="ExternalInput").ap()
    lng_d = nc.dram_tensor("lng", [1, C], F32, kind="ExternalInput").ap()
    lnb_d = nc.dram_tensor("lnb", [1, C], F32, kind="ExternalInput").ap()
    ident_d = nc.dram_tensor("identin", [128, 128], F32, kind="ExternalInput").ap()
    ltm_d = nc.dram_tensor("ltmin", [128, 128], F32, kind="ExternalInput").ap()
    utm4_d = nc.dram_tensor("utm4in", [128, 512], F32, kind="ExternalInput").ap()
    hselT_d = nc.dram_tensor("hselTin", [16, C], F32, kind="ExternalInput").ap()
    hsel_d = nc.dram_tensor("hselin", [C, 16], F32, kind="ExternalInput").ap()
    ones_d = nc.dram_tensor("onesin", [128, 128], F32, kind="ExternalInput").ap()
    selc_d = nc.dram_tensor("selcin", [64, 1], F32, kind="ExternalInput").ap()
    eo_d = nc.dram_tensor("eoin", [128, 2], F32, kind="ExternalInput").ap()
    qtarg_d = nc.dram_tensor("qtargin", [1, 16], F32, kind="ExternalInput").ap()
    out_d = nc.dram_tensor("out", [NT, C], F32, kind="ExternalOutput").ap()
    dbg_d = {nm: nc.dram_tensor(nm, list(sh), dt, kind="ExternalOutput").ap()
             for nm, sh, dt in dbg}

    qhat_sp = nc.dram_tensor("qhat_sp", [NCH, 64, H * CS], BF16).ap()
    yint_sp = nc.dram_tensor("yint_sp", [NCH, 128, H * D], F32).ap()
    gh_sp = nc.dram_tensor("gh_sp", [NCH, 64, H * D], BF16).ap()
    bh_sp = nc.dram_tensor("bh_sp", [NCH, 64, H * D], BF16).ap()
    n_sp = nc.dram_tensor("n_sp", [NCH, 64, H * D], BF16).ap()
    x2_sp = nc.dram_tensor("x2_sp", [NT, C], F32).ap()
    h1_sp = nc.dram_tensor("h1_sp", [NT, C], F32).ap()
    g_sp = nc.dram_tensor("g_sp", [NSLAB, HID, SLAB], F32).ap()
    pool2_d = nc.dram_tensor("pool2_d", [128, 8192], F32).ap()

    cc_nf_in = nc.dram_tensor("cc_nf_in", [64, H * D], F32).ap()
    cc_nf_out = nc.dram_tensor("cc_nf_out", [128, H * D], F32).ap()

    def cpair(nm, rows, cols, out_rows=None):
        return (nc.dram_tensor(nm + "_i", [rows, cols], F32).ap(),
                nc.dram_tensor(nm + "_o", [out_rows or rows, cols], F32).ap())

    cc_th = [cpair(f"cc_th{q}", 1, 8) for q in range(2)]
    cc_ct = [cpair(f"cc_ct{q}", 1, 8) for q in range(2)]
    cc_e = [[cpair(f"cc_e{q}_{i}", 1, 16) for i in range(4)] for q in range(2)]
    cc_gc = [cpair(f"cc_gc{q}", 128, 16, out_rows=1024) for q in range(2)]
    RG_ALL = [list(range(8))]
    RG_PAIR = [[0, 1], [2, 3], [4, 5], [6, 7]]

    for s in ["s_ld", "s_w", "s_mm", "s_ev", "s_sc", "s_sp", "s_cc", "s_out", "s_gp",
              "s_gs0", "s_gs1", "s_ps0", "s_ps1", "s_cr"]:
        p.sem(s)

    sbl = lambda name, shape, dt: es.enter_context(nc.sbuf_tensor(name, shape, dt))

    ident = sbl("ident", [128, 128], BF16)
    ones32 = sbl("ones32", [128, 128], F32)
    ident32 = sbl("ident32", [128, 128], F32)
    selcol = sbl("selcol", [64, 1], F32)
    lngb = sbl("lngb", [128, 2 * C], F32)
    stats = sbl("stats", [128, 64], F32)
    qrow = sbl("qrow", [1, 256], F32)
    qcol = sbl("qcol", [128, 16], F32)
    zbuf = sbl("zbuf", [128, C], F32) if STAGE < 5 else None
    LSattc = ExitStack()
    sbac = lambda name, shape, dt: LSattc.enter_context(nc.sbuf_tensor(name, shape, dt))
    ltm_all = sbac("ltm_all", [128, H * 128], BF16)
    utm_all = sbac("utm_all", [128, H * 128], BF16)
    sup_mask = sbac("sup_mask", [128, 512], F32)
    hselT = sbac("hselT", [16, C], F32)
    hsel = sbac("hsel", [128, 8 * 16], F32)
    ab_row = sbac("ab_row", [1, 4 * H], F32)
    acol = sbac("acol", [128, 4 * H], F32)
    ltm32 = sbac("ltm32", [128, 128], F32)
    eomask = sbac("eomask", [128, 2], F32)

    ps = [es.enter_context(nc.psum_tensor(f"ps{i}", [128, 512], F32)) for i in range(6)]
    pt = es.enter_context(nc.psum_tensor("pt", [128, 1024], BF16))
    BLK = es.enter_context(nc.Block())

    def fence(v):
        p.wait(v, "s_ev")

    if True:
        blk = BLK
        @blk.sync
        def _(s):
            for dst, srcd in [(ones32, ones_d), (sup_mask, utm4_d), (hselT, hselT_d),
                             (ltm32, ltm_d), (selcol, selc_d)]:
                p.inc("s_w", s.dma_start(out=dst[:, :], in_=srcd[:, :]), 16)
            p.inc("s_w", s.dma_start(out=hsel.rearrange("p (k h) -> p k h", h=16),
                                     in_=hsel_d.rearrange("(k p) h -> p k h", p=128)), 16)
            p.inc("s_w", s.dma_start(out=ab_row[0:1, 0:H], in_=alpha_d[:, :]), 16)
            p.inc("s_w", s.dma_start(out=ab_row[0:1, H:2 * H], in_=beta_d[:, :]), 16)
            p.inc("s_w", s.dma_start(out=qrow[0:1, 0:16], in_=qtarg_d[:, :]), 16)
            p.inc("s_w", s.dma_start(out=lngb[0:1, 0:C], in_=lng_d[:, :]), 16)
            p.inc("s_w", s.dma_start(out=lngb[0:1, C:2 * C], in_=lnb_d[:, :]), 16)
            p.inc("s_w", s.dma_start(out=eomask[:, :], in_=eo_d[:, :]), 16)
            p.inc("s_w", s.dma_start(out=ident32[:, :], in_=ident_d[:, :]), 16)

        @blk.gpsimd
        def _(g):
            p.inc("s_gp", g.dma_start(out=ident[:, :], in_=ident_d[:, :]), 16)

        @blk.vector
        def _(v):
            p.inc("s_ev", v.memset(qrow[0:1, 16:256], 0.0))

    if True:

        blk = BLK
        @blk.scalar
        def _(s):
            p.wait(s, "s_w")
            p.inc("s_sc", s.activation(ab_row[0:1, 0:2 * H], ab_row[0:1, 0:2 * H],
                                       AF.Sigmoid))
            p.wait(s, "s_sc")
            p.inc("s_sc", s.mul(ab_row[0:1, 0:2 * H], ab_row[0:1, 0:2 * H], 0.5))
            p.wait(s, "s_sc")
            p.inc("s_sc", s.mul(ab_row[0:1, 2 * H:4 * H], ab_row[0:1, 0:2 * H], -1.0))

        @blk.tensor
        def _(t):
            p.wait(t, "s_sc")
            p.inc("s_mm", t.matmul(ps[0][:, 0:4 * H], ones32[0:1, :], ab_row[0:1, :],
                                   start=True, stop=True))

        @blk.vector
        def _(v):
            p.wait(v, "s_mm")
            p.inc("s_ev", v.tensor_copy(acol[:, :], ps[0][:, 0:4 * H]))
            p.wait(v, "s_ev")
            for hh in range(H):
                na = acol[:, 2 * H + hh:2 * H + hh + 1]
                v.tensor_scalar(ltm_all[:, hh * 128:(hh + 1) * 128], ltm32[:, :],
                                na, None, op0=ALU.mult)
            last = None
            for hh in range(H):
                na = acol[:, 2 * H + hh:2 * H + hh + 1]
                last = v.tensor_scalar(utm_all[:, hh * 128:(hh + 1) * 128],
                                       sup_mask[:, 0:128], na, None, op0=ALU.mult)
            p.inc("s_ev", last)

    for j in range(4):
        if True:
            blk = BLK
            @blk.tensor
            def _(t, j=j):
                p.wait(t, "s_w")
                p.wait(t, "s_ev")
                p.inc("s_mm", t.matmul(ps[1][:, :], ones32[0:1, :],
                                       lngb[0:1, j * 512:(j + 1) * 512],
                                       start=True, stop=True))

            @blk.vector
            def _(v, j=j):
                p.wait(v, "s_mm")
                p.inc("s_ev", v.tensor_copy(lngb[:, j * 512:(j + 1) * 512], ps[1][:, :]))

    LSw = ExitStack()
    sbw = lambda name, shape, dt: LSw.enter_context(nc.sbuf_tensor(name, shape, dt))
    LS3h = ExitStack()
    habsA = LS3h.enter_context(nc.sbuf_tensor("habsA", [128, 4 * C], BF16))
    LSwo = ExitStack()
    wo_sb = LSwo.enter_context(nc.sbuf_tensor("wo_sb", [128, 8 * C], BF16))
    wq_sb = sbw("wq_sb", [128, 8 * C], BF16)
    wk_sb = sbw("wk_sb", [128, 8 * C], BF16)
    wv_sb = sbw("wv_sb", [128, 8 * C], BF16)
    if True:
        blk = BLK
        @blk.gpsimd
        def _(g):
            for w_sb, w_d in [(wq_sb, wq_d), (wk_sb, wk_d), (wv_sb, wv_d), (wo_sb, wo_d)]:
                p.inc("s_gp", g.dma_start(out=w_sb.rearrange("p (k o) -> p k o", o=C),
                                         in_=w_d.rearrange("(k p) o -> p k o", p=128)), 16)
    W_READY = p.c["s_gp"]

    LS1 = ExitStack()
    sb1 = lambda name, shape, dt: LS1.enter_context(nc.sbuf_tensor(name, shape, dt))
    xsl = sb1("xsl", [128, 4 * C], F32)
    scr32 = sb1("scr32", [128, 2 * C], F32)
    xlnb = sb1("xlnb", [128, 4 * C], BF16)
    xlnT = sb1("xlnT", [128, 8 * SLAB], BF16)
    q_fm = sb1("q_fm", [128, 8 * SLAB], BF16)
    k_fm = sb1("k_fm", [128, 8 * SLAB], BF16)
    kt_fm = sb1("kt_fm", [128, 8 * SLAB], BF16)
    kto = sb1("kto", [128, 8 * SLAB], BF16)
    v_tok = sb1("v_tok", [128, 4 * C], BF16)
    ktok = sb1("ktok", [128, C], BF16)
    xq = sb1("xq", [128, SLAB], BF16)
    pl_a = sb1("pl_a", [128, SLAB], BF16)
    pu_a = sb1("pu_a", [128, SLAB], BF16)
    pl_b = sb1("pl_b", [128, SLAB], BF16)
    pu_b = sb1("pu_b", [128, SLAB], BF16)
    naw = sb1("naw", [128, 256], BF16)
    rbq = sb1("rbq", [128, 256], BF16)
    sup_q = sb1("sup_q", [128, SLAB], BF16)
    pl_c = sb1("pl_c", [128, SLAB], BF16)
    pu_c = sb1("pu_c", [128, SLAB], BF16)
    pl_d = sb1("pl_d", [128, SLAB], BF16)
    pu_d = sb1("pu_d", [128, SLAB], BF16)
    xq2 = sb1("xq2", [128, SLAB], BF16)
    naw2 = sb1("naw2", [128, 256], BF16)
    rbq2 = sb1("rbq2", [128, 256], BF16)
    sup_q2 = sb1("sup_q2", [128, SLAB], BF16)
    ghq = sb1("ghq", [64, H * D], BF16)
    bhq = sb1("bhq", [64, H * D], BF16)
    qhats = sb1("qhats", [64, H * CS], BF16)
    yints = sb1("yints", [128, H * D], F32)

    h1sp_sn = {}

    def p1_chunk(s, ccq, cc):
        for half in range(2):
            if True:
                blk = BLK
                @blk.tensor
                def _(t, half=half):
                    p.wait(t, "s_sc")
                    last = None
                    for j in range(4):
                        kk = half * 4 + j
                        last = t.transpose(pt[:, j * 256:j * 256 + 128],
                                           kt_fm[:, kk * SLAB + ccq * 128:
                                                 kk * SLAB + (ccq + 1) * 128],
                                           ident[:, :])
                    p.inc("s_mm", last)

                @blk.vector
                def _(v, half=half):
                    p.wait(v, "s_mm")
                    last = None
                    for j in range(4):
                        kk = half * 4 + j
                        last = v.tensor_copy(ktok[:, kk * 128:(kk + 1) * 128],
                                       pt[:, j * 256:j * 256 + 128])
                    p.inc("s_sc", last)
        KT_DONE = p.c["s_sc"]
        if 'quads' not in FEAT:
            return

        for qp in range(2):
            qa, qb = 2 * qp, 2 * qp + 1
            SETS = [
                dict(qd=qa, heads=[4 * qa + j for j in range(4)],
                     pl0=pl_a, pu0=pu_a, pl1=pl_b, pu1=pu_b, xqt=xq,
                     nawt=naw, rbqt=rbq, supt=sup_q, so=0, po=512,
                     bX=ps[1], bL=ps[2], bU=ps[3], bGh=ps[4], bPq=ps[0]),
                dict(qd=qb, heads=[4 * qb + j for j in range(4)],
                     pl0=pl_c, pu0=pu_c, pl1=pl_d, pu1=pu_d, xqt=xq2,
                     nawt=naw2, rbqt=rbq2, supt=sup_q2, so=1024, po=1536,
                     bX=ps[4], bL=ps[5], bU=ps[0], bGh=ps[5], bPq=ps[2]),
            ]
            blk = BLK

            def st_mms(t, S):
                last = None
                for j, hh in enumerate(S['heads']):
                    kk = hh // 2
                    ksel = (k_fm if hh % 2 == 0 else kto)[
                        :, kk * SLAB + ccq * 128:kk * SLAB + (ccq + 1) * 128]
                    kful = kt_fm[:, kk * SLAB + ccq * 128:kk * SLAB + (ccq + 1) * 128]
                    last = t.matmul(ps[0][:, j * 128:(j + 1) * 128], ksel, kful,
                                    start=True, stop=True)
                p.inc("s_mm", last)

            def masks_x0(v, S):
                qd = S['qd']
                v.tensor_tensor(S['pl0'][:, :], ps[0][:, :],
                                ltm_all[:, qd * 512:(qd + 1) * 512], op=ALU.mult)
                p.inc("s_ev", v.tensor_tensor(
                    S['pu0'][:, :], ps[0][:, :],
                    utm_all[:, qd * 512:(qd + 1) * 512], op=ALU.mult))
                S['mk'] = p.c["s_ev"]
                v.tensor_copy(
                    S['xqt'].rearrange("p (j two d) -> p j two d", two=2, d=64)[:, :, 0, :],
                    ktok.rearrange("p (j d) -> p j d", d=64)[:, qd * 4:qd * 4 + 4, :])
                p.inc("s_ev", v.tensor_copy(
                    S['xqt'].rearrange("p (j two d) -> p j two d", two=2, d=64)[:, :, 1, :],
                    v_tok.rearrange("p (cq j d) -> p cq j d", cq=4, d=64)
                    [:, ccq, qd * 4:qd * 4 + 4, :]))

            @blk.tensor
            def _(t):
                p.wait(t, "s_sc")
                p.wait(t, "s_ev")
                st_mms(t, SETS[0])
            stA = p.c["s_mm"]

            @blk.vector
            def _(v):
                p.wait(v, "s_mm", stA)
                masks_x0(v, SETS[0])
            xA = p.c["s_ev"]

            @blk.tensor
            def _(t, mkA=SETS[0]['mk']):
                p.wait(t, "s_ev", mkA)
                st_mms(t, SETS[1])
            stB = p.c["s_mm"]

            @blk.vector
            def _(v):
                p.wait(v, "s_mm", stB)
                masks_x0(v, SETS[1])
            xB = p.c["s_ev"]

            nA = nB = None
            for lvl in range(7):
                for si, S in enumerate(SETS):
                    cl, cu = (S['pl0'], S['pu0']) if lvl % 2 == 0 else (S['pl1'], S['pu1'])
                    nl, nu = (S['pl1'], S['pu1']) if lvl % 2 == 0 else (S['pl0'], S['pu0'])

                    @blk.tensor
                    def _(t, S=S, cl=cl, cu=cu, lvl=lvl,
                          xs=(xA if si == 0 else xB),
                          ns=(nA if si == 0 else nB)):
                        p.wait(t, "s_ev", xs)
                        if ns is not None:
                            p.wait(t, "s_sc", ns)
                        bX, bL, bU = S['bX'], S['bL'], S['bU']
                        last = None
                        for j in range(4):
                            last = t.matmul(bX[:, j * 128:(j + 1) * 128],
                                            cu[:, j * 128:(j + 1) * 128],
                                            S['xqt'][:, j * 128:(j + 1) * 128],
                                            start=True, stop=True)
                        if lvl < 6:
                            for j in range(4):
                                t.matmul(bL[:, j * 128:(j + 1) * 128],
                                         cu[:, j * 128:(j + 1) * 128],
                                         cl[:, j * 128:(j + 1) * 128],
                                         start=True, stop=True)
                            for j in range(4):
                                last = t.matmul(bU[:, j * 128:(j + 1) * 128],
                                                cl[:, j * 128:(j + 1) * 128],
                                                cu[:, j * 128:(j + 1) * 128],
                                                start=True, stop=True)
                        p.inc("s_mm", last)
                    msnap = p.c["s_mm"]

                    @blk.vector
                    def _(v, S=S, msnap=msnap, nu=nu, lvl=lvl):
                        p.wait(v, "s_mm", msnap)
                        v.tensor_tensor(S['xqt'][:, :], S['xqt'][:, :],
                                        S['bX'][:, :], op=ALU.add)
                        if lvl < 6:
                            p.inc("s_ev", v.tensor_copy(nu[:, :], S['bU'][:, :]))
                        else:
                            p.inc("s_ev", v.engine_nop())
                    if si == 0:
                        xA = p.c["s_ev"]
                    else:
                        xB = p.c["s_ev"]

                    if lvl < 6:
                        @blk.scalar
                        def _(sc, S=S, msnap=msnap, nl=nl):
                            p.wait(sc, "s_mm", msnap)
                            p.inc("s_sc", sc.copy(nl[:, :], S['bL'][:, :]))
                        if si == 0:
                            nA = p.c["s_sc"]
                        else:
                            nB = p.c["s_sc"]

            nw = [None, None]
            for si, S in enumerate(SETS):
                @blk.vector
                def _(v, S=S):
                    fence(v)
                    for j, hh in enumerate(S['heads']):
                        v.tensor_scalar(S['nawt'][:, j * 64:(j + 1) * 64],
                                        S['xqt'][:, j * 128:j * 128 + 64],
                                        acol[:, 2 * H + hh:2 * H + hh + 1],
                                        None, op0=ALU.mult)
                    last = None
                    for j, hh in enumerate(S['heads']):
                        last = v.tensor_scalar(S['rbqt'][:, j * 64:(j + 1) * 64],
                                               S['xqt'][:, j * 128 + 64:(j + 1) * 128],
                                               acol[:, H + hh:H + hh + 1],
                                               None, op0=ALU.mult)
                    p.inc("s_ev", last)
                nw[si] = p.c["s_ev"]

            gsn = [None, None]
            sup_sn = [None, None]
            for si, S in enumerate(SETS):
                @blk.tensor
                def _(t, S=S, si=si):
                    p.wait(t, "s_ev", nw[si])
                    p.wait(t, "s_sc", nB)
                    bGh, bPq = S['bGh'], S['bPq']
                    for j, hh in enumerate(S['heads']):
                        t.matmul(bGh[0:64, j * 64:(j + 1) * 64],
                                 S['nawt'][:, j * 64:(j + 1) * 64],
                                 ktok[:, hh * 64:(hh + 1) * 64],
                                 start=True, stop=True)
                    for j, hh in enumerate(S['heads']):
                        t.matmul(bGh[0:64, 256 + j * 64:256 + (j + 1) * 64],
                                 ktok[:, hh * 64:(hh + 1) * 64],
                                 S['rbqt'][:, j * 64:(j + 1) * 64],
                                 start=True, stop=True)
                    last = None
                    for j, hh in enumerate(S['heads']):
                        kk = hh // 2
                        kslice = (k_fm if hh % 2 == 0 else kto)[
                            :, kk * SLAB + ccq * 128:kk * SLAB + (ccq + 1) * 128]
                        qslice = q_fm[:, kk * SLAB + ccq * 128:kk * SLAB + (ccq + 1) * 128]
                        last = t.matmul(bPq[:, j * 128:(j + 1) * 128],
                                        kslice, qslice, start=True, stop=True)
                    p.inc("s_mm", last)
                gsn[si] = p.c["s_mm"]

                @blk.vector
                def _(v, S=S, si=si):
                    p.wait(v, "s_mm", gsn[si])
                    p.wait(v, "s_sp")
                    qd = S['qd']
                    v.tensor_copy(ghq[:, qd * 256:(qd + 1) * 256], S['bGh'][0:64, 0:256])
                    p.inc("s_sc", v.tensor_copy(bhq[:, qd * 256:(qd + 1) * 256],
                                                S['bGh'][0:64, 256:512]))

                @blk.vector
                def _(v, S=S, si=si):
                    p.wait(v, "s_mm", gsn[si])
                    p.inc("s_ev", v.tensor_tensor(S['supt'][:, :], S['bPq'][:, :],
                                                  sup_mask[:, :], op=ALU.mult))
                sup_sn[si] = p.c["s_ev"]

            prev_q = None
            for si, S in enumerate(SETS):
                @blk.tensor
                def _(t, S=S, si=si, prev_q=prev_q):
                    p.wait(t, "s_ev", sup_sn[si] if si == 0 else p.c["s_ev"])
                    p.wait(t, "s_sc", nB if si == 0 else prev_q)
                    for j, hh in enumerate(S['heads']):
                        kk, hf = hh // 2, (hh % 2) * 64
                        qtile = q_fm[:, kk * SLAB + ccq * 128:kk * SLAB + (ccq + 1) * 128]
                        t.matmul(ps[1][0:64, j * 128:(j + 1) * 128],
                                 ident[:, hf:hf + 64], qtile, start=True, stop=False)
                        t.matmul(ps[1][0:64, j * 128:(j + 1) * 128],
                                 S['nawt'][:, j * 64:(j + 1) * 64],
                                 S['supt'][:, j * 128:(j + 1) * 128],
                                 start=False, stop=True)
                    last = None
                    for j in range(4):
                        last = t.matmul(ps[3][:, j * 64:(j + 1) * 64],
                                        S['supt'][:, j * 128:(j + 1) * 128],
                                        S['rbqt'][:, j * 64:(j + 1) * 64],
                                        start=True, stop=True)
                    p.inc("s_mm", last)
                qsn = p.c["s_mm"]

                @blk.vector
                def _(v, S=S, qsn=qsn):
                    p.wait(v, "s_mm", qsn)
                    p.wait(v, "s_sp")
                    qd = S['qd']
                    p.inc("s_sc", v.tensor_copy(qhats[:, qd * 512:(qd + 1) * 512],
                                                ps[1][0:64, :]))
                    p.inc("s_ev", v.tensor_copy(yints[:, qd * 256:(qd + 1) * 256],
                                                ps[3][:, 0:256]))
                prev_q = p.c["s_sc"]

        SP_READY_SC = p.c["s_sc"]
        SP_READY_EV = p.c["s_ev"]
        if True:
            blk = BLK
            @blk.sync
            def _(sy):
                p.wait(sy, "s_sc", SP_READY_SC)
                p.wait(sy, "s_ev", SP_READY_EV)
                p.inc("s_sp", sy.dma_start(out=qhat_sp[cc], in_=qhats[:, :]), 16)
                p.inc("s_sp", sy.dma_start(out=yint_sp[cc], in_=yints[:, :]), 16)
                p.inc("s_sp", sy.dma_start(out=gh_sp[cc], in_=ghq[:, :]), 16)
                p.inc("s_sp", sy.dma_start(out=bh_sp[cc], in_=bhq[:, :]), 16)

    def p1_slab(s):
        if True:
            blk = BLK
            @blk.sync
            def _(sy):
                p.wait(sy, "s_ev")
                for ccq in range(4):
                    p.inc("s_ld", sy.dma_start(
                        out=xsl[:, ccq * C:(ccq + 1) * C],
                        in_=xin[(s * 4 + ccq) * 128:(s * 4 + ccq + 1) * 128, :]), 16)

        if True:
            blk = BLK
            @blk.vector
            def _(v):
                p.wait(v, "s_ld")
                if s == 0:
                    p.wait(v, "s_ev")
                for ccq in range(4):
                    xt = xsl[:, ccq * C:(ccq + 1) * C]
                    mcol = stats[:, ccq:ccq + 1]
                    p.inc("s_ev", v.reduce_sum(mcol, xt, axis=AX.X))
                    p.wait(v, "s_ev")
                    p.inc("s_ev", v.tensor_scalar(mcol, mcol, 1.0 / C, None,
                                                  op0=ALU.mult))
                    p.wait(v, "s_ev")
                    p.inc("s_ev", v.tensor_scalar(xt, xt, mcol, None,
                                                  op0=ALU.subtract))

            @blk.vector
            def _(v):
                p.wait(v, "s_ev")
                p.inc("s_ev", v.tensor_tensor(scr32[:, :], xsl[:, 0:2 * C],
                                              xsl[:, 0:2 * C], op=ALU.mult))
                fence(v)
                last = None
                for ccq in range(2):
                    last = v.reduce_sum(stats[:, 4 + ccq:5 + ccq],
                                 scr32[:, ccq * C:(ccq + 1) * C], axis=AX.X)
                p.inc("s_ev", last)

            @blk.vector
            def _(v):
                p.wait(v, "s_ev")
                p.inc("s_ev", v.tensor_tensor(scr32[:, :], xsl[:, 2 * C:4 * C],
                                              xsl[:, 2 * C:4 * C], op=ALU.mult))
                fence(v)
                last = None
                for ccq in range(2, 4):
                    last = v.reduce_sum(stats[:, 4 + ccq:5 + ccq],
                                 scr32[:, (ccq - 2) * C:(ccq - 1) * C], axis=AX.X)
                p.inc("s_ev", last)
                p.wait(v, "s_ev")
                p.inc("s_ev", v.tensor_scalar(stats[:, 8:12], stats[:, 4:8],
                                              1.0 / C, 1e-5, op0=ALU.mult, op1=ALU.add))

            @blk.vector
            def _(v):
                p.wait(v, "s_ev")
                p.inc("s_ev", v.reciprocal(stats[:, 8:12], stats[:, 8:12]))

            @blk.scalar
            def _(sc):
                p.wait(sc, "s_ev")
                p.inc("s_sc", sc.activation(stats[:, 8:12], stats[:, 8:12], AF.Sqrt))

            @blk.vector
            def _(v):
                p.wait(v, "s_sc")
                last = None
                for ccq in range(4):
                    last = v.tensor_scalar(xlnb[:, ccq * C:(ccq + 1) * C],
                                           xsl[:, ccq * C:(ccq + 1) * C],
                                           stats[:, 8 + ccq:9 + ccq], None,
                                           op0=ALU.mult)
                p.inc("s_ev", last)
        LN_DONE = p.c["s_ev"]

        if 'xt' not in FEAT:
            return
        blocks = [(ccq, kk) for ccq in range(4) for kk in range(8)]
        for bi in range(0, 32, 4):
            if True:
                blk = BLK
                @blk.tensor
                def _(t, bi=bi):
                    p.wait(t, "s_ev", LN_DONE)
                    if s == 0 and bi == 0:
                        p.wait(t, "s_gp", W_READY)
                    p.wait(t, "s_sc")
                    last = None
                    for j in range(4):
                        ccq, kk = blocks[bi + j]
                        last = t.transpose(
                            pt[:, j * 256:j * 256 + 128],
                            xlnb[:, ccq * C + kk * 128:ccq * C + (kk + 1) * 128],
                            ident[:, :])
                    p.inc("s_mm", last)

                @blk.vector
                def _(v, bi=bi):
                    p.wait(v, "s_mm")
                    last = None
                    for j in range(4):
                        ccq, kk = blocks[bi + j]
                        last = v.tensor_copy(xlnT[:, kk * SLAB + ccq * 128:
                                            kk * SLAB + (ccq + 1) * 128],
                                       pt[:, j * 256:j * 256 + 128])
                    p.inc("s_sc", last)
        XT_DONE = p.c["s_sc"]

        if 'proj' not in FEAT:
            return
        for wsb, dst in [(wq_sb, q_fm), (wk_sb, k_fm)]:
            for mo in range(8):
                bank = ps[mo % 2]
                if True:
                    blk = BLK
                    @blk.tensor
                    def _(t, mo=mo, wsb=wsb, bank=bank):
                        p.wait(t, "s_sc", XT_DONE)
                        p.wait(t, "s_ev")
                        last = None
                        for kk in range(8):
                            last = t.matmul(
                                bank[:, :],
                                wsb[:, kk * C + mo * 128:kk * C + (mo + 1) * 128],
                                xlnT[:, kk * SLAB:(kk + 1) * SLAB],
                                start=(kk == 0), stop=(kk == 7))
                        p.inc("s_mm", last)

                    @blk.vector
                    def _(v, mo=mo, dst=dst, bank=bank):
                        p.wait(v, "s_mm")
                        p.inc("s_ev", v.tensor_copy(dst[:, mo * SLAB:(mo + 1) * SLAB],
                                                    bank[:, :]))

        for ccq in range(4):
            for ob in range(2):
                bank = ps[2 + (ob % 2)]
                if True:
                    blk = BLK
                    @blk.tensor
                    def _(t, ccq=ccq, ob=ob, bank=bank):
                        p.wait(t, "s_ev")
                        p.wait(t, "s_sc")
                        last = None
                        for kk in range(8):
                            last = t.matmul(
                                bank[:, :],
                                xlnT[:, kk * SLAB + ccq * 128:kk * SLAB + (ccq + 1) * 128],
                                wv_sb[:, kk * C + ob * 512:kk * C + (ob + 1) * 512],
                                start=(kk == 0), stop=(kk == 7))
                        p.inc("s_mm", last)

                    @blk.vector
                    def _(v, ccq=ccq, ob=ob, bank=bank):
                        p.wait(v, "s_mm")
                        p.inc("s_sc", v.tensor_copy(
                            v_tok[:, ccq * C + ob * 512:ccq * C + (ob + 1) * 512],
                            bank[:, :]))

        if 'knorm' not in FEAT:
            return
        if True:
            blk = BLK
            @blk.vector
            def _(v):
                p.wait(v, "s_ev")
                p.inc("s_ev", v.tensor_tensor(scr32[:, :], k_fm[:, 0:4 * SLAB],
                                              k_fm[:, 0:4 * SLAB], op=ALU.mult))

            @blk.tensor
            def _(t):
                p.wait(t, "s_ev")
                last = None
                for kk in range(4):
                    last = t.matmul(ps[4][0:16, :], hsel[:, kk * 16:(kk + 1) * 16],
                                    scr32[:, kk * SLAB:(kk + 1) * SLAB],
                                    start=(kk == 0), stop=False)
                p.inc("s_mm", last)

            @blk.vector
            def _(v):
                p.wait(v, "s_mm")
                p.inc("s_ev", v.tensor_tensor(scr32[:, :], k_fm[:, 4 * SLAB:8 * SLAB],
                                              k_fm[:, 4 * SLAB:8 * SLAB], op=ALU.mult))

            @blk.tensor
            def _(t):
                p.wait(t, "s_ev")
                last = None
                for kk in range(4):
                    last = t.matmul(ps[4][0:16, :], hsel[:, (4 + kk) * 16:(5 + kk) * 16],
                                    scr32[:, kk * SLAB:(kk + 1) * SLAB],
                                    start=False, stop=(kk == 3))
                p.inc("s_mm", last)

            @blk.vector
            def _(v):
                p.wait(v, "s_mm")
                p.inc("s_ev", v.tensor_scalar(scr32[0:16, 0:SLAB],
                                              ps[4][0:16, :], 1e-24, None, op0=ALU.max))

            @blk.vector
            def _(v):
                p.wait(v, "s_ev")
                p.inc("s_ev", v.reciprocal(scr32[0:16, 0:SLAB], scr32[0:16, 0:SLAB]))

            @blk.scalar
            def _(sc):
                p.wait(sc, "s_ev")
                p.inc("s_sc", sc.activation(scr32[0:16, 0:SLAB], scr32[0:16, 0:SLAB], AF.Sqrt))

        for kk in range(8):
            bank = ps[2 + (kk % 2)]
            if True:
                blk = BLK
                @blk.tensor
                def _(t, kk=kk, bank=bank):
                    p.wait(t, "s_sc")
                    p.wait(t, "s_ev")
                    p.inc("s_mm", t.matmul(bank[:, :], hselT[:, kk * 128:(kk + 1) * 128],
                                           scr32[0:16, 0:SLAB], start=True, stop=True))

                @blk.vector
                def _(v, kk=kk, bank=bank):
                    p.wait(v, "s_mm")
                    p.inc("s_ev", v.tensor_tensor(kt_fm[:, kk * SLAB:(kk + 1) * SLAB],
                                                  k_fm[:, kk * SLAB:(kk + 1) * SLAB],
                                                  bank[:, :], op=ALU.mult))
                    p.wait(v, "s_ev")
                    v.tensor_scalar(k_fm[:, kk * SLAB:(kk + 1) * SLAB],
                                    kt_fm[:, kk * SLAB:(kk + 1) * SLAB],
                                    eomask[:, 0:1], None, op0=ALU.mult)
                    p.inc("s_ev", v.tensor_scalar(kto[:, kk * SLAB:(kk + 1) * SLAB],
                                                  kt_fm[:, kk * SLAB:(kk + 1) * SLAB],
                                                  eomask[:, 1:2], None, op0=ALU.mult))

        if 'chunks' in FEAT:
            for ccq in range(4):
                p1_chunk(s, ccq, s * 4 + ccq)

    nslab_run = NSLAB if STAGE >= 2 else (1 if STAGE >= 1 else 0)
    for s in range(nslab_run):
        p1_slab(s)

    def dbg_store(name, src_ap, rows=None):
        if name in dbg_d:
            if True:
                blk = BLK
                @blk.gpsimd
                def _(sy):
                    p.wait(sy, "s_ev")
                    p.wait(sy, "s_sc")
                    p.wait(sy, "s_mm")
                    p.inc("s_gp", sy.dma_start(out=dbg_d[name][:, :], in_=src_ap), 16)

    dbg_store("d_qfm", q_fm[:, :])
    dbg_store("d_ktfm", kt_fm[:, :])
    dbg_store("d_vtok", v_tok[:, :])
    dbg_store("d_xq", xq[:, :])
    dbg_store("d_ghq", ghq[:, :])
    dbg_store("d_bhq", bhq[:, :])
    dbg_store("d_qhats", qhats[:, :])
    dbg_store("d_yints", yints[:, :])
    LS1.close()
    LSw.close()

    if STAGE >= 3:
        LS2 = ExitStack()
        sb2 = lambda name, shape, dt: LS2.enter_context(nc.sbuf_tensor(name, shape, dt))
        ghb = sb2("ghb", [64, 2 * H * D], BF16)
        bhb = sb2("bhb", [64, 2 * H * D], BF16)
        nst = sb2("nst", [64, 2 * H * D], BF16)
        nb_s = sb2("nb_s", [64, H * D], BF16)
        nfin = sb2("nfin", [64, H * D], F32)
        ninit = sb2("ninit", [64, H * D], F32)

        def scan_pass(save):
            if True:
                blk = BLK
                @blk.vector
                def _(v):
                    p.wait(v, "s_cc")
                    p.wait(v, "s_ld")
                    p.wait(v, "s_cr")
                    if not save:
                        p.inc("s_ev", v.memset(nst[:, 0:H * D], 0.0))
                    else:
                        p.inc("s_ev", v.tensor_scalar(nst[:, 0:H * D], ninit[:, :],
                                                      selcol[:, 0:1], None, op0=ALU.mult))
                @blk.sync
                def _(sy):
                    p.wait(sy, "s_ev")
                    if save:
                        p.inc("s_sp", sy.dma_start(out=n_sp[0], in_=nst[:, 0:H * D]), 16)
            for cc in range(NCH):
                cur, nxt = cc % 2, (cc + 1) % 2
                if True:
                    blk = BLK
                    @blk.sync
                    def _(sy):
                        p.wait(sy, "s_sc")
                        p.wait(sy, "s_sp")
                        p.inc("s_ld", sy.dma_start(out=ghb[:, cur * H * D:(cur + 1) * H * D],
                                                   in_=gh_sp[cc]), 16)
                        p.inc("s_ld", sy.dma_start(out=bhb[:, cur * H * D:(cur + 1) * H * D],
                                                   in_=bh_sp[cc]), 16)

                    @blk.vector
                    def _(v):
                        p.wait(v, "s_ld")
                        p.wait(v, "s_ev")
                        p.inc("s_ev", v.tensor_tensor(nb_s[:, :], nst[:, cur * H * D:(cur + 1) * H * D],
                                                      bhb[:, cur * H * D:(cur + 1) * H * D],
                                                      op=ALU.add))

                    @blk.tensor
                    def _(t):
                        p.wait(t, "s_ev")
                        p.wait(t, "s_sc")
                        last = None
                        for hh in range(H):
                            reg = ps[4 if hh < 8 else 5][0:64,
                                  (hh % 8) * 64:(hh % 8 + 1) * 64]
                            t.matmul(reg, ghb[:, cur * H * D + hh * D:cur * H * D + (hh + 1) * D],
                                     nst[:, cur * H * D + hh * D:cur * H * D + (hh + 1) * D],
                                     start=True, stop=False)
                            last = t.matmul(reg, ident[0:64, 0:64],
                                            nb_s[:, hh * D:(hh + 1) * D],
                                            start=False, stop=True)
                        p.inc("s_mm", last)

                    @blk.vector
                    def _(v):
                        p.wait(v, "s_mm")
                        v.tensor_copy(nst[:, nxt * H * D:nxt * H * D + 512], ps[4][0:64, :])
                        p.inc("s_sc", v.tensor_copy(nst[:, nxt * H * D + 512:(nxt + 1) * H * D],
                                              ps[5][0:64, :]))

                    @blk.sync
                    def _(sy, cc=cc):
                        p.wait(sy, "s_sc")
                        if save and cc < NCH - 1:
                            p.inc("s_sp", sy.dma_start(out=n_sp[cc + 1],
                                                       in_=nst[:, nxt * H * D:(nxt + 1) * H * D]), 16)

            fin = (NCH % 2)
            return fin

        fin = scan_pass(save=False)
        if True:
            blk = BLK
            @blk.vector
            def _(v):
                p.wait(v, "s_sc")
                p.inc("s_ev", v.tensor_copy(nfin[:, :], nst[:, fin * H * D:(fin + 1) * H * D]))

            @blk.sync
            def _(sy):
                p.wait(sy, "s_ev")
                p.inc("s_sp", sy.dma_start(out=cc_nf_in[:, :], in_=nfin[:, :]), 16)

            @blk.gpsimd
            def _(g):
                p.wait(g, "s_sp")
                p.inc("s_cc", g.collective_compute(
                    "AllGather", ALU.bypass, replica_groups=RG_PAIR,
                    ins=[cc_nf_in[:, :]], outs=[cc_nf_out[:, :]]), 1)

            @blk.sync
            def _(sy):
                p.wait(sy, "s_cc")
                p.inc("s_cr", sy.dma_start(out=ninit[:, :], in_=cc_nf_out[0:64, :]), 16)

        if True:

            blk = BLK
            @blk.vector
            def _(v):
                p.wait(v, "s_ld")
                p.inc("s_ev", v.engine_nop())
        scan_pass(save=True)
        LS2.close()

    if STAGE >= 4:
        LS3 = ExitStack()
        sb3 = lambda name, shape, dt: LS3.enter_context(nc.sbuf_tensor(name, shape, dt))
        qhb = sb3("qhb", [64, H * CS], BF16)
        yib = sb3("yib", [128, H * D], F32)
        n0b = sb3("n0b", [64, H * D], BF16)
        xrb = sb3("xrb", [128, C], F32)
        ych = sb3("ych", [128, C], BF16)
        yT = sb3("yT", [128, C], BF16)
        p3s = sb3("p3s", [128, 4 * C], F32)

        for cc in range(NCH):
            if True:
                blk = BLK
                @blk.sync
                def _(sy, cc=cc):
                    p.wait(sy, "s_ev")
                    p.wait(sy, "s_sc")
                    p.wait(sy, "s_sp")
                    p.inc("s_ld", sy.dma_start(out=qhb[:, :], in_=qhat_sp[cc]), 16)
                    p.inc("s_ld", sy.dma_start(out=yib[:, :], in_=yint_sp[cc]), 16)
                    p.inc("s_ld", sy.dma_start(out=n0b[:, :], in_=n_sp[cc]), 16)
                    p.inc("s_ld", sy.dma_start(out=xrb[:, :],
                                               in_=xin[cc * 128:(cc + 1) * 128, :]), 16)

                @blk.tensor
                def _(t):
                    p.wait(t, "s_ld")
                    p.wait(t, "s_ev")
                    last = None
                    for hh in range(H):
                        reg = ps[0 if hh < 8 else 1][:, (hh % 8) * 64:(hh % 8 + 1) * 64]
                        last = t.matmul(reg, qhb[:, hh * CS:(hh + 1) * CS],
                                        n0b[:, hh * D:(hh + 1) * D],
                                        start=True, stop=True)
                    p.inc("s_mm", last)

                @blk.vector
                def _(v):
                    p.wait(v, "s_mm")
                    v.tensor_tensor(ych[:, 0:512], ps[0][:, :], yib[:, 0:512], op=ALU.add)
                    p.inc("s_ev", v.tensor_tensor(ych[:, 512:1024], ps[1][:, :],
                                                  yib[:, 512:1024], op=ALU.add))
            Y_DONE = p.c["s_ev"]

            for half in range(2):
                if True:
                    blk = BLK
                    @blk.tensor
                    def _(t, half=half):
                        p.wait(t, "s_ev", Y_DONE)
                        p.wait(t, "s_sc")
                        last = None
                        for j in range(4):
                            kk = half * 4 + j
                            last = t.transpose(pt[:, j * 256:j * 256 + 128],
                                               ych[:, kk * 128:(kk + 1) * 128],
                                               ident[:, :])
                        p.inc("s_mm", last)

                    @blk.vector
                    def _(v, half=half):
                        p.wait(v, "s_mm")
                        last = None
                        for j in range(4):
                            kk = half * 4 + j
                            last = v.tensor_copy(yT[:, kk * 128:(kk + 1) * 128],
                                           pt[:, j * 256:j * 256 + 128])
                        p.inc("s_sc", last)
            YT_DONE = p.c["s_sc"]

            if True:
                blk = BLK
                @blk.tensor
                def _(t):
                    p.wait(t, "s_sc", YT_DONE)
                    p.wait(t, "s_ev")
                    last = None
                    for ob in range(2):
                        for kk in range(8):
                            last = t.matmul(ps[2 + ob][:, :],
                                            yT[:, kk * 128:(kk + 1) * 128],
                                            wo_sb[:, kk * C + ob * 512:kk * C + (ob + 1) * 512],
                                            start=(kk == 0), stop=(kk == 7))
                    p.inc("s_mm", last)

                @blk.vector
                def _(v):
                    p.wait(v, "s_mm")
                    v.tensor_copy(p3s[:, 0:512], ps[2][:, :])
                    p.inc("s_sc", v.tensor_copy(p3s[:, 512:1024], ps[3][:, :]))

            def ln_stats(v, sc, seg, mc, vc):
                pass

            if True:

                blk = BLK
                @blk.vector
                def _(v):
                    p.wait(v, "s_sc")
                    att = p3s[:, 0:1024]
                    p.inc("s_ev", v.reduce_sum(stats[:, 0:1], att, axis=AX.X))
                    p.wait(v, "s_ev")
                    p.inc("s_ev", v.tensor_scalar(stats[:, 0:1], stats[:, 0:1],
                                                  1.0 / C, None, op0=ALU.mult))
                    p.wait(v, "s_ev")
                    p.inc("s_ev", v.tensor_scalar(att, att, stats[:, 0:1], None,
                                                  op0=ALU.subtract))

                @blk.vector
                def _(v):
                    p.wait(v, "s_ev")
                    p.inc("s_ev", v.tensor_tensor(p3s[:, 3072:4096], p3s[:, 0:1024],
                                                  p3s[:, 0:1024], op=ALU.mult))
                    fence(v)
                    p.inc("s_ev", v.reduce_sum(stats[:, 1:2], p3s[:, 3072:4096], axis=AX.X))
                    p.wait(v, "s_ev")
                    p.inc("s_ev", v.tensor_scalar(stats[:, 1:2], stats[:, 1:2],
                                                  1.0 / C, 1e-5, op0=ALU.mult, op1=ALU.add))

                @blk.vector
                def _(v):
                    p.wait(v, "s_ev")
                    p.inc("s_ev", v.reciprocal(stats[:, 1:2], stats[:, 1:2]))

                @blk.scalar
                def _(sc):
                    p.wait(sc, "s_ev")
                    p.inc("s_sc", sc.activation(stats[:, 1:2], stats[:, 1:2], AF.Sqrt))

                @blk.vector
                def _(v):
                    p.wait(v, "s_sc")
                    p.wait(v, "s_sp")
                    p.inc("s_ev", v.tensor_scalar(p3s[:, 0:1024], p3s[:, 0:1024],
                                                  stats[:, 1:2], None, op0=ALU.mult))
                    p.wait(v, "s_ev")
                    p.inc("s_ev", v.tensor_tensor(p3s[:, 1024:2048], xrb[:, :],
                                                  p3s[:, 0:1024], op=ALU.add))

            if True:
                blk = BLK
                @blk.vector
                def _(v):
                    p.wait(v, "s_ev")
                    x2 = p3s[:, 1024:2048]
                    p.inc("s_ev", v.reduce_sum(stats[:, 2:3], x2, axis=AX.X))
                    p.wait(v, "s_ev")
                    p.inc("s_ev", v.tensor_scalar(stats[:, 2:3], stats[:, 2:3],
                                                  1.0 / C, None, op0=ALU.mult))
                    p.wait(v, "s_ev")
                    p.inc("s_ev", v.tensor_scalar(p3s[:, 2048:3072], x2, stats[:, 2:3],
                                                  None, op0=ALU.subtract))

                @blk.vector
                def _(v):
                    p.wait(v, "s_ev")
                    p.inc("s_ev", v.tensor_tensor(p3s[:, 3072:4096], p3s[:, 2048:3072],
                                                  p3s[:, 2048:3072], op=ALU.mult))
                    fence(v)
                    p.inc("s_ev", v.reduce_sum(stats[:, 3:4], p3s[:, 3072:4096], axis=AX.X))
                    p.wait(v, "s_ev")
                    p.inc("s_ev", v.tensor_scalar(stats[:, 3:4], stats[:, 3:4],
                                                  1.0 / C, 1e-5, op0=ALU.mult, op1=ALU.add))

                @blk.vector
                def _(v):
                    p.wait(v, "s_ev")
                    p.inc("s_ev", v.reciprocal(stats[:, 3:4], stats[:, 3:4]))

                @blk.scalar
                def _(sc):
                    p.wait(sc, "s_ev")
                    p.inc("s_sc", sc.activation(stats[:, 3:4], stats[:, 3:4], AF.Sqrt))

                @blk.vector
                def _(v):
                    p.wait(v, "s_sc")
                    z = p3s[:, 2048:3072]
                    p.inc("s_ev", v.tensor_scalar(z, z, stats[:, 3:4], None, op0=ALU.mult))
                    p.wait(v, "s_ev")
                    p.inc("s_ev", v.tensor_tensor(z, z, lngb[:, 0:C], op=ALU.mult))
                    p.wait(v, "s_ev")
                    p.inc("s_ev", v.tensor_tensor(z, z, lngb[:, C:2 * C], op=ALU.add))
                @blk.vector
                def _(v, cc=cc):
                    p.wait(v, "s_ev")
                    p.inc("s_ev", v.tensor_scalar(p3s[:, 3072:4096], p3s[:, 2048:3072],
                                                  -1.0, None, op0=ALU.mult))
                    fence(v)
                    p.inc("s_ev", v.tensor_tensor(p3s[:, 3072:4096], p3s[:, 3072:4096],
                                                  p3s[:, 2048:3072], op=ALU.max))
                    if cc < 4:
                        fence(v)
                        p.inc("s_ev", v.tensor_copy(habsA[:, cc * C:(cc + 1) * C],
                                                    p3s[:, 3072:4096]))
                    p.inc("s_sc", v.engine_nop())

                @blk.sync
                def _(sy, cc=cc):
                    p.wait(sy, "s_ev")
                    p.inc("s_sp", sy.dma_start(out=x2_sp[cc * 128:(cc + 1) * 128, :],
                                               in_=p3s[:, 1024:2048]), 16)
                    p.inc("s_sp", sy.dma_start(out=h1_sp[cc * 128:(cc + 1) * 128, :],
                                               in_=p3s[:, 2048:3072]), 16)
                h1sp_sn[cc] = p.c["s_sp"]
        if "d_x2c" in dbg_d:
            if True:
                blk = BLK
                @blk.gpsimd
                def _(g):
                    p.wait(g, "s_ev")
                    p.wait(g, "s_sp")
                    p.inc("s_gp", g.dma_start(out=dbg_d["d_x2c"][:, :],
                                              in_=p3s[:, 1024:2048]), 16)
                    p.inc("s_gp", g.dma_start(out=dbg_d["d_h1c"][:, :],
                                              in_=p3s[:, 2048:3072]), 16)
                    p.inc("s_gp", g.dma_start(out=dbg_d["d_n0"][:, :], in_=n_sp[8]), 16)
        if True:
            blk = BLK
            @blk.vector
            def _(v):
                p.wait(v, "s_sp")
                p.wait(v, "s_gp")
                p.inc("s_ev", v.engine_nop())
        LS3.close()
        LSwo.close()




    if STAGE < 4:
        LSwo.close()
    if STAGE < 5:
        LS3h.close()
        LSattc.close()
    if STAGE >= 5:
        LS4s = ExitStack()
        sb4s = lambda name, shape, dt: LS4s.enter_context(nc.sbuf_tensor(name, shape, dt))
        witmp = sb4s("witmp", [128, HID], I32)
        wacc = sb4s("wacc", [128, 40], F32)
        qtr = sb4s("qtr", [128, 4 * C], F32)
        cand = sb4s("cand", [128, 128], F32)
        negc = sb4s("negc", [128, 128], F32)
        ttmp = sb4s("ttmp", [128, 128], F32)
        LS4 = ExitStack()
        sb4 = lambda name, shape, dt: LS4.enter_context(nc.sbuf_tensor(name, shape, dt))
        t1_sb = sb4("t1_sb", [128, 8 * HID], BF16)
        gsA = sb4("gsA", [128, 16384], BF16)
        gt32 = sb4("gt32", [128, 2 * 512], F32)
        pstg = sb4("pstg", [128, 64], F32)
        n1T = sb4("n1T", [128, 4096], BF16)
        n1stage = sb4("n1stage", [128, C], BF16)

        hbuf = sb4("hbuf", [128, 2 * C], F32)
        pool1 = sb4("pool1", [128, 2048], F32)
        QS = 20

        def v_ss(v, dst, a, s1, s2, o0, o1=None):
            if o1 is None:
                p.inc("s_ev", v.tensor_scalar(dst, a, s1, None, op0=o0))
            else:
                p.inc("s_ev", v.tensor_scalar(dst, a, s1, s2, op0=o0, op1=o1))
            fence(v)

        def v_tt(v, dst, a, b, op):
            p.inc("s_ev", v.tensor_tensor(dst, a, b, op=op))
            fence(v)

        def bcast_row_to_cols(row_ap, col_ap, n):
            if True:
                blk = BLK
                @blk.tensor
                def _(t):
                    p.wait(t, "s_ev")
                    p.wait(t, "s_sc")
                    p.inc("s_mm", t.matmul(ps[5][:, 0:n], ones32[0:1, :], row_ap,
                                           start=True, stop=True))

                @blk.vector
                def _(v):
                    p.wait(v, "s_mm")
                    p.inc("s_ev", v.tensor_copy(col_ap, ps[5][:, 0:n]))

        def cross_sum_cols(col_ap, out_row, n):
            if True:
                blk = BLK
                @blk.tensor
                def _(t):
                    p.wait(t, "s_ev")
                    p.inc("s_mm", t.matmul(ps[5][0:1, 0:n], ones32[:, 0:1], col_ap,
                                           start=True, stop=True))

                @blk.vector
                def _(v):
                    p.wait(v, "s_mm")
                    p.inc("s_ev", v.tensor_copy(out_row, ps[5][0:1, 0:n]))

        def cross_max_col(col_ap, out_11):
            if True:
                blk = BLK
                @blk.vector
                def _(v):
                    p.wait(v, "s_ev")
                    p.inc("s_ev", v.memset(ttmp[:, :], 0.0))
                    fence(v)
                    p.inc("s_ev", v.tensor_copy(ttmp[:, 0:1], col_ap))

                @blk.tensor
                def _(t):
                    p.wait(t, "s_ev")
                    p.inc("s_mm", t.transpose(ps[5][:, 0:128], ttmp[:, :], ident32[:, :]))

                @blk.vector
                def _(v):
                    p.wait(v, "s_mm")
                    p.inc("s_ev", v.reduce_max(out_11, ps[5][0:1, 0:128], axis=AX.X))

        def count_gt(segs, thr_col, out_11):
            pieces = []
            for ap, fd in segs:
                off = 0
                while off < fd:
                    w = min(4096, fd - off)
                    pieces.append((ap[:, off:off + w] if (off or w < fd) else ap, w))
                    off += w
            if True:
                blk = BLK
                @blk.vector
                def _(v):
                    p.wait(v, "s_ev")
                    for i, (ap, w) in enumerate(pieces):
                        if i:
                            fence(v)
                        p.inc("s_ev", v.tensor_scalar(qtr[:, 0:w], ap, thr_col, 0.0,
                                                      op0=ALU.is_gt, op1=ALU.add,
                                                      accum_out=wacc[:, 32 + i:33 + i]))

                @blk.tensor
                def _(t):
                    p.wait(t, "s_ev")
                    p.inc("s_mm", t.matmul(ps[5][0:1, 0:len(pieces)], ones32[:, 0:1],
                                           wacc[:, 32:32 + len(pieces)],
                                           start=True, stop=True))

                @blk.vector
                def _(v):
                    p.wait(v, "s_mm")
                    p.inc("s_ev", v.reduce_sum(out_11, ps[5][0:1, 0:len(pieces)],
                                               axis=AX.X))

        def emit_bisect(segs, rounds, lo, hi, target, out_ap):
            for r in range(rounds):
                if True:
                    blk = BLK
                    @blk.vector
                    def _(v):
                        p.wait(v, "s_ev")
                        v_tt(v, qrow[0:1, QS:QS + 1], lo, hi, ALU.add)
                        v_ss(v, qrow[0:1, QS:QS + 1], qrow[0:1, QS:QS + 1], 0.5, None,
                             ALU.mult)
                bcast_row_to_cols(qrow[0:1, QS:QS + 1], qcol[:, 0:1], 1)
                count_gt(segs, qcol[:, 0:1], qrow[0:1, QS + 1:QS + 2])
                if True:
                    blk = BLK
                    @blk.vector
                    def _(v):
                        p.wait(v, "s_ev")
                        v_tt(v, qrow[0:1, QS + 2:QS + 3], qrow[0:1, QS + 1:QS + 2],
                             target, ALU.is_gt)
                        v_tt(v, qrow[0:1, QS + 3:QS + 4], qrow[0:1, QS:QS + 1], lo,
                             ALU.subtract)
                        v_tt(v, qrow[0:1, QS + 3:QS + 4], qrow[0:1, QS + 3:QS + 4],
                             qrow[0:1, QS + 2:QS + 3], ALU.mult)
                        v_tt(v, lo, lo, qrow[0:1, QS + 3:QS + 4], ALU.add)
                        v_tt(v, qrow[0:1, QS + 4:QS + 5], qrow[0:1, QS:QS + 1], hi,
                             ALU.subtract)
                        v_ss(v, qrow[0:1, QS + 5:QS + 6], qrow[0:1, QS + 2:QS + 3],
                             -1.0, 1.0, ALU.mult, ALU.add)
                        v_tt(v, qrow[0:1, QS + 4:QS + 5], qrow[0:1, QS + 4:QS + 5],
                             qrow[0:1, QS + 5:QS + 6], ALU.mult)
                        v_tt(v, hi, hi, qrow[0:1, QS + 4:QS + 5], ALU.add)
            if out_ap is not None:
                if True:
                    blk = BLK
                    @blk.vector
                    def _(v):
                        p.wait(v, "s_ev")
                        v_tt(v, out_ap, lo, hi, ALU.add)
                        v_ss(v, out_ap, out_ap, 0.5, None, ALU.mult)

        def allreduce_row(pair, width, scale=None, cols=8):
            cin, cout = pair
            if True:
                blk = BLK
                @blk.sync
                def _(sy):
                    p.wait(sy, "s_ev")
                    p.inc("s_sp", sy.dma_start(out=cin[0:1, 0:cols],
                                               in_=qrow[0:1, 40:40 + cols]), 16)

                @blk.gpsimd
                def _(g):
                    p.wait(g, "s_sp")
                    p.inc("s_cc", g.collective_compute(
                        "AllReduce", ALU.add, replica_groups=RG_ALL,
                        ins=[cin[:, :]], outs=[cout[:, :]]), 1)

                @blk.sync
                def _(sy):
                    p.wait(sy, "s_cc")
                    p.inc("s_cr", sy.dma_start(out=qrow[0:1, 40:40 + width],
                                               in_=cout[0:1, 0:width]), 16)

                @blk.vector
                def _(v):
                    p.wait(v, "s_cr")
                    if scale is not None:
                        v_ss(v, qrow[0:1, 40:40 + width], qrow[0:1, 40:40 + width],
                             scale, None, ALU.mult)
                    else:
                        p.inc("s_ev", v.engine_nop())

        def emit_quantile_tail(qidx, pool_ap, pool_fd, gather_pair, ka_slot,
                               frac_slot, s_out_col):
            if True:
                blk = BLK
                @blk.vector
                def _(v):
                    p.wait(v, "s_ev")
                    p.inc("s_ev", v.tensor_copy(qrow[0:1, 40:46], qrow[0:1, 72:78]))
            allreduce_row(cc_ct[qidx], 6)
            if True:
                blk = BLK
                @blk.vector
                def _(v):
                    p.wait(v, "s_ev")
                    p.inc("s_ev", v.tensor_copy(qrow[0:1, 80:86], qrow[0:1, 40:46]))
                    fence(v)
                    p.inc("s_ev", v.tensor_copy(qrow[0:1, 35:36], qrow[0:1, 85:86]))
                    fence(v)
                    v_tt(v, qrow[0:1, 36:37], qrow[0:1, ka_slot:ka_slot + 1],
                         qrow[0:1, 35:36], ALU.subtract)
                    v_ss(v, qrow[0:1, 20:21], qrow[0:1, ka_slot:ka_slot + 1],
                         1.0, None, ALU.add)
                    p.inc("s_ev", v.tensor_scalar(qrow[0:1, 96:102], qrow[0:1, 80:86],
                                                  qrow[0:1, 20:21], None, op0=ALU.is_ge))
                    fence(v)
                    v_tt(v, qrow[0:1, 104:110], qrow[0:1, 64:70], qrow[0:1, 96:102],
                         ALU.mult)
                    p.inc("s_ev", v.tensor_reduce(qrow[0:1, 30:31], qrow[0:1, 104:110],
                                                  axis=AX.X, op=ALU.max))
                    fence(v)
                    v_ss(v, qrow[0:1, 20:21], qrow[0:1, ka_slot:ka_slot + 1],
                         -1.0, None, ALU.add)
                    p.inc("s_ev", v.tensor_scalar(qrow[0:1, 96:102], qrow[0:1, 80:86],
                                                  qrow[0:1, 20:21], None, op0=ALU.is_le))
                    fence(v)
                    v_tt(v, qrow[0:1, 104:110], qrow[0:1, 64:70], qrow[0:1, 96:102],
                         ALU.mult)
                    v_ss(v, qrow[0:1, 96:102], qrow[0:1, 96:102], -1e9, 1e9,
                         ALU.mult, ALU.add)
                    v_tt(v, qrow[0:1, 104:110], qrow[0:1, 104:110], qrow[0:1, 96:102],
                         ALU.add)
                    p.inc("s_ev", v.tensor_reduce(qrow[0:1, 31:32], qrow[0:1, 104:110],
                                                  axis=AX.X, op=ALU.min))
                    fence(v)
            for er in range(3):
                if True:
                    blk = BLK
                    @blk.vector
                    def _(v):
                        p.wait(v, "s_ev")
                        v_tt(v, qrow[0:1, 20:21], qrow[0:1, 31:32], qrow[0:1, 30:31],
                             ALU.subtract)
                        for j in range(8):
                            v_ss(v, qrow[0:1, 88 + j:89 + j], qrow[0:1, 20:21],
                                 (j + 1) / 9.0, None, ALU.mult)
                            v_tt(v, qrow[0:1, 88 + j:89 + j], qrow[0:1, 88 + j:89 + j],
                                 qrow[0:1, 30:31], ALU.add)
                for j in range(8):
                    bcast_row_to_cols(qrow[0:1, 88 + j:89 + j], qcol[:, 1:2], 1)
                    count_gt([(pool_ap, pool_fd)], qcol[:, 1:2], qrow[0:1, 40 + j:41 + j])
                allreduce_row(cc_e[qidx][er], 8, cols=16)
                if True:
                    blk = BLK
                    @blk.vector
                    def _(v):
                        p.wait(v, "s_ev")
                        p.inc("s_ev", v.tensor_copy(qrow[0:1, 96:104], qrow[0:1, 40:48]))
                        fence(v)
                        v_ss(v, qrow[0:1, 20:21], qrow[0:1, 36:37], 1.0, None, ALU.add)
                        p.inc("s_ev", v.tensor_scalar(qrow[0:1, 104:112],
                                                      qrow[0:1, 96:104],
                                                      qrow[0:1, 20:21], None,
                                                      op0=ALU.is_ge))
                        fence(v)
                        v_tt(v, qrow[0:1, 112:120], qrow[0:1, 88:96], qrow[0:1, 104:112],
                             ALU.mult)
                        p.inc("s_ev", v.tensor_reduce(qrow[0:1, 21:22],
                                                      qrow[0:1, 112:120], axis=AX.X,
                                                      op=ALU.max))
                        fence(v)
                        v_tt(v, qrow[0:1, 30:31], qrow[0:1, 30:31], qrow[0:1, 21:22],
                             ALU.max)
                        v_ss(v, qrow[0:1, 20:21], qrow[0:1, 36:37], -65.0, None,
                             ALU.add)
                        p.inc("s_ev", v.tensor_scalar(qrow[0:1, 104:112],
                                                      qrow[0:1, 96:104],
                                                      qrow[0:1, 20:21], None,
                                                      op0=ALU.is_le))
                        fence(v)
                        v_tt(v, qrow[0:1, 112:120], qrow[0:1, 88:96], qrow[0:1, 104:112],
                             ALU.mult)
                        v_ss(v, qrow[0:1, 104:112], qrow[0:1, 104:112], -1e9, 1e9,
                             ALU.mult, ALU.add)
                        v_tt(v, qrow[0:1, 112:120], qrow[0:1, 112:120],
                             qrow[0:1, 104:112], ALU.add)
                        p.inc("s_ev", v.tensor_reduce(qrow[0:1, 21:22],
                                                      qrow[0:1, 112:120], axis=AX.X,
                                                      op=ALU.min))
                        fence(v)
                        v_tt(v, qrow[0:1, 31:32], qrow[0:1, 31:32], qrow[0:1, 21:22],
                             ALU.min)
            bcast_row_to_cols(qrow[0:1, 31:32], qcol[:, 2:3], 1)
            count_gt([(pool_ap, pool_fd)], qcol[:, 2:3], qrow[0:1, 40:41])
            allreduce_row(cc_e[qidx][3], 8, cols=16)
            if True:
                blk = BLK
                @blk.vector
                def _(v):
                    p.wait(v, "s_ev")
                    p.inc("s_ev", v.tensor_copy(qrow[0:1, 37:38], qrow[0:1, 40:41]))
                    fence(v)
                    v_tt(v, qrow[0:1, 38:39], qrow[0:1, 36:37], qrow[0:1, 37:38],
                         ALU.subtract)
            npc = max(1, pool_fd // 4096)
            bpp = 16 // npc
            if True:
                blk = BLK
                @blk.vector
                def _(v):
                    p.wait(v, "s_ev")
                    for h in range(npc):
                        w = pool_fd // npc
                        p.inc("s_ev", v.scalar_tensor_tensor(
                            qtr[:, 0:w], pool_ap[:, h * w:(h + 1) * w],
                            qcol[:, 2:3], pool_ap[:, h * w:(h + 1) * w],
                            op0=ALU.is_le, op1=ALU.mult))
                        fence(v)
                        p.inc("s_ev", v.tensor_reduce(
                            wacc[:, h * bpp:(h + 1) * bpp],
                            qtr[:, 0:w].rearrange("p (b f) -> p b f", b=bpp),
                            axis=AX.X, op=ALU.max))
                        fence(v)

                @blk.sync
                def _(sy):
                    p.wait(sy, "s_ev")
                    p.inc("s_sp", sy.dma_start(out=gather_pair[0][:, :],
                                               in_=wacc[:, 0:16]), 16)

                @blk.gpsimd
                def _(g):
                    p.wait(g, "s_sp")
                    p.inc("s_cc", g.collective_compute(
                        "AllGather", ALU.bypass, replica_groups=RG_ALL,
                        ins=[gather_pair[0][:, :]], outs=[gather_pair[1][:, :]]), 1)

                @blk.sync
                def _(sy):
                    p.wait(sy, "s_cc")
                    p.inc("s_cr", sy.dma_start(
                        out=cand.rearrange("p (r c) -> p r c", r=8),
                        in_=gather_pair[1].rearrange("(r p) c -> p r c", p=128)), 16)

                @blk.vector
                def _(v):
                    p.wait(v, "s_cr")
                    p.inc("s_ev", v.tensor_scalar(negc[:, :], cand[:, :], -1.0, None,
                                                  op0=ALU.mult))
            emit_bisect([(cand[:, :], 128)], 20, qrow[0:1, 30:31], qrow[0:1, 31:32],
                        qrow[0:1, 38:39], None)
            bcast_row_to_cols(qrow[0:1, 31:32], qcol[:, 3:4], 1)
            if True:
                blk = BLK
                @blk.vector
                def _(v):
                    p.wait(v, "s_ev")
                    p.inc("s_ev", v.scalar_tensor_tensor(
                        qtr[:, 0:128], cand[:, :], qcol[:, 3:4], cand[:, :],
                        op0=ALU.is_le, op1=ALU.mult))
                    fence(v)
                    p.inc("s_ev", v.tensor_reduce(wacc[:, 20:21], qtr[:, 0:128],
                                                  axis=AX.X, op=ALU.max))
            cross_max_col(wacc[:, 20:21], qrow[0:1, 26:27])
            if True:
                blk = BLK
                @blk.vector
                def _(v):
                    p.wait(v, "s_ev")
                    p.inc("s_ev", v.scalar_tensor_tensor(
                        qtr[:, 0:128], cand[:, :], qcol[:, 3:4], negc[:, :],
                        op0=ALU.is_gt, op1=ALU.mult))
                    fence(v)
                    p.inc("s_ev", v.tensor_reduce(wacc[:, 21:22], qtr[:, 0:128],
                                                  axis=AX.X, op=ALU.max))
            cross_max_col(wacc[:, 21:22], qrow[0:1, 27:28])
            if True:
                blk = BLK
                @blk.vector
                def _(v):
                    p.wait(v, "s_ev")
                    v_ss(v, qrow[0:1, 27:28], qrow[0:1, 27:28], -1.0, None, ALU.mult)
                    v_tt(v, qrow[0:1, 27:28], qrow[0:1, 27:28], qrow[0:1, 26:27],
                         ALU.max)
                    v_tt(v, qrow[0:1, 28:29], qrow[0:1, 27:28], qrow[0:1, 26:27],
                         ALU.subtract)
                    p.inc("s_ev", v.tensor_scalar(
                        qrow[0:1, 28:29], qrow[0:1, 28:29],
                        qrow[0:1, frac_slot:frac_slot + 1], None, op0=ALU.mult))
                    fence(v)
                    v_tt(v, qrow[0:1, 33:34], qrow[0:1, 26:27], qrow[0:1, 28:29],
                         ALU.add)
                    v_ss(v, qrow[0:1, 33:34], qrow[0:1, 33:34], 1e-5, None, ALU.max)
                    p.inc("s_ev", v.reciprocal(qrow[0:1, 34:35], qrow[0:1, 33:34]))
                    fence(v)
                    v_ss(v, qrow[0:1, 34:35], qrow[0:1, 34:35], 127.0, None, ALU.mult)
            bcast_row_to_cols(qrow[0:1, 34:35], s_out_col, 1)

        for kk in range(8):
            if True:
                blk = BLK
                @blk.sync
                def _(sy, kk=kk):
                    p.wait(sy, "s_ev")
                    p.inc("s_ld", sy.dma_start(
                        out=hbuf[:, 0:C] if False else qtr[:, 0:4096],
                        in_=w1t_d[kk * 128:(kk + 1) * 128, :]), 16)

                @blk.vector
                def _(v, kk=kk):
                    p.wait(v, "s_ld")
                    p.inc("s_ev", v.reduce_sum(wacc[:, kk:kk + 1], qtr[:, 0:4096],
                                               axis=AX.X, apply_absolute_value=True))
        cross_sum_cols(wacc[:, 0:8], qrow[0:1, 40:48], 8)
        if True:
            blk = BLK
            @blk.vector
            def _(v):
                p.wait(v, "s_ev")
                p.inc("s_ev", v.reduce_sum(qrow[0:1, 39:40], qrow[0:1, 40:48], axis=AX.X))
                fence(v)
                v_ss(v, qrow[0:1, 10:11], qrow[0:1, 39:40], 1.0 / (C * HID), 1e-5,
                     ALU.mult, ALU.add)
                p.inc("s_ev", v.reciprocal(qrow[0:1, 11:12], qrow[0:1, 10:11]))
        bcast_row_to_cols(qrow[0:1, 11:12], qcol[:, 4:5], 1)
        for kk in range(8):
            if True:
                blk = BLK
                @blk.sync
                def _(sy, kk=kk):
                    p.wait(sy, "s_ev")
                    p.inc("s_ld", sy.dma_start(out=qtr[:, 0:4096],
                                               in_=w1t_d[kk * 128:(kk + 1) * 128, :]), 16)

                @blk.vector
                def _(v, kk=kk):
                    p.wait(v, "s_ld")
                    p.inc("s_ev", v.tensor_scalar(qtr[:, 0:4096], qtr[:, 0:4096],
                                                  qcol[:, 4:5], 1.0, op0=ALU.mult,
                                                  op1=ALU.min))
                    fence(v)
                    p.inc("s_ev", v.tensor_scalar(qtr[:, 0:4096], qtr[:, 0:4096],
                                                  -1.0, None, op0=ALU.max))
                    fence(v)
                    p.inc("s_ev", v.tensor_scalar(witmp[:, :], qtr[:, 0:4096],
                                                  0.0, None, op0=ALU.add))
                    fence(v)
                    p.inc("s_ev", v.tensor_scalar(t1_sb[:, kk * HID:(kk + 1) * HID],
                                                  witmp[:, :], 0.0, None, op0=ALU.add))
        T1_DONE = p.c["s_ev"]

        for half in range(8):
            if True:
                blk = BLK
                @blk.sync
                def _(sy, half=half):
                    p.wait(sy, "s_ev")
                    p.inc("s_ld", sy.dma_start(
                        out=qtr[:, 0:4096].rearrange("p (a c) -> p a c", a=4),
                        in_=w2t_d[half * 512:(half + 1) * 512, :].rearrange(
                            "(a p) c -> p a c", p=128)), 16)

                @blk.vector
                def _(v, half=half):
                    p.wait(v, "s_ld")
                    p.inc("s_ev", v.reduce_sum(wacc[:, 16 + half:17 + half],
                                               qtr[:, 0:4096], axis=AX.X,
                                               apply_absolute_value=True))
        cross_sum_cols(wacc[:, 16:24], qrow[0:1, 40:48], 8)
        if True:
            blk = BLK
            @blk.vector
            def _(v):
                p.wait(v, "s_ev")
                p.inc("s_ev", v.reduce_sum(qrow[0:1, 44:45], qrow[0:1, 40:48], axis=AX.X))
                fence(v)
                v_ss(v, qrow[0:1, 12:13], qrow[0:1, 44:45], 1.0 / (C * HID), 1e-5,
                     ALU.mult, ALU.add)
                p.inc("s_ev", v.reciprocal(qrow[0:1, 13:14], qrow[0:1, 12:13]))
        bcast_row_to_cols(qrow[0:1, 13:14], qcol[:, 14:15], 1)

        if True:
            blk = BLK
            @blk.vector
            def _(v):
                p.wait(v, "s_ev")
                p.inc("s_ev", v.reduce_max(wacc[:, 24:25], habsA[:, :], axis=AX.X))
        cross_max_col(wacc[:, 24:25], qrow[0:1, 31:32])
        if True:
            blk = BLK
            @blk.vector
            def _(v):
                p.wait(v, "s_ev")
                p.inc("s_ev", v.memset(qrow[0:1, 30:31], 0.0))
                fence(v)
                p.inc("s_ev", v.memset(qrow[0:1, QS + 9:QS + 10], 0.005 * 4 * 128 * C))
        emit_bisect([(habsA[:, :], 4096)], 14, qrow[0:1, 30:31], qrow[0:1, 31:32],
                    qrow[0:1, QS + 9:QS + 10], qrow[0:1, 32:33])
        if True:
            blk = BLK
            @blk.vector
            def _(v):
                p.wait(v, "s_ev")
                p.inc("s_ev", v.tensor_copy(qrow[0:1, 40:41], qrow[0:1, 32:33]))
        allreduce_row(cc_th[0], 1, scale=1.0 / 8.0)
        if True:
            blk = BLK
            @blk.vector
            def _(v):
                p.wait(v, "s_ev")
                p.inc("s_ev", v.tensor_copy(qrow[0:1, 32:33], qrow[0:1, 40:41]))

        DLT = 0.012
        if True:
            blk = BLK
            @blk.vector
            def _(v):
                p.wait(v, "s_ev")
                for j in range(6):
                    f = -DLT + (2 * DLT) * j / 5.0
                    v_ss(v, qrow[0:1, 40 + j:41 + j], qrow[0:1, 32:33], 1.0 + f,
                         None, ALU.mult)
                p.inc("s_ev", v.memset(wacc[:, 8:14], 0.0))
        bcast_row_to_cols(qrow[0:1, 40:46], qcol[:, 8:14], 6)
        for cc in range(NCH):
            cur = cc % 2
            if True:
                blk = BLK
                @blk.sync
                def _(sy, cc=cc, cur=cur):
                    p.wait(sy, "s_ev")
                    p.wait(sy, "s_sp", h1sp_sn[cc])
                    p.inc("s_ld", sy.dma_start(out=hbuf[:, cur * C:(cur + 1) * C],
                                               in_=h1_sp[cc * 128:(cc + 1) * 128, :]), 16)

                @blk.vector
                def _(v, cc=cc, cur=cur):
                    p.wait(v, "s_ld")
                    hb0 = hbuf[:, cur * C:(cur + 1) * C]
                    hb = qtr[:, 0:C]
                    p.inc("s_ev", v.tensor_scalar(qtr[:, 1024:2048], hb0, -1.0, None,
                                                  op0=ALU.mult))
                    fence(v)
                    p.inc("s_ev", v.tensor_tensor(hb, hb0, qtr[:, 1024:2048],
                                                  op=ALU.max))
                    fence(v)
                    for j in range(6):
                        sco = 1024 + (j % 3) * 1024
                        if j == 3:
                            fence(v)
                        p.inc("s_ev", v.tensor_scalar(qtr[:, sco:sco + 1024], hb,
                                                      qcol[:, 8 + j:9 + j], 0.0,
                                                      op0=ALU.is_gt, op1=ALU.add,
                                                      accum_out=wacc[:, 30 + j:31 + j]))
                    fence(v)
                    p.inc("s_ev", v.tensor_tensor(wacc[:, 8:14], wacc[:, 8:14],
                                                  wacc[:, 30:36], op=ALU.add))
                    fence(v)
                    p.inc("s_ev", v.scalar_tensor_tensor(qtr[:, 0:C], hb,
                                                         qcol[:, 13:14], hb,
                                                         op0=ALU.is_le, op1=ALU.mult))
                    fence(v)
                    p.inc("s_ev", v.tensor_tensor(qtr[:, 0:512], qtr[:, 0:512],
                                                  qtr[:, 512:1024], op=ALU.max))
                    fence(v)
                    p.inc("s_ev", v.tensor_tensor(qtr[:, 0:256], qtr[:, 0:256],
                                                  qtr[:, 256:512], op=ALU.max))
                    fence(v)
                    p.inc("s_ev", v.tensor_tensor(qtr[:, 0:128], qtr[:, 0:128],
                                                  qtr[:, 128:256], op=ALU.max))
                    fence(v)
                    p.inc("s_ev", v.tensor_tensor(pool1[:, cc * 64:(cc + 1) * 64],
                                                  qtr[:, 0:64], qtr[:, 64:128],
                                                  op=ALU.max))
                    fence(v)
        cross_sum_cols(wacc[:, 8:14], qrow[0:1, 72:78], 6)
        if True:
            blk = BLK
            @blk.vector
            def _(v):
                p.wait(v, "s_ev")
                v_ss(v, qrow[0:1, 14:15], qrow[0:1, 0:1], -1.0, None, ALU.add)
                p.inc("s_ev", v.tensor_copy(qrow[0:1, 64:70], qrow[0:1, 40:46]))
        emit_quantile_tail(0, pool1[:, :], 2048, cc_gc[0], 14, 1, qcol[:, 5:6])
        if True:
            blk = BLK
            @blk.vector
            def _(v):
                p.wait(v, "s_ev")
                v_tt(v, qrow[0:1, 16:17], qrow[0:1, 10:11], qrow[0:1, 33:34], ALU.mult)
                v_ss(v, qrow[0:1, 16:17], qrow[0:1, 16:17], 1.0 / 127.0, None, ALU.mult)
        bcast_row_to_cols(qrow[0:1, 16:17], qcol[:, 6:7], 1)
        if "d_qrow1" in dbg_d:
            if True:
                blk = BLK
                @blk.gpsimd
                def _(g):
                    p.wait(g, "s_ev")
                    p.inc("s_gp", g.dma_start(out=dbg_d["d_qrow1"][:, :],
                                              in_=qrow[0:1, :]), 16)
                    p.inc("s_gp", g.dma_start(out=dbg_d["d_qcol"][:, :],
                                              in_=qcol[:, :]), 16)
                    p.inc("s_gp", g.dma_start(out=dbg_d["d_pool1"][:, :],
                                              in_=pool1[:, :]), 16)
                    p.inc("s_gp", g.dma_start(out=dbg_d["d_habsA"][:, :],
                                              in_=habsA[:, :]), 16)
        Q1_DONE = p.c["s_ev"]

        def q2_tile_counts(v, gb, ti):
            for j in range(6):
                sco = j * 512
                p.inc("s_ev", v.tensor_scalar(qtr[:, sco:sco + 512], gb,
                                              qcol[:, 8 + j:9 + j], 0.0,
                                              op0=ALU.is_gt, op1=ALU.add,
                                              accum_out=wacc[:, 30 + j:31 + j]))
            fence(v)
            p.inc("s_ev", v.tensor_tensor(wacc[:, 8:14], wacc[:, 8:14],
                                          wacc[:, 30:36], op=ALU.add))
            fence(v)
            p.inc("s_ev", v.scalar_tensor_tensor(qtr[:, 0:512], gb,
                                                 qcol[:, 13:14], gb,
                                                 op0=ALU.is_le, op1=ALU.mult))
            fence(v)
            p.inc("s_ev", v.tensor_tensor(qtr[:, 0:256], qtr[:, 0:256],
                                          qtr[:, 256:512], op=ALU.max))
            fence(v)
            p.inc("s_ev", v.tensor_tensor(qtr[:, 0:128], qtr[:, 0:128],
                                          qtr[:, 128:256], op=ALU.max))
            fence(v)
            p.inc("s_ev", v.tensor_tensor(qtr[:, 0:64], qtr[:, 0:64],
                                          qtr[:, 64:128], op=ALU.max))
            fence(v)
            slot = ti % 2
            if ti >= 2:
                p.wait(v, "s_ps%d" % slot, pspill_sn[ti - 2])
            p.inc("s_ev", v.tensor_tensor(pstg[:, slot * 32:(slot + 1) * 32],
                                          qtr[:, 0:32], qtr[:, 32:64],
                                          op=ALU.max))
            ptree_sn[ti] = p.c["s_ev"]

        def q2_tile_spill(ti):
            blk = BLK
            slot = ti % 2

            @blk.sync
            def _(sy, ti=ti, slot=slot):
                p.wait(sy, "s_ev", ptree_sn[ti])
                p.inc("s_ps%d" % slot, sy.dma_start(
                    out=pool2_d[:, ti * 32:(ti + 1) * 32],
                    in_=pstg[:, slot * 32:(slot + 1) * 32]), 16)
            pspill_sn[ti] = p.c["s_ps%d" % slot]

        scale_snap = {}
        pspill_sn = {}
        ptree_sn = {}
        gel_snap = {}
        gspill_sn = {}
        cnt_snap = {}
        for s in range(NSLAB):
            for ccq in range(4):
                cc = s * 4 + ccq
                cur = cc % 2
                if True:
                    blk = BLK
                    @blk.sync
                    def _(sy, cc=cc, cur=cur):
                        p.wait(sy, "s_ev")
                        p.wait(sy, "s_sp", h1sp_sn[NCH - 1])
                        p.inc("s_ld", sy.dma_start(
                            out=hbuf[:, cur * C:(cur + 1) * C],
                            in_=h1_sp[cc * 128:(cc + 1) * 128, :]), 16)

                    @blk.vector
                    def _(v, ccq=ccq, cur=cur):
                        p.wait(v, "s_ld")
                        p.wait(v, "s_mm")
                        hb = hbuf[:, cur * C:(cur + 1) * C]
                        p.inc("s_ev", v.tensor_scalar(qtr[:, 0:C], hb,
                                                      qcol[:, 5:6], 127.0,
                                                      op0=ALU.mult, op1=ALU.min))
                        fence(v)
                        p.inc("s_ev", v.tensor_scalar(qtr[:, 0:C], qtr[:, 0:C],
                                                      -128.0, None, op0=ALU.max))
                        fence(v)
                        p.inc("s_ev", v.tensor_scalar(witmp[:, 0:C], qtr[:, 0:C],
                                                      0.0, None, op0=ALU.add))
                        fence(v)
                        p.inc("s_ev", v.tensor_scalar(n1stage[:, :], witmp[:, 0:C],
                                                      0.0, None, op0=ALU.add))
                for half in range(2):
                    if True:
                        blk = BLK
                        @blk.tensor
                        def _(t, half=half, ccq=ccq):
                            p.wait(t, "s_ev")
                            p.wait(t, "s_sc")
                            last = None
                            for j in range(4):
                                kk = half * 4 + j
                                last = t.transpose(
                                    pt[:, j * 256:j * 256 + 128],
                                    n1stage[:, kk * 128:(kk + 1) * 128],
                                    ident[:, :])
                            p.inc("s_mm", last)

                        @blk.vector
                        def _(v, half=half, ccq=ccq):
                            p.wait(v, "s_mm")
                            last = None
                            for j in range(4):
                                kk = half * 4 + j
                                last = v.tensor_copy(n1T[:, kk * SLAB + ccq * 128:
                                                   kk * SLAB + (ccq + 1) * 128],
                                               pt[:, j * 256:j * 256 + 128])
                            p.inc("s_sc", last)
            N1T_DONE = p.c["s_sc"]
            for mh in range(32):
                ti = s * 32 + mh
                slot = ti % 2
                bank = ps[slot]
                ssl = gt32[:, slot * 512:(slot + 1) * 512]
                gsl = qtr[:, 3072 + slot * 512:3072 + (slot + 1) * 512]
                if True:
                    blk = BLK
                    @blk.tensor
                    def _(t, mh=mh, bank=bank, s=s,
                          bsn=gel_snap.get(ti - 2)):
                        if bsn is None:
                            p.wait(t, "s_sc", N1T_DONE)
                            p.wait(t, "s_ev")
                        else:
                            p.wait(t, "s_sc", max(N1T_DONE, bsn))
                        last = None
                        for kk in range(8):
                            last = t.matmul(
                                bank[:, :],
                                t1_sb[:, kk * HID + mh * 128:kk * HID + (mh + 1) * 128],
                                n1T[:, kk * SLAB:(kk + 1) * SLAB],
                                start=(kk == 0), stop=(kk == 7))
                        p.inc("s_mm", last)
                    mm_sn = p.c["s_mm"]

                    @blk.scalar
                    def _(sc, mh=mh, s=s, bank=bank, gsl=gsl, mm_sn=mm_sn,
                          csn=cnt_snap.get(ti - 2),
                          dsn=gspill_sn.get(ti - 2), slot=slot):
                        p.wait(sc, "s_mm", mm_sn)
                        if dsn is not None:
                            p.wait(sc, "s_gs%d" % slot, dsn)
                        if csn is not None:
                            p.wait(sc, "s_ev", csn)
                        p.inc("s_sc", sc.activation(gsl, bank[:, :], AF.Gelu,
                                                    scale=qcol[:, 6:7]))
                        if s == 0:
                            p.wait(sc, "s_sc")
                            p.inc("s_sc", sc.activation(gsA[:, mh * 512:(mh + 1) * 512],
                                                        gsl, AF.Abs))
                    gel_snap[ti] = p.c["s_sc"]

                    if s > 0:
                        @blk.vector
                        def _(v, gsl=gsl, ti=ti, gel_sn=gel_snap[ti]):
                            p.wait(v, "s_sc", gel_sn)
                            q2_tile_counts(v, gsl, ti)
                        cnt_snap[ti] = p.c["s_ev"]
                        q2_tile_spill(ti)

                    @blk.sync
                    def _(sy, mh=mh, s=s, gsl=gsl, slot=slot):
                        p.wait(sy, "s_sc")
                        p.inc("s_gs%d" % slot, sy.dma_start(
                            out=g_sp[s, mh * 128:(mh + 1) * 128, :],
                            in_=gsl), 16)
                    gspill_sn[ti] = p.c["s_gs%d" % slot]

            if s == 0:
                if True:
                    blk = BLK
                    @blk.vector
                    def _(v):
                        p.wait(v, "s_ev")
                        p.wait(v, "s_sc")
                        p.inc("s_ev", v.reduce_max(wacc[:, 24:25], gsA[:, :], axis=AX.X))
                cross_max_col(wacc[:, 24:25], qrow[0:1, 31:32])
                if True:
                    blk = BLK
                    @blk.vector
                    def _(v):
                        p.wait(v, "s_ev")
                        p.inc("s_ev", v.memset(qrow[0:1, 30:31], 0.0))
                        fence(v)
                        p.inc("s_ev", v.memset(qrow[0:1, 29:30], 0.005 * 128 * 16384))
                emit_bisect([(gsA[:, k * 2048:(k + 1) * 2048], 2048)
                             for k in range(8)],
                            14, qrow[0:1, 30:31], qrow[0:1, 31:32], qrow[0:1, 29:30],
                            qrow[0:1, 32:33])
                if True:
                    blk = BLK
                    @blk.vector
                    def _(v):
                        p.wait(v, "s_ev")
                        p.inc("s_ev", v.tensor_copy(qrow[0:1, 40:41], qrow[0:1, 32:33]))
                allreduce_row(cc_th[1], 1, scale=1.0 / 8.0)
                if True:
                    blk = BLK
                    @blk.vector
                    def _(v):
                        p.wait(v, "s_ev")
                        p.inc("s_ev", v.tensor_copy(qrow[0:1, 32:33], qrow[0:1, 40:41]))
                        fence(v)
                        for j in range(6):
                            f = -0.012 + 0.024 * j / 5.0
                            v_ss(v, qrow[0:1, 40 + j:41 + j], qrow[0:1, 32:33], 1.0 + f,
                                 None, ALU.mult)
                        p.inc("s_ev", v.memset(wacc[:, 8:14], 0.0))
                bcast_row_to_cols(qrow[0:1, 40:46], qcol[:, 8:14], 6)
                s0cnt_sn = {}
                for mh0 in range(32):
                    sl4 = mh0 % 4
                    if True:
                        blk = BLK
                        @blk.sync
                        def _(sy, mh0=mh0, sl4=sl4):
                            if mh0 < 4:
                                p.wait(sy, "s_gs0")
                                p.wait(sy, "s_gs1")
                            else:
                                p.wait(sy, "s_ev", s0cnt_sn[mh0 - 4])
                            p.inc("s_ld", sy.dma_start(
                                out=pool1[:, sl4 * 512:(sl4 + 1) * 512],
                                in_=g_sp[0, mh0 * 128:(mh0 + 1) * 128, :]), 16)
                        ld_sn = p.c["s_ld"]

                        @blk.vector
                        def _(v, mh0=mh0, sl4=sl4, ld_sn=ld_sn):
                            p.wait(v, "s_ld", ld_sn)
                            if mh0 == 0:
                                fence(v)
                            q2_tile_counts(v, pool1[:, sl4 * 512:(sl4 + 1) * 512], mh0)
                        q2_tile_spill(mh0)
                        s0cnt_sn[mh0] = p.c["s_ev"]
        MM1_DONE = p.c["s_sp"]
        if "d_t1" in dbg_d:
            if True:
                blk = BLK
                @blk.gpsimd
                def _(g):
                    p.wait(g, "s_ev")
                    p.wait(g, "s_sc")
                    p.wait(g, "s_sp")
                    p.inc("s_gp", g.dma_start(out=dbg_d["d_t1"][:, :],
                                              in_=t1_sb[:, 0:4096]), 16)
                    p.inc("s_gp", g.dma_start(out=dbg_d["d_n1T"][:, :],
                                              in_=n1T[:, :]), 16)
                    p.inc("s_gp", g.dma_start(out=dbg_d["d_gsA"][:, :],
                                              in_=gsA[:, 0:4096]), 16)


        cross_sum_cols(wacc[:, 8:14], qrow[0:1, 72:78], 6)
        if True:
            blk = BLK
            @blk.vector
            def _(v):
                p.wait(v, "s_ev")
                v_ss(v, qrow[0:1, 15:16], qrow[0:1, 2:3], -1.0, None, ALU.add)
                p.inc("s_ev", v.tensor_copy(qrow[0:1, 64:70], qrow[0:1, 40:46]))
        LS4.close()
        LS4t = ExitStack()
        pool2 = LS4t.enter_context(nc.sbuf_tensor("pool2", [128, 8192], F32))
        if True:
            blk = BLK
            @blk.sync
            def _(sy):
                p.wait(sy, "s_ps0")
                p.wait(sy, "s_ps1")
                p.wait(sy, "s_gs0")
                p.wait(sy, "s_gs1")
                p.inc("s_ld", sy.dma_start(out=pool2[:, 0:4096],
                                           in_=pool2_d[:, 0:4096]), 16)
                p.inc("s_ld", sy.dma_start(out=pool2[:, 4096:8192],
                                           in_=pool2_d[:, 4096:8192]), 16)

            @blk.vector
            def _(v):
                p.wait(v, "s_ld")
                p.inc("s_ev", v.engine_nop())
        emit_quantile_tail(1, pool2[:, :], 8192, cc_gc[1], 15, 3, qcol[:, 7:8])
        if True:
            blk = BLK
            @blk.vector
            def _(v):
                p.wait(v, "s_ev")
                p.inc("s_ev", v.reciprocal(qrow[0:1, 21:22], qrow[0:1, 34:35]))
                fence(v)
                v_tt(v, qrow[0:1, 22:23], qrow[0:1, 12:13], qrow[0:1, 21:22], ALU.mult)
        bcast_row_to_cols(qrow[0:1, 22:23], qcol[:, 9:10], 1)
        LS4t.close()

        LS4c = ExitStack()
        sb4c = lambda name, shape, dt: LS4c.enter_context(nc.sbuf_tensor(name, shape, dt))
        t2_sb = sb4c("t2_sb", [128, 32 * C], BF16)
        gbuf2 = sb4c("gbuf2", [128, 2 * 512], F32)
        n2s = sb4c("n2s", [128, 32 * SLAB], BF16)
        p4s = sb4c("p4s", [128, 2 * C], F32)
        x2b = sb4c("x2b", [128, C], F32)
        for ti in range(32):
            if True:
                blk = BLK
                @blk.sync
                def _(sy, ti=ti):
                    p.wait(sy, "s_ev")
                    p.inc("s_ld", sy.dma_start(out=qtr[:, 0:C],
                                               in_=w2t_d[ti * 128:(ti + 1) * 128, :]), 16)

                @blk.vector
                def _(v, ti=ti):
                    p.wait(v, "s_ld")
                    p.inc("s_ev", v.tensor_scalar(qtr[:, 0:C], qtr[:, 0:C],
                                                  qcol[:, 14:15], 1.0, op0=ALU.mult,
                                                  op1=ALU.min))
                    fence(v)
                    p.inc("s_ev", v.tensor_scalar(qtr[:, 0:C], qtr[:, 0:C],
                                                  -1.0, None, op0=ALU.max))
                    fence(v)
                    p.inc("s_ev", v.tensor_scalar(witmp[:, 0:C], qtr[:, 0:C],
                                                  0.0, None, op0=ALU.add))
                    fence(v)
                    p.inc("s_ev", v.tensor_scalar(t2_sb[:, ti * C:(ti + 1) * C],
                                                  witmp[:, 0:C], 0.0, None, op0=ALU.add))
        for s in range(NSLAB):
            for mh in range(32):
                cur = mh % 2
                if True:
                    blk = BLK
                    @blk.sync
                    def _(sy, s=s, mh=mh, cur=cur):
                        p.wait(sy, "s_ev")
                        p.wait(sy, "s_gs%d" % ((s * 32 + mh) % 2), gspill_sn[s * 32 + mh])
                        p.inc("s_ld", sy.dma_start(
                            out=gbuf2[:, cur * 512:(cur + 1) * 512],
                            in_=g_sp[s, mh * 128:(mh + 1) * 128, :]), 16)

                    @blk.vector
                    def _(v, mh=mh, cur=cur):
                        p.wait(v, "s_ld")
                        gb = gbuf2[:, cur * 512:(cur + 1) * 512]
                        p.inc("s_ev", v.tensor_scalar(qtr[:, 0:512], gb,
                                                      qcol[:, 7:8], 127.0,
                                                      op0=ALU.mult, op1=ALU.min))
                        fence(v)
                        p.inc("s_ev", v.tensor_scalar(qtr[:, 0:512], qtr[:, 0:512],
                                                      -128.0, None, op0=ALU.max))
                        fence(v)
                        p.inc("s_ev", v.tensor_scalar(witmp[:, 0:512], qtr[:, 0:512],
                                                      0.0, None, op0=ALU.add))
                        fence(v)
                        p.inc("s_ev", v.tensor_scalar(n2s[:, mh * SLAB:(mh + 1) * SLAB],
                                                      witmp[:, 0:512], 0.0, None,
                                                      op0=ALU.add))
            N2_DONE = p.c["s_ev"]
            for ccq in range(4):
                cc = s * 4 + ccq
                if True:
                    blk = BLK
                    @blk.tensor
                    def _(t, ccq=ccq):
                        p.wait(t, "s_ev")
                        p.wait(t, "s_sc")
                        last = None
                        for ob in range(2):
                            for hid in range(32):
                                last = t.matmul(
                                    ps[2 + ob][:, :],
                                    n2s[:, hid * SLAB + ccq * 128:
                                        hid * SLAB + (ccq + 1) * 128],
                                    t2_sb[:, hid * C + ob * 512:hid * C + (ob + 1) * 512],
                                    start=(hid == 0), stop=(hid == 31))
                        p.inc("s_mm", last)

                    @blk.sync
                    def _(sy, cc=cc):
                        p.wait(sy, "s_ev")
                        p.wait(sy, "s_sp", h1sp_sn[cc])
                        p.inc("s_ld", sy.dma_start(out=x2b[:, :],
                                                   in_=x2_sp[cc * 128:(cc + 1) * 128, :]), 16)

                    @blk.vector
                    def _(v):
                        p.wait(v, "s_mm")
                        p.wait(v, "s_ld")
                        p.inc("s_ev", v.tensor_scalar(p4s[:, 0:512], ps[2][:, :],
                                                      qcol[:, 9:10], None, op0=ALU.mult))
                        fence(v)
                        p.inc("s_ev", v.tensor_scalar(p4s[:, 512:1024], ps[3][:, :],
                                                      qcol[:, 9:10], None, op0=ALU.mult))
                        fence(v)

                    @blk.gpsimd
                    def _(g, cc=cc):
                        if cc == NCH - 1 and "d_raw" in dbg_d:
                            p.wait(g, "s_ev")
                            p.inc("s_gp", g.dma_start(out=dbg_d["d_raw"][:, :],
                                                      in_=p4s[:, 0:1024]), 16)

                    @blk.vector
                    def _(v):
                        p.wait(v, "s_gp")
                        p.inc("s_ev", v.engine_nop())
                        p.inc("s_ev", v.reduce_sum(stats[:, 0:1], p4s[:, 0:1024],
                                                   axis=AX.X))
                        fence(v)
                        p.inc("s_ev", v.tensor_scalar(stats[:, 0:1], stats[:, 0:1],
                                                      1.0 / C, None, op0=ALU.mult))
                        fence(v)
                        p.inc("s_ev", v.tensor_scalar(p4s[:, 0:1024], p4s[:, 0:1024],
                                                      stats[:, 0:1], None,
                                                      op0=ALU.subtract))

                    @blk.vector
                    def _(v):
                        p.wait(v, "s_ev")
                        p.wait(v, "s_out")
                        p.inc("s_ev", v.tensor_tensor(p4s[:, 1024:2048], p4s[:, 0:1024],
                                                      p4s[:, 0:1024], op=ALU.mult))
                        fence(v)
                        p.inc("s_ev", v.reduce_sum(stats[:, 1:2], p4s[:, 1024:2048],
                                                   axis=AX.X))
                        fence(v)
                        p.inc("s_ev", v.tensor_scalar(stats[:, 1:2], stats[:, 1:2],
                                                      1.0 / C, 1e-5, op0=ALU.mult,
                                                      op1=ALU.add))

                    @blk.vector
                    def _(v):
                        p.wait(v, "s_ev")
                        p.inc("s_ev", v.reciprocal(stats[:, 1:2], stats[:, 1:2]))

                    @blk.scalar
                    def _(sc):
                        p.wait(sc, "s_ev")
                        p.inc("s_sc", sc.activation(stats[:, 1:2], stats[:, 1:2], AF.Sqrt))

                    @blk.vector
                    def _(v):
                        p.wait(v, "s_sc")
                        p.wait(v, "s_out")
                        p.inc("s_ev", v.tensor_scalar(p4s[:, 0:1024], p4s[:, 0:1024],
                                                      stats[:, 1:2], None, op0=ALU.mult))
                        fence(v)
                        p.inc("s_ev", v.tensor_tensor(p4s[:, 1024:2048], x2b[:, :],
                                                      p4s[:, 0:1024], op=ALU.add))

                    @blk.sync
                    def _(sy, cc=cc):
                        p.wait(sy, "s_ev")
                        p.inc("s_out", sy.dma_start(out=out_d[cc * 128:(cc + 1) * 128, :],
                                                    in_=p4s[:, 1024:2048]), 16)
        if "d_qrow" in dbg_d:
            if True:
                blk = BLK
                @blk.gpsimd
                def _(g):
                    p.wait(g, "s_ev")
                    p.wait(g, "s_out")
                    p.inc("s_gp", g.dma_start(out=dbg_d["d_qrow"][:, :],
                                              in_=qrow[0:1, :]), 16)
                    p.inc("s_gp", g.dma_start(out=dbg_d["d_gt"][:, :],
                                              in_=g_sp[0, 0:128, :]), 16)
                    p.inc("s_gp", g.dma_start(out=dbg_d["d_n2"][:, :],
                                              in_=n2s[:, 0:512]), 16)
                    p.inc("s_gp", g.dma_start(out=dbg_d["d_t2"][:, :],
                                              in_=t2_sb[:, 0:1024]), 16)
                    p.inc("s_gp", g.dma_start(out=dbg_d["d_fout"][:, :],
                                              in_=p4s[:, 0:2048]), 16)

        LS4c.close()
        LS4s.close()
        LS3h.close()
        LSattc.close()


    if STAGE < 5:
        if True:
            blk = BLK
            @blk.vector
            def _(v):
                p.inc("s_ev", v.memset(zbuf[:, :], 0.0))

            @blk.sync
            def _(sy):
                p.wait(sy, "s_sp")
                p.wait(sy, "s_gp")
                p.wait(sy, "s_ev")
                for i in range(NT // 128):
                    p.inc("s_out", sy.dma_start(out=out_d[i * 128:(i + 1) * 128, :],
                                                in_=zbuf[:, 0:C]), 16)
                p.wait(sy, "s_out")
    else:
        if True:
            blk = BLK
            @blk.sync
            def _(sy):
                p.wait(sy, "s_sp")
                p.wait(sy, "s_gp")
                p.wait(sy, "s_out")

    es.close()
    return nc, [nm for nm, _, _ in dbg]


_cache = {}


def host_inputs(inputs, core):
    b, half = core // 2, core % 2
    x = inputs["L0_attn_x"]
    ut = np.triu(np.ones((128, 128), np.float32), 1)
    hselT = np.zeros((16, C), np.float32)
    for hh in range(H):
        hselT[hh, hh * D:(hh + 1) * D] = 1.0
    n1 = B * T * C
    n2 = B * T * HID
    r1, r2 = 0.995 * (n1 - 1), 0.995 * (n2 - 1)
    qtarg = np.zeros((1, 16), np.float32)
    qtarg[0, 0] = n1 - float(np.floor(r1))
    qtarg[0, 1] = r1 - np.floor(r1)
    qtarg[0, 2] = n2 - float(np.floor(r2))
    qtarg[0, 3] = r2 - np.floor(r2)
    qtarg[0, 4] = float(n1)
    qtarg[0, 5] = float(n2)
    return {
        "xin": np.ascontiguousarray(x[b, half * NT:(half + 1) * NT, :]),
        "wq": inputs["L0_attn_Wq"], "wk": inputs["L0_attn_Wk"],
        "wv": inputs["L0_attn_Wv"], "wo": inputs["L0_attn_Wo"],
        "w1t": np.ascontiguousarray(inputs["L0_ffn_w1"].T),
        "w2t": np.ascontiguousarray(inputs["L0_ffn_w2"].T),
        "alpha": np.ascontiguousarray(inputs["L0_attn_alpha_raw"].reshape(1, H)),
        "beta": np.ascontiguousarray(inputs["L0_attn_beta_raw"].reshape(1, H)),
        "lng": np.ascontiguousarray(inputs["L0_ffn_ln_g"].reshape(1, C)),
        "lnb": np.ascontiguousarray(inputs["L0_ffn_ln_b"].reshape(1, C)),
        "identin": np.eye(128, dtype=np.float32),
        "ltmin": np.tril(np.ones((128, 128), np.float32), -1),
        "utm4in": np.ascontiguousarray(np.tile(ut, (1, 4))),
        "hselTin": hselT,
        "hselin": np.ascontiguousarray(hselT.T),
        "onesin": np.ones((128, 128), np.float32),
        "selcin": np.full((64, 1), float(half), np.float32),
        "eoin": np.ascontiguousarray(np.concatenate([
            np.repeat([[1.0], [0.0]], 64, 0), np.repeat([[0.0], [1.0]], 64, 0)],
            1).astype(np.float32)),
        "qtargin": qtarg,
    }


def kernel(**inputs):
    global _dbg_out
    key = tuple(nm for nm, _, _ in DEBUG)
    if key not in _cache:
        _cache[key] = build(DEBUG)
    nc, dbg_names = _cache[key]
    in_maps = [host_inputs(inputs, c) for c in range(8)]
    res = run_bass_kernel_spmd(nc, in_maps, core_ids=list(range(8)))
    if dbg_names:
        _dbg_out = [{nm: r[nm] for nm in dbg_names} for r in res.results]
    out = np.empty((B, T, C), np.float32)
    for c in range(8):
        b, half = c // 2, c % 2
        out[b, half * NT:(half + 1) * NT, :] = res.results[c]["out"]
    return out


# version 15 (final)
# speedup vs baseline: 1.1040x; 1.0109x over previous
import numpy as np
from contextlib import ExitStack

import concourse.bass as bass
import concourse.mybir as mybir
from concourse.bass_utils import run_bass_kernel_spmd
from concourse.alu_op_type import AluOpType as ALU

F32 = mybir.dt.float32
BF16 = mybir.dt.bfloat16
I32 = mybir.dt.int32
AF = mybir.ActivationFunctionType
AX = mybir.AxisListType

B, T, C = 4, 8192, 1024
H, D, CS = 16, 64, 128
NT, NCH, HID, SLAB, NSLAB = 4096, 32, 4096, 512, 8

DEBUG = []
_dbg_out = None
STAGE = 99
SUB = 9
FEAT = {'xt','proj','knorm','chunks','quads','quad4','sevac','masktt','x0','solve','post','qhat'}


class P:
    def __init__(self, nc, es):
        self.nc, self.es, self.c = nc, es, {}

    def sem(self, name):
        h = self.es.enter_context(self.nc.semaphore(name))
        self.c[name] = 0
        setattr(self, name, h)

    def inc(self, name, instr, n=1):
        instr.then_inc(getattr(self, name), n)
        self.c[name] += n

    def wait(self, eng, name, val=None):
        eng.wait_ge(getattr(self, name), self.c[name] if val is None else val)


def build(dbg=()):
    nc = bass.Bass("TRN2", target_bir_lowering=False)
    es = ExitStack()
    p = P(nc, es)

    xin = nc.dram_tensor("xin", [NT, C], F32, kind="ExternalInput").ap()
    wq_d = nc.dram_tensor("wq", [C, C], F32, kind="ExternalInput").ap()
    wk_d = nc.dram_tensor("wk", [C, C], F32, kind="ExternalInput").ap()
    wv_d = nc.dram_tensor("wv", [C, C], F32, kind="ExternalInput").ap()
    wo_d = nc.dram_tensor("wo", [C, C], F32, kind="ExternalInput").ap()
    w1t_d = nc.dram_tensor("w1t", [C, HID], F32, kind="ExternalInput").ap()
    w2t_d = nc.dram_tensor("w2t", [HID, C], F32, kind="ExternalInput").ap()
    alpha_d = nc.dram_tensor("alpha", [1, H], F32, kind="ExternalInput").ap()
    beta_d = nc.dram_tensor("beta", [1, H], F32, kind="ExternalInput").ap()
    lng_d = nc.dram_tensor("lng", [1, C], F32, kind="ExternalInput").ap()
    lnb_d = nc.dram_tensor("lnb", [1, C], F32, kind="ExternalInput").ap()
    ident_d = nc.dram_tensor("identin", [128, 128], F32, kind="ExternalInput").ap()
    ltm_d = nc.dram_tensor("ltmin", [128, 128], F32, kind="ExternalInput").ap()
    utm4_d = nc.dram_tensor("utm4in", [128, 512], F32, kind="ExternalInput").ap()
    hselT_d = nc.dram_tensor("hselTin", [16, C], F32, kind="ExternalInput").ap()
    hsel_d = nc.dram_tensor("hselin", [C, 16], F32, kind="ExternalInput").ap()
    ones_d = nc.dram_tensor("onesin", [128, 128], F32, kind="ExternalInput").ap()
    selc_d = nc.dram_tensor("selcin", [64, 1], F32, kind="ExternalInput").ap()
    eo_d = nc.dram_tensor("eoin", [128, 2], F32, kind="ExternalInput").ap()
    qtarg_d = nc.dram_tensor("qtargin", [1, 16], F32, kind="ExternalInput").ap()
    out_d = nc.dram_tensor("out", [NT, C], F32, kind="ExternalOutput").ap()
    dbg_d = {nm: nc.dram_tensor(nm, list(sh), dt, kind="ExternalOutput").ap()
             for nm, sh, dt in dbg}

    qhat_sp = nc.dram_tensor("qhat_sp", [NCH, 64, H * CS], BF16).ap()
    yint_sp = nc.dram_tensor("yint_sp", [NCH, 128, H * D], F32).ap()
    gh_sp = nc.dram_tensor("gh_sp", [NCH, 64, H * D], BF16).ap()
    bh_sp = nc.dram_tensor("bh_sp", [NCH, 64, H * D], BF16).ap()
    n_sp = nc.dram_tensor("n_sp", [NCH, 64, H * D], BF16).ap()
    x2_sp = nc.dram_tensor("x2_sp", [NT, C], F32).ap()
    h1_sp = nc.dram_tensor("h1_sp", [NT, C], F32).ap()
    g_sp = nc.dram_tensor("g_sp", [NSLAB, HID, SLAB], F32).ap()
    pool2_d = nc.dram_tensor("pool2_d", [128, 8192], F32).ap()

    cc_nf_in = nc.dram_tensor("cc_nf_in", [64, H * D], F32).ap()
    cc_nf_out = nc.dram_tensor("cc_nf_out", [128, H * D], F32).ap()

    def cpair(nm, rows, cols, out_rows=None):
        return (nc.dram_tensor(nm + "_i", [rows, cols], F32).ap(),
                nc.dram_tensor(nm + "_o", [out_rows or rows, cols], F32).ap())

    cc_th = [cpair(f"cc_th{q}", 1, 8) for q in range(2)]
    cc_ct = [cpair(f"cc_ct{q}", 1, 8) for q in range(2)]
    cc_e = [[cpair(f"cc_e{q}_{i}", 1, 16) for i in range(4)] for q in range(2)]
    cc_gc = [cpair(f"cc_gc{q}", 128, 16, out_rows=1024) for q in range(2)]
    RG_ALL = [list(range(8))]
    RG_PAIR = [[0, 1], [2, 3], [4, 5], [6, 7]]

    for s in ["s_ld", "s_w", "s_mm", "s_ev", "s_sc", "s_sp", "s_cc", "s_out", "s_gp",
              "s_gs0", "s_gs1", "s_ps0", "s_ps1", "s_cr"]:
        p.sem(s)

    sbl = lambda name, shape, dt: es.enter_context(nc.sbuf_tensor(name, shape, dt))

    ident = sbl("ident", [128, 128], BF16)
    ones32 = sbl("ones32", [128, 128], F32)
    ident32 = sbl("ident32", [128, 128], F32)
    selcol = sbl("selcol", [64, 1], F32)
    lngb = sbl("lngb", [128, 2 * C], F32)
    stats = sbl("stats", [128, 64], F32)
    qrow = sbl("qrow", [1, 256], F32)
    qcol = sbl("qcol", [128, 16], F32)
    zbuf = sbl("zbuf", [128, C], F32) if STAGE < 5 else None
    LSattc = ExitStack()
    sbac = lambda name, shape, dt: LSattc.enter_context(nc.sbuf_tensor(name, shape, dt))
    ltm_all = sbac("ltm_all", [128, H * 128], BF16)
    utm_all = sbac("utm_all", [128, H * 128], BF16)
    sup_mask = sbac("sup_mask", [128, 512], F32)
    hselT = sbac("hselT", [16, C], F32)
    hsel = sbac("hsel", [128, 8 * 16], F32)
    ab_row = sbac("ab_row", [1, 4 * H], F32)
    acol = sbac("acol", [128, 4 * H], F32)
    ltm32 = sbac("ltm32", [128, 128], F32)
    eomask = sbac("eomask", [128, 2], F32)

    ps = [es.enter_context(nc.psum_tensor(f"ps{i}", [128, 512], F32)) for i in range(6)]
    pt = es.enter_context(nc.psum_tensor("pt", [128, 1024], BF16))
    BLK = es.enter_context(nc.Block())

    def fence(v):
        p.wait(v, "s_ev")

    if True:
        blk = BLK
        @blk.sync
        def _(s):
            for dst, srcd in [(ones32, ones_d), (sup_mask, utm4_d), (hselT, hselT_d),
                             (ltm32, ltm_d), (selcol, selc_d)]:
                p.inc("s_w", s.dma_start(out=dst[:, :], in_=srcd[:, :]), 16)
            p.inc("s_w", s.dma_start(out=hsel.rearrange("p (k h) -> p k h", h=16),
                                     in_=hsel_d.rearrange("(k p) h -> p k h", p=128)), 16)
            p.inc("s_w", s.dma_start(out=ab_row[0:1, 0:H], in_=alpha_d[:, :]), 16)
            p.inc("s_w", s.dma_start(out=ab_row[0:1, H:2 * H], in_=beta_d[:, :]), 16)
            p.inc("s_w", s.dma_start(out=qrow[0:1, 0:16], in_=qtarg_d[:, :]), 16)
            p.inc("s_w", s.dma_start(out=lngb[0:1, 0:C], in_=lng_d[:, :]), 16)
            p.inc("s_w", s.dma_start(out=lngb[0:1, C:2 * C], in_=lnb_d[:, :]), 16)
            p.inc("s_w", s.dma_start(out=eomask[:, :], in_=eo_d[:, :]), 16)
            p.inc("s_w", s.dma_start(out=ident32[:, :], in_=ident_d[:, :]), 16)

        @blk.gpsimd
        def _(g):
            p.inc("s_gp", g.dma_start(out=ident[:, :], in_=ident_d[:, :]), 16)

        @blk.vector
        def _(v):
            p.inc("s_ev", v.memset(qrow[0:1, 16:256], 0.0))

    if True:

        blk = BLK
        @blk.scalar
        def _(s):
            p.wait(s, "s_w")
            p.inc("s_sc", s.activation(ab_row[0:1, 0:2 * H], ab_row[0:1, 0:2 * H],
                                       AF.Sigmoid))
            p.wait(s, "s_sc")
            p.inc("s_sc", s.mul(ab_row[0:1, 0:2 * H], ab_row[0:1, 0:2 * H], 0.5))
            p.wait(s, "s_sc")
            p.inc("s_sc", s.mul(ab_row[0:1, 2 * H:4 * H], ab_row[0:1, 0:2 * H], -1.0))

        @blk.tensor
        def _(t):
            p.wait(t, "s_sc")
            p.inc("s_mm", t.matmul(ps[0][:, 0:4 * H], ones32[0:1, :], ab_row[0:1, :],
                                   start=True, stop=True))

        @blk.vector
        def _(v):
            p.wait(v, "s_mm")
            p.inc("s_ev", v.tensor_copy(acol[:, :], ps[0][:, 0:4 * H]))
            p.wait(v, "s_ev")
            for hh in range(H):
                na = acol[:, 2 * H + hh:2 * H + hh + 1]
                v.tensor_scalar(ltm_all[:, hh * 128:(hh + 1) * 128], ltm32[:, :],
                                na, None, op0=ALU.mult)
            last = None
            for hh in range(H):
                na = acol[:, 2 * H + hh:2 * H + hh + 1]
                last = v.tensor_scalar(utm_all[:, hh * 128:(hh + 1) * 128],
                                       sup_mask[:, 0:128], na, None, op0=ALU.mult)
            p.inc("s_ev", last)

    for j in range(4):
        if True:
            blk = BLK
            @blk.tensor
            def _(t, j=j):
                p.wait(t, "s_w")
                p.wait(t, "s_ev")
                p.inc("s_mm", t.matmul(ps[1][:, :], ones32[0:1, :],
                                       lngb[0:1, j * 512:(j + 1) * 512],
                                       start=True, stop=True))

            @blk.vector
            def _(v, j=j):
                p.wait(v, "s_mm")
                p.inc("s_ev", v.tensor_copy(lngb[:, j * 512:(j + 1) * 512], ps[1][:, :]))

    LSw = ExitStack()
    sbw = lambda name, shape, dt: LSw.enter_context(nc.sbuf_tensor(name, shape, dt))
    LS3h = ExitStack()
    habsA = LS3h.enter_context(nc.sbuf_tensor("habsA", [128, 4 * C], BF16))
    LSwo = ExitStack()
    wo_sb = LSwo.enter_context(nc.sbuf_tensor("wo_sb", [128, 8 * C], BF16))
    wq_sb = sbw("wq_sb", [128, 8 * C], BF16)
    wk_sb = sbw("wk_sb", [128, 8 * C], BF16)
    wv_sb = sbw("wv_sb", [128, 8 * C], BF16)
    if True:
        blk = BLK
        @blk.gpsimd
        def _(g):
            for w_sb, w_d in [(wq_sb, wq_d), (wk_sb, wk_d), (wv_sb, wv_d), (wo_sb, wo_d)]:
                p.inc("s_gp", g.dma_start(out=w_sb.rearrange("p (k o) -> p k o", o=C),
                                         in_=w_d.rearrange("(k p) o -> p k o", p=128)), 16)
    W_READY = p.c["s_gp"]

    LS1 = ExitStack()
    sb1 = lambda name, shape, dt: LS1.enter_context(nc.sbuf_tensor(name, shape, dt))
    xsl = sb1("xsl", [128, 4 * C], F32)
    scr32 = sb1("scr32", [128, 2 * C], F32)
    xlnb = sb1("xlnb", [128, 4 * C], BF16)
    xlnT = sb1("xlnT", [128, 8 * SLAB], BF16)
    q_fm = sb1("q_fm", [128, 8 * SLAB], BF16)
    k_fm = sb1("k_fm", [128, 8 * SLAB], BF16)
    kt_fm = sb1("kt_fm", [128, 8 * SLAB], BF16)
    kto = sb1("kto", [128, 8 * SLAB], BF16)
    v_tok = sb1("v_tok", [128, 4 * C], BF16)
    ktok = sb1("ktok", [128, C], BF16)
    xq = sb1("xq", [128, SLAB], BF16)
    pl_a = sb1("pl_a", [128, SLAB], BF16)
    pu_a = sb1("pu_a", [128, SLAB], BF16)
    pl_b = sb1("pl_b", [128, SLAB], BF16)
    pu_b = sb1("pu_b", [128, SLAB], BF16)
    naw = sb1("naw", [128, 256], BF16)
    rbq = sb1("rbq", [128, 256], BF16)
    sup_q = sb1("sup_q", [128, SLAB], BF16)
    pl_c = sb1("pl_c", [128, SLAB], BF16)
    pu_c = sb1("pu_c", [128, SLAB], BF16)
    pl_d = sb1("pl_d", [128, SLAB], BF16)
    pu_d = sb1("pu_d", [128, SLAB], BF16)
    xq2 = sb1("xq2", [128, SLAB], BF16)
    naw2 = sb1("naw2", [128, 256], BF16)
    rbq2 = sb1("rbq2", [128, 256], BF16)
    sup_q2 = sb1("sup_q2", [128, SLAB], BF16)
    ghq = sb1("ghq", [64, H * D], BF16)
    bhq = sb1("bhq", [64, H * D], BF16)
    qhats = sb1("qhats", [64, H * CS], BF16)
    yints = sb1("yints", [128, H * D], F32)

    h1sp_sn = {}

    def p1_chunk(s, ccq, cc):
        for half in range(2):
            if True:
                blk = BLK
                @blk.tensor
                def _(t, half=half):
                    p.wait(t, "s_sc")
                    last = None
                    for j in range(4):
                        kk = half * 4 + j
                        last = t.transpose(pt[:, j * 256:j * 256 + 128],
                                           kt_fm[:, kk * SLAB + ccq * 128:
                                                 kk * SLAB + (ccq + 1) * 128],
                                           ident[:, :])
                    p.inc("s_mm", last)

                @blk.vector
                def _(v, half=half):
                    p.wait(v, "s_mm")
                    last = None
                    for j in range(4):
                        kk = half * 4 + j
                        last = v.tensor_copy(ktok[:, kk * 128:(kk + 1) * 128],
                                       pt[:, j * 256:j * 256 + 128])
                    p.inc("s_sc", last)
        KT_DONE = p.c["s_sc"]
        if 'quads' not in FEAT:
            return

        for qp in range(2):
            qa, qb = 2 * qp, 2 * qp + 1
            SETS = [
                dict(qd=qa, heads=[4 * qa + j for j in range(4)],
                     pl0=pl_a, pu0=pu_a, pl1=pl_b, pu1=pu_b, xqt=xq,
                     nawt=naw, rbqt=rbq, supt=sup_q, so=0, po=512,
                     bX=ps[1], bL=ps[2], bU=ps[3], bGh=ps[4], bPq=ps[0]),
                dict(qd=qb, heads=[4 * qb + j for j in range(4)],
                     pl0=pl_c, pu0=pu_c, pl1=pl_d, pu1=pu_d, xqt=xq2,
                     nawt=naw2, rbqt=rbq2, supt=sup_q2, so=1024, po=1536,
                     bX=ps[4], bL=ps[5], bU=ps[0], bGh=ps[5], bPq=ps[2]),
            ]
            blk = BLK

            def st_mms(t, S):
                last = None
                for j, hh in enumerate(S['heads']):
                    kk = hh // 2
                    ksel = (k_fm if hh % 2 == 0 else kto)[
                        :, kk * SLAB + ccq * 128:kk * SLAB + (ccq + 1) * 128]
                    kful = kt_fm[:, kk * SLAB + ccq * 128:kk * SLAB + (ccq + 1) * 128]
                    last = t.matmul(ps[0][:, j * 128:(j + 1) * 128], ksel, kful,
                                    start=True, stop=True)
                p.inc("s_mm", last)

            def masks_x0(v, S):
                qd = S['qd']
                v.tensor_tensor(S['pl0'][:, :], ps[0][:, :],
                                ltm_all[:, qd * 512:(qd + 1) * 512], op=ALU.mult)
                p.inc("s_ev", v.tensor_tensor(
                    S['pu0'][:, :], ps[0][:, :],
                    utm_all[:, qd * 512:(qd + 1) * 512], op=ALU.mult))
                S['mk'] = p.c["s_ev"]
                v.tensor_copy(
                    S['xqt'].rearrange("p (j two d) -> p j two d", two=2, d=64)[:, :, 0, :],
                    ktok.rearrange("p (j d) -> p j d", d=64)[:, qd * 4:qd * 4 + 4, :])
                p.inc("s_ev", v.tensor_copy(
                    S['xqt'].rearrange("p (j two d) -> p j two d", two=2, d=64)[:, :, 1, :],
                    v_tok.rearrange("p (cq j d) -> p cq j d", cq=4, d=64)
                    [:, ccq, qd * 4:qd * 4 + 4, :]))

            @blk.tensor
            def _(t):
                p.wait(t, "s_sc")
                p.wait(t, "s_ev")
                st_mms(t, SETS[0])
            stA = p.c["s_mm"]

            @blk.vector
            def _(v):
                p.wait(v, "s_mm", stA)
                masks_x0(v, SETS[0])
            xA = p.c["s_ev"]

            @blk.tensor
            def _(t, mkA=SETS[0]['mk']):
                p.wait(t, "s_ev", mkA)
                st_mms(t, SETS[1])
            stB = p.c["s_mm"]

            @blk.vector
            def _(v):
                p.wait(v, "s_mm", stB)
                masks_x0(v, SETS[1])
            xB = p.c["s_ev"]

            for lvl in range(7):
                for si, S in enumerate(SETS):
                    cl, cu = (S['pl0'], S['pu0']) if lvl % 2 == 0 else (S['pl1'], S['pu1'])
                    nl, nu = (S['pl1'], S['pu1']) if lvl % 2 == 0 else (S['pl0'], S['pu0'])

                    @blk.tensor
                    def _(t, S=S, cl=cl, cu=cu, lvl=lvl,
                          xs=(xA if si == 0 else xB)):
                        p.wait(t, "s_ev", xs)
                        bX, bL, bU = S['bX'], S['bL'], S['bU']
                        last = None
                        for j in range(4):
                            last = t.matmul(bX[:, j * 128:(j + 1) * 128],
                                            cu[:, j * 128:(j + 1) * 128],
                                            S['xqt'][:, j * 128:(j + 1) * 128],
                                            start=True, stop=True)
                        if lvl < 6:
                            for j in range(4):
                                t.matmul(bL[:, j * 128:(j + 1) * 128],
                                         cu[:, j * 128:(j + 1) * 128],
                                         cl[:, j * 128:(j + 1) * 128],
                                         start=True, stop=True)
                            for j in range(4):
                                last = t.matmul(bU[:, j * 128:(j + 1) * 128],
                                                cl[:, j * 128:(j + 1) * 128],
                                                cu[:, j * 128:(j + 1) * 128],
                                                start=True, stop=True)
                        p.inc("s_mm", last)
                    msnap = p.c["s_mm"]

                    @blk.vector
                    def _(v, S=S, msnap=msnap, nu=nu, nl=nl, lvl=lvl):
                        p.wait(v, "s_mm", msnap)
                        v.tensor_tensor(S['xqt'][:, :], S['xqt'][:, :],
                                        S['bX'][:, :], op=ALU.add)
                        if lvl < 6:
                            v.tensor_copy(nu[:, :], S['bU'][:, :])
                            p.inc("s_ev", v.tensor_copy(nl[:, :], S['bL'][:, :]))
                        else:
                            p.inc("s_ev", v.engine_nop())
                    if si == 0:
                        xA = p.c["s_ev"]
                    else:
                        xB = p.c["s_ev"]

            nw = [None, None]
            for si, S in enumerate(SETS):
                @blk.vector
                def _(v, S=S):
                    fence(v)
                    for j, hh in enumerate(S['heads']):
                        v.tensor_scalar(S['nawt'][:, j * 64:(j + 1) * 64],
                                        S['xqt'][:, j * 128:j * 128 + 64],
                                        acol[:, 2 * H + hh:2 * H + hh + 1],
                                        None, op0=ALU.mult)
                    last = None
                    for j, hh in enumerate(S['heads']):
                        last = v.tensor_scalar(S['rbqt'][:, j * 64:(j + 1) * 64],
                                               S['xqt'][:, j * 128 + 64:(j + 1) * 128],
                                               acol[:, H + hh:H + hh + 1],
                                               None, op0=ALU.mult)
                    p.inc("s_ev", last)
                nw[si] = p.c["s_ev"]

            gsn = [None, None]
            sup_sn = [None, None]
            for si, S in enumerate(SETS):
                @blk.tensor
                def _(t, S=S, si=si):
                    p.wait(t, "s_ev", nw[si])
                    bGh, bPq = S['bGh'], S['bPq']
                    for j, hh in enumerate(S['heads']):
                        t.matmul(bGh[0:64, j * 64:(j + 1) * 64],
                                 S['nawt'][:, j * 64:(j + 1) * 64],
                                 ktok[:, hh * 64:(hh + 1) * 64],
                                 start=True, stop=True)
                    for j, hh in enumerate(S['heads']):
                        t.matmul(bGh[0:64, 256 + j * 64:256 + (j + 1) * 64],
                                 ktok[:, hh * 64:(hh + 1) * 64],
                                 S['rbqt'][:, j * 64:(j + 1) * 64],
                                 start=True, stop=True)
                    last = None
                    for j, hh in enumerate(S['heads']):
                        kk = hh // 2
                        kslice = (k_fm if hh % 2 == 0 else kto)[
                            :, kk * SLAB + ccq * 128:kk * SLAB + (ccq + 1) * 128]
                        qslice = q_fm[:, kk * SLAB + ccq * 128:kk * SLAB + (ccq + 1) * 128]
                        last = t.matmul(bPq[:, j * 128:(j + 1) * 128],
                                        kslice, qslice, start=True, stop=True)
                    p.inc("s_mm", last)
                gsn[si] = p.c["s_mm"]

                @blk.vector
                def _(v, S=S, si=si):
                    p.wait(v, "s_mm", gsn[si])
                    p.wait(v, "s_sp")
                    qd = S['qd']
                    v.tensor_copy(ghq[:, qd * 256:(qd + 1) * 256], S['bGh'][0:64, 0:256])
                    p.inc("s_sc", v.tensor_copy(bhq[:, qd * 256:(qd + 1) * 256],
                                                S['bGh'][0:64, 256:512]))

                @blk.vector
                def _(v, S=S, si=si):
                    p.wait(v, "s_mm", gsn[si])
                    p.inc("s_ev", v.tensor_tensor(S['supt'][:, :], S['bPq'][:, :],
                                                  sup_mask[:, :], op=ALU.mult))
                sup_sn[si] = p.c["s_ev"]

            prev_q = None
            for si, S in enumerate(SETS):
                @blk.tensor
                def _(t, S=S, si=si, prev_q=prev_q):
                    p.wait(t, "s_ev", sup_sn[si] if si == 0 else p.c["s_ev"])
                    if si == 1:
                        p.wait(t, "s_sc", prev_q)
                    for j, hh in enumerate(S['heads']):
                        kk, hf = hh // 2, (hh % 2) * 64
                        qtile = q_fm[:, kk * SLAB + ccq * 128:kk * SLAB + (ccq + 1) * 128]
                        t.matmul(ps[1][0:64, j * 128:(j + 1) * 128],
                                 ident[:, hf:hf + 64], qtile, start=True, stop=False)
                        t.matmul(ps[1][0:64, j * 128:(j + 1) * 128],
                                 S['nawt'][:, j * 64:(j + 1) * 64],
                                 S['supt'][:, j * 128:(j + 1) * 128],
                                 start=False, stop=True)
                    last = None
                    for j in range(4):
                        last = t.matmul(ps[3][:, j * 64:(j + 1) * 64],
                                        S['supt'][:, j * 128:(j + 1) * 128],
                                        S['rbqt'][:, j * 64:(j + 1) * 64],
                                        start=True, stop=True)
                    p.inc("s_mm", last)
                qsn = p.c["s_mm"]

                @blk.vector
                def _(v, S=S, qsn=qsn):
                    p.wait(v, "s_mm", qsn)
                    p.wait(v, "s_sp")
                    qd = S['qd']
                    p.inc("s_sc", v.tensor_copy(qhats[:, qd * 512:(qd + 1) * 512],
                                                ps[1][0:64, :]))
                    p.inc("s_ev", v.tensor_copy(yints[:, qd * 256:(qd + 1) * 256],
                                                ps[3][:, 0:256]))
                prev_q = p.c["s_sc"]

        SP_READY_SC = p.c["s_sc"]
        SP_READY_EV = p.c["s_ev"]
        if True:
            blk = BLK
            @blk.sync
            def _(sy):
                p.wait(sy, "s_sc", SP_READY_SC)
                p.wait(sy, "s_ev", SP_READY_EV)
                p.inc("s_sp", sy.dma_start(out=qhat_sp[cc], in_=qhats[:, :]), 16)
                p.inc("s_sp", sy.dma_start(out=yint_sp[cc], in_=yints[:, :]), 16)
                p.inc("s_sp", sy.dma_start(out=gh_sp[cc], in_=ghq[:, :]), 16)
                p.inc("s_sp", sy.dma_start(out=bh_sp[cc], in_=bhq[:, :]), 16)

    def p1_slab(s):
        if True:
            blk = BLK
            @blk.sync
            def _(sy):
                p.wait(sy, "s_ev")
                for ccq in range(4):
                    p.inc("s_ld", sy.dma_start(
                        out=xsl[:, ccq * C:(ccq + 1) * C],
                        in_=xin[(s * 4 + ccq) * 128:(s * 4 + ccq + 1) * 128, :]), 16)

        if True:
            blk = BLK
            @blk.vector
            def _(v):
                p.wait(v, "s_ld")
                if s == 0:
                    p.wait(v, "s_ev")
                for ccq in range(4):
                    xt = xsl[:, ccq * C:(ccq + 1) * C]
                    mcol = stats[:, ccq:ccq + 1]
                    p.inc("s_ev", v.reduce_sum(mcol, xt, axis=AX.X))
                    p.wait(v, "s_ev")
                    p.inc("s_ev", v.tensor_scalar(mcol, mcol, 1.0 / C, None,
                                                  op0=ALU.mult))
                    p.wait(v, "s_ev")
                    p.inc("s_ev", v.tensor_scalar(xt, xt, mcol, None,
                                                  op0=ALU.subtract))

            @blk.vector
            def _(v):
                p.wait(v, "s_ev")
                p.inc("s_ev", v.tensor_tensor(scr32[:, :], xsl[:, 0:2 * C],
                                              xsl[:, 0:2 * C], op=ALU.mult))
                fence(v)
                last = None
                for ccq in range(2):
                    last = v.reduce_sum(stats[:, 4 + ccq:5 + ccq],
                                 scr32[:, ccq * C:(ccq + 1) * C], axis=AX.X)
                p.inc("s_ev", last)

            @blk.vector
            def _(v):
                p.wait(v, "s_ev")
                p.inc("s_ev", v.tensor_tensor(scr32[:, :], xsl[:, 2 * C:4 * C],
                                              xsl[:, 2 * C:4 * C], op=ALU.mult))
                fence(v)
                last = None
                for ccq in range(2, 4):
                    last = v.reduce_sum(stats[:, 4 + ccq:5 + ccq],
                                 scr32[:, (ccq - 2) * C:(ccq - 1) * C], axis=AX.X)
                p.inc("s_ev", last)
                p.wait(v, "s_ev")
                p.inc("s_ev", v.tensor_scalar(stats[:, 8:12], stats[:, 4:8],
                                              1.0 / C, 1e-5, op0=ALU.mult, op1=ALU.add))

            @blk.vector
            def _(v):
                p.wait(v, "s_ev")
                p.inc("s_ev", v.reciprocal(stats[:, 8:12], stats[:, 8:12]))

            @blk.scalar
            def _(sc):
                p.wait(sc, "s_ev")
                p.inc("s_sc", sc.activation(stats[:, 8:12], stats[:, 8:12], AF.Sqrt))

            @blk.vector
            def _(v):
                p.wait(v, "s_sc")
                last = None
                for ccq in range(4):
                    last = v.tensor_scalar(xlnb[:, ccq * C:(ccq + 1) * C],
                                           xsl[:, ccq * C:(ccq + 1) * C],
                                           stats[:, 8 + ccq:9 + ccq], None,
                                           op0=ALU.mult)
                p.inc("s_ev", last)
        LN_DONE = p.c["s_ev"]

        if 'xt' not in FEAT:
            return
        blocks = [(ccq, kk) for ccq in range(4) for kk in range(8)]
        for bi in range(0, 32, 4):
            if True:
                blk = BLK
                @blk.tensor
                def _(t, bi=bi):
                    p.wait(t, "s_ev", LN_DONE)
                    if s == 0 and bi == 0:
                        p.wait(t, "s_gp", W_READY)
                    p.wait(t, "s_sc")
                    last = None
                    for j in range(4):
                        ccq, kk = blocks[bi + j]
                        last = t.transpose(
                            pt[:, j * 256:j * 256 + 128],
                            xlnb[:, ccq * C + kk * 128:ccq * C + (kk + 1) * 128],
                            ident[:, :])
                    p.inc("s_mm", last)

                @blk.vector
                def _(v, bi=bi):
                    p.wait(v, "s_mm")
                    last = None
                    for j in range(4):
                        ccq, kk = blocks[bi + j]
                        last = v.tensor_copy(xlnT[:, kk * SLAB + ccq * 128:
                                            kk * SLAB + (ccq + 1) * 128],
                                       pt[:, j * 256:j * 256 + 128])
                    p.inc("s_sc", last)
        XT_DONE = p.c["s_sc"]

        if 'proj' not in FEAT:
            return
        for wsb, dst in [(wq_sb, q_fm), (wk_sb, k_fm)]:
            for mo in range(8):
                bank = ps[mo % 2]
                if True:
                    blk = BLK
                    @blk.tensor
                    def _(t, mo=mo, wsb=wsb, bank=bank):
                        p.wait(t, "s_sc", XT_DONE)
                        p.wait(t, "s_ev")
                        last = None
                        for kk in range(8):
                            last = t.matmul(
                                bank[:, :],
                                wsb[:, kk * C + mo * 128:kk * C + (mo + 1) * 128],
                                xlnT[:, kk * SLAB:(kk + 1) * SLAB],
                                start=(kk == 0), stop=(kk == 7))
                        p.inc("s_mm", last)

                    @blk.vector
                    def _(v, mo=mo, dst=dst, bank=bank):
                        p.wait(v, "s_mm")
                        p.inc("s_ev", v.tensor_copy(dst[:, mo * SLAB:(mo + 1) * SLAB],
                                                    bank[:, :]))

        for ccq in range(4):
            for ob in range(2):
                bank = ps[2 + (ob % 2)]
                if True:
                    blk = BLK
                    @blk.tensor
                    def _(t, ccq=ccq, ob=ob, bank=bank):
                        p.wait(t, "s_ev")
                        p.wait(t, "s_sc")
                        last = None
                        for kk in range(8):
                            last = t.matmul(
                                bank[:, :],
                                xlnT[:, kk * SLAB + ccq * 128:kk * SLAB + (ccq + 1) * 128],
                                wv_sb[:, kk * C + ob * 512:kk * C + (ob + 1) * 512],
                                start=(kk == 0), stop=(kk == 7))
                        p.inc("s_mm", last)

                    @blk.vector
                    def _(v, ccq=ccq, ob=ob, bank=bank):
                        p.wait(v, "s_mm")
                        p.inc("s_sc", v.tensor_copy(
                            v_tok[:, ccq * C + ob * 512:ccq * C + (ob + 1) * 512],
                            bank[:, :]))

        if 'knorm' not in FEAT:
            return
        if True:
            blk = BLK
            @blk.vector
            def _(v):
                p.wait(v, "s_ev")
                p.inc("s_ev", v.tensor_tensor(scr32[:, :], k_fm[:, 0:4 * SLAB],
                                              k_fm[:, 0:4 * SLAB], op=ALU.mult))

            @blk.tensor
            def _(t):
                p.wait(t, "s_ev")
                last = None
                for kk in range(4):
                    last = t.matmul(ps[4][0:16, :], hsel[:, kk * 16:(kk + 1) * 16],
                                    scr32[:, kk * SLAB:(kk + 1) * SLAB],
                                    start=(kk == 0), stop=False)
                p.inc("s_mm", last)

            @blk.vector
            def _(v):
                p.wait(v, "s_mm")
                p.inc("s_ev", v.tensor_tensor(scr32[:, :], k_fm[:, 4 * SLAB:8 * SLAB],
                                              k_fm[:, 4 * SLAB:8 * SLAB], op=ALU.mult))

            @blk.tensor
            def _(t):
                p.wait(t, "s_ev")
                last = None
                for kk in range(4):
                    last = t.matmul(ps[4][0:16, :], hsel[:, (4 + kk) * 16:(5 + kk) * 16],
                                    scr32[:, kk * SLAB:(kk + 1) * SLAB],
                                    start=False, stop=(kk == 3))
                p.inc("s_mm", last)

            @blk.vector
            def _(v):
                p.wait(v, "s_mm")
                p.inc("s_ev", v.tensor_scalar(scr32[0:16, 0:SLAB],
                                              ps[4][0:16, :], 1e-24, None, op0=ALU.max))

            @blk.vector
            def _(v):
                p.wait(v, "s_ev")
                p.inc("s_ev", v.reciprocal(scr32[0:16, 0:SLAB], scr32[0:16, 0:SLAB]))

            @blk.scalar
            def _(sc):
                p.wait(sc, "s_ev")
                p.inc("s_sc", sc.activation(scr32[0:16, 0:SLAB], scr32[0:16, 0:SLAB], AF.Sqrt))

        for kk in range(8):
            bank = ps[2 + (kk % 2)]
            if True:
                blk = BLK
                @blk.tensor
                def _(t, kk=kk, bank=bank):
                    p.wait(t, "s_sc")
                    p.wait(t, "s_ev")
                    p.inc("s_mm", t.matmul(bank[:, :], hselT[:, kk * 128:(kk + 1) * 128],
                                           scr32[0:16, 0:SLAB], start=True, stop=True))

                @blk.vector
                def _(v, kk=kk, bank=bank):
                    p.wait(v, "s_mm")
                    p.inc("s_ev", v.tensor_tensor(kt_fm[:, kk * SLAB:(kk + 1) * SLAB],
                                                  k_fm[:, kk * SLAB:(kk + 1) * SLAB],
                                                  bank[:, :], op=ALU.mult))
                    p.wait(v, "s_ev")
                    v.tensor_scalar(k_fm[:, kk * SLAB:(kk + 1) * SLAB],
                                    kt_fm[:, kk * SLAB:(kk + 1) * SLAB],
                                    eomask[:, 0:1], None, op0=ALU.mult)
                    p.inc("s_ev", v.tensor_scalar(kto[:, kk * SLAB:(kk + 1) * SLAB],
                                                  kt_fm[:, kk * SLAB:(kk + 1) * SLAB],
                                                  eomask[:, 1:2], None, op0=ALU.mult))

        if 'chunks' in FEAT:
            for ccq in range(4):
                p1_chunk(s, ccq, s * 4 + ccq)

    nslab_run = NSLAB if STAGE >= 2 else (1 if STAGE >= 1 else 0)
    for s in range(nslab_run):
        p1_slab(s)

    def dbg_store(name, src_ap, rows=None):
        if name in dbg_d:
            if True:
                blk = BLK
                @blk.gpsimd
                def _(sy):
                    p.wait(sy, "s_ev")
                    p.wait(sy, "s_sc")
                    p.wait(sy, "s_mm")
                    p.inc("s_gp", sy.dma_start(out=dbg_d[name][:, :], in_=src_ap), 16)

    dbg_store("d_qfm", q_fm[:, :])
    dbg_store("d_ktfm", kt_fm[:, :])
    dbg_store("d_vtok", v_tok[:, :])
    dbg_store("d_xq", xq[:, :])
    dbg_store("d_ghq", ghq[:, :])
    dbg_store("d_bhq", bhq[:, :])
    dbg_store("d_qhats", qhats[:, :])
    dbg_store("d_yints", yints[:, :])
    LS1.close()
    LSw.close()

    if STAGE >= 3:
        LS2 = ExitStack()
        sb2 = lambda name, shape, dt: LS2.enter_context(nc.sbuf_tensor(name, shape, dt))
        ghb = sb2("ghb", [64, 2 * H * D], BF16)
        bhb = sb2("bhb", [64, 2 * H * D], BF16)
        nst = sb2("nst", [64, 2 * H * D], BF16)
        nb_s = sb2("nb_s", [64, H * D], BF16)
        nfin = sb2("nfin", [64, H * D], F32)
        ninit = sb2("ninit", [64, H * D], F32)

        def scan_pass(save):
            if True:
                blk = BLK
                @blk.vector
                def _(v):
                    p.wait(v, "s_cc")
                    p.wait(v, "s_ld")
                    p.wait(v, "s_cr")
                    if not save:
                        p.inc("s_ev", v.memset(nst[:, 0:H * D], 0.0))
                    else:
                        p.inc("s_ev", v.tensor_scalar(nst[:, 0:H * D], ninit[:, :],
                                                      selcol[:, 0:1], None, op0=ALU.mult))
                @blk.sync
                def _(sy):
                    p.wait(sy, "s_ev")
                    if save:
                        p.inc("s_sp", sy.dma_start(out=n_sp[0], in_=nst[:, 0:H * D]), 16)
            for cc in range(NCH):
                cur, nxt = cc % 2, (cc + 1) % 2
                if True:
                    blk = BLK
                    @blk.sync
                    def _(sy):
                        p.wait(sy, "s_sc")
                        p.wait(sy, "s_sp")
                        p.inc("s_ld", sy.dma_start(out=ghb[:, cur * H * D:(cur + 1) * H * D],
                                                   in_=gh_sp[cc]), 16)
                        p.inc("s_ld", sy.dma_start(out=bhb[:, cur * H * D:(cur + 1) * H * D],
                                                   in_=bh_sp[cc]), 16)

                    @blk.vector
                    def _(v):
                        p.wait(v, "s_ld")
                        p.wait(v, "s_ev")
                        p.inc("s_ev", v.tensor_tensor(nb_s[:, :], nst[:, cur * H * D:(cur + 1) * H * D],
                                                      bhb[:, cur * H * D:(cur + 1) * H * D],
                                                      op=ALU.add))

                    @blk.tensor
                    def _(t):
                        p.wait(t, "s_ev")
                        p.wait(t, "s_sc")
                        last = None
                        for hh in range(H):
                            reg = ps[4 if hh < 8 else 5][0:64,
                                  (hh % 8) * 64:(hh % 8 + 1) * 64]
                            t.matmul(reg, ghb[:, cur * H * D + hh * D:cur * H * D + (hh + 1) * D],
                                     nst[:, cur * H * D + hh * D:cur * H * D + (hh + 1) * D],
                                     start=True, stop=False)
                            last = t.matmul(reg, ident[0:64, 0:64],
                                            nb_s[:, hh * D:(hh + 1) * D],
                                            start=False, stop=True)
                        p.inc("s_mm", last)

                    @blk.vector
                    def _(v):
                        p.wait(v, "s_mm")
                        v.tensor_copy(nst[:, nxt * H * D:nxt * H * D + 512], ps[4][0:64, :])
                        p.inc("s_sc", v.tensor_copy(nst[:, nxt * H * D + 512:(nxt + 1) * H * D],
                                              ps[5][0:64, :]))

                    @blk.sync
                    def _(sy, cc=cc):
                        p.wait(sy, "s_sc")
                        if save and cc < NCH - 1:
                            p.inc("s_sp", sy.dma_start(out=n_sp[cc + 1],
                                                       in_=nst[:, nxt * H * D:(nxt + 1) * H * D]), 16)

            fin = (NCH % 2)
            return fin

        fin = scan_pass(save=False)
        if True:
            blk = BLK
            @blk.vector
            def _(v):
                p.wait(v, "s_sc")
                p.inc("s_ev", v.tensor_copy(nfin[:, :], nst[:, fin * H * D:(fin + 1) * H * D]))

            @blk.sync
            def _(sy):
                p.wait(sy, "s_ev")
                p.inc("s_sp", sy.dma_start(out=cc_nf_in[:, :], in_=nfin[:, :]), 16)

            @blk.gpsimd
            def _(g):
                p.wait(g, "s_sp")
                p.inc("s_cc", g.collective_compute(
                    "AllGather", ALU.bypass, replica_groups=RG_PAIR,
                    ins=[cc_nf_in[:, :]], outs=[cc_nf_out[:, :]]), 1)

            @blk.sync
            def _(sy):
                p.wait(sy, "s_cc")
                p.inc("s_cr", sy.dma_start(out=ninit[:, :], in_=cc_nf_out[0:64, :]), 16)

        if True:

            blk = BLK
            @blk.vector
            def _(v):
                p.wait(v, "s_ld")
                p.inc("s_ev", v.engine_nop())
        scan_pass(save=True)
        LS2.close()

    if STAGE >= 4:
        LS3 = ExitStack()
        sb3 = lambda name, shape, dt: LS3.enter_context(nc.sbuf_tensor(name, shape, dt))
        qhb = sb3("qhb", [64, H * CS], BF16)
        yib = sb3("yib", [128, H * D], F32)
        n0b = sb3("n0b", [64, H * D], BF16)
        xrb = sb3("xrb", [128, C], F32)
        ych = sb3("ych", [128, C], BF16)
        yT = sb3("yT", [128, C], BF16)
        p3s = sb3("p3s", [128, 4 * C], F32)

        for cc in range(NCH):
            if True:
                blk = BLK
                @blk.sync
                def _(sy, cc=cc):
                    p.wait(sy, "s_ev")
                    p.wait(sy, "s_sc")
                    p.wait(sy, "s_sp")
                    p.inc("s_ld", sy.dma_start(out=qhb[:, :], in_=qhat_sp[cc]), 16)
                    p.inc("s_ld", sy.dma_start(out=yib[:, :], in_=yint_sp[cc]), 16)
                    p.inc("s_ld", sy.dma_start(out=n0b[:, :], in_=n_sp[cc]), 16)
                    p.inc("s_ld", sy.dma_start(out=xrb[:, :],
                                               in_=xin[cc * 128:(cc + 1) * 128, :]), 16)

                @blk.tensor
                def _(t):
                    p.wait(t, "s_ld")
                    p.wait(t, "s_ev")
                    last = None
                    for hh in range(H):
                        reg = ps[0 if hh < 8 else 1][:, (hh % 8) * 64:(hh % 8 + 1) * 64]
                        last = t.matmul(reg, qhb[:, hh * CS:(hh + 1) * CS],
                                        n0b[:, hh * D:(hh + 1) * D],
                                        start=True, stop=True)
                    p.inc("s_mm", last)

                @blk.vector
                def _(v):
                    p.wait(v, "s_mm")
                    v.tensor_tensor(ych[:, 0:512], ps[0][:, :], yib[:, 0:512], op=ALU.add)
                    p.inc("s_ev", v.tensor_tensor(ych[:, 512:1024], ps[1][:, :],
                                                  yib[:, 512:1024], op=ALU.add))
            Y_DONE = p.c["s_ev"]

            for half in range(2):
                if True:
                    blk = BLK
                    @blk.tensor
                    def _(t, half=half):
                        p.wait(t, "s_ev", Y_DONE)
                        p.wait(t, "s_sc")
                        last = None
                        for j in range(4):
                            kk = half * 4 + j
                            last = t.transpose(pt[:, j * 256:j * 256 + 128],
                                               ych[:, kk * 128:(kk + 1) * 128],
                                               ident[:, :])
                        p.inc("s_mm", last)

                    @blk.vector
                    def _(v, half=half):
                        p.wait(v, "s_mm")
                        last = None
                        for j in range(4):
                            kk = half * 4 + j
                            last = v.tensor_copy(yT[:, kk * 128:(kk + 1) * 128],
                                           pt[:, j * 256:j * 256 + 128])
                        p.inc("s_sc", last)
            YT_DONE = p.c["s_sc"]

            if True:
                blk = BLK
                @blk.tensor
                def _(t):
                    p.wait(t, "s_sc", YT_DONE)
                    p.wait(t, "s_ev")
                    last = None
                    for ob in range(2):
                        for kk in range(8):
                            last = t.matmul(ps[2 + ob][:, :],
                                            yT[:, kk * 128:(kk + 1) * 128],
                                            wo_sb[:, kk * C + ob * 512:kk * C + (ob + 1) * 512],
                                            start=(kk == 0), stop=(kk == 7))
                    p.inc("s_mm", last)

                @blk.vector
                def _(v):
                    p.wait(v, "s_mm")
                    v.tensor_copy(p3s[:, 0:512], ps[2][:, :])
                    p.inc("s_sc", v.tensor_copy(p3s[:, 512:1024], ps[3][:, :]))

            def ln_stats(v, sc, seg, mc, vc):
                pass

            if True:

                blk = BLK
                @blk.vector
                def _(v):
                    p.wait(v, "s_sc")
                    att = p3s[:, 0:1024]
                    p.inc("s_ev", v.reduce_sum(stats[:, 0:1], att, axis=AX.X))
                    p.wait(v, "s_ev")
                    p.inc("s_ev", v.tensor_scalar(stats[:, 0:1], stats[:, 0:1],
                                                  1.0 / C, None, op0=ALU.mult))
                    p.wait(v, "s_ev")
                    p.inc("s_ev", v.tensor_scalar(att, att, stats[:, 0:1], None,
                                                  op0=ALU.subtract))

                @blk.vector
                def _(v):
                    p.wait(v, "s_ev")
                    p.inc("s_ev", v.tensor_tensor(p3s[:, 3072:4096], p3s[:, 0:1024],
                                                  p3s[:, 0:1024], op=ALU.mult))
                    fence(v)
                    p.inc("s_ev", v.reduce_sum(stats[:, 1:2], p3s[:, 3072:4096], axis=AX.X))
                    p.wait(v, "s_ev")
                    p.inc("s_ev", v.tensor_scalar(stats[:, 1:2], stats[:, 1:2],
                                                  1.0 / C, 1e-5, op0=ALU.mult, op1=ALU.add))

                @blk.vector
                def _(v):
                    p.wait(v, "s_ev")
                    p.inc("s_ev", v.reciprocal(stats[:, 1:2], stats[:, 1:2]))

                @blk.scalar
                def _(sc):
                    p.wait(sc, "s_ev")
                    p.inc("s_sc", sc.activation(stats[:, 1:2], stats[:, 1:2], AF.Sqrt))

                @blk.vector
                def _(v):
                    p.wait(v, "s_sc")
                    p.wait(v, "s_sp")
                    p.inc("s_ev", v.tensor_scalar(p3s[:, 0:1024], p3s[:, 0:1024],
                                                  stats[:, 1:2], None, op0=ALU.mult))
                    p.wait(v, "s_ev")
                    p.inc("s_ev", v.tensor_tensor(p3s[:, 1024:2048], xrb[:, :],
                                                  p3s[:, 0:1024], op=ALU.add))

            if True:
                blk = BLK
                @blk.vector
                def _(v):
                    p.wait(v, "s_ev")
                    x2 = p3s[:, 1024:2048]
                    p.inc("s_ev", v.reduce_sum(stats[:, 2:3], x2, axis=AX.X))
                    p.wait(v, "s_ev")
                    p.inc("s_ev", v.tensor_scalar(stats[:, 2:3], stats[:, 2:3],
                                                  1.0 / C, None, op0=ALU.mult))
                    p.wait(v, "s_ev")
                    p.inc("s_ev", v.tensor_scalar(p3s[:, 2048:3072], x2, stats[:, 2:3],
                                                  None, op0=ALU.subtract))

                @blk.vector
                def _(v):
                    p.wait(v, "s_ev")
                    p.inc("s_ev", v.tensor_tensor(p3s[:, 3072:4096], p3s[:, 2048:3072],
                                                  p3s[:, 2048:3072], op=ALU.mult))
                    fence(v)
                    p.inc("s_ev", v.reduce_sum(stats[:, 3:4], p3s[:, 3072:4096], axis=AX.X))
                    p.wait(v, "s_ev")
                    p.inc("s_ev", v.tensor_scalar(stats[:, 3:4], stats[:, 3:4],
                                                  1.0 / C, 1e-5, op0=ALU.mult, op1=ALU.add))

                @blk.vector
                def _(v):
                    p.wait(v, "s_ev")
                    p.inc("s_ev", v.reciprocal(stats[:, 3:4], stats[:, 3:4]))

                @blk.scalar
                def _(sc):
                    p.wait(sc, "s_ev")
                    p.inc("s_sc", sc.activation(stats[:, 3:4], stats[:, 3:4], AF.Sqrt))

                @blk.vector
                def _(v):
                    p.wait(v, "s_sc")
                    z = p3s[:, 2048:3072]
                    p.inc("s_ev", v.tensor_scalar(z, z, stats[:, 3:4], None, op0=ALU.mult))
                    p.wait(v, "s_ev")
                    p.inc("s_ev", v.tensor_tensor(z, z, lngb[:, 0:C], op=ALU.mult))
                    p.wait(v, "s_ev")
                    p.inc("s_ev", v.tensor_tensor(z, z, lngb[:, C:2 * C], op=ALU.add))
                @blk.vector
                def _(v, cc=cc):
                    p.wait(v, "s_ev")
                    p.inc("s_ev", v.tensor_scalar(p3s[:, 3072:4096], p3s[:, 2048:3072],
                                                  -1.0, None, op0=ALU.mult))
                    fence(v)
                    p.inc("s_ev", v.tensor_tensor(p3s[:, 3072:4096], p3s[:, 3072:4096],
                                                  p3s[:, 2048:3072], op=ALU.max))
                    if cc < 4:
                        fence(v)
                        p.inc("s_ev", v.tensor_copy(habsA[:, cc * C:(cc + 1) * C],
                                                    p3s[:, 3072:4096]))
                    p.inc("s_sc", v.engine_nop())

                @blk.sync
                def _(sy, cc=cc):
                    p.wait(sy, "s_ev")
                    p.inc("s_sp", sy.dma_start(out=x2_sp[cc * 128:(cc + 1) * 128, :],
                                               in_=p3s[:, 1024:2048]), 16)
                    p.inc("s_sp", sy.dma_start(out=h1_sp[cc * 128:(cc + 1) * 128, :],
                                               in_=p3s[:, 2048:3072]), 16)
                h1sp_sn[cc] = p.c["s_sp"]
        if "d_x2c" in dbg_d:
            if True:
                blk = BLK
                @blk.gpsimd
                def _(g):
                    p.wait(g, "s_ev")
                    p.wait(g, "s_sp")
                    p.inc("s_gp", g.dma_start(out=dbg_d["d_x2c"][:, :],
                                              in_=p3s[:, 1024:2048]), 16)
                    p.inc("s_gp", g.dma_start(out=dbg_d["d_h1c"][:, :],
                                              in_=p3s[:, 2048:3072]), 16)
                    p.inc("s_gp", g.dma_start(out=dbg_d["d_n0"][:, :], in_=n_sp[8]), 16)
        if True:
            blk = BLK
            @blk.vector
            def _(v):
                p.wait(v, "s_sp")
                p.wait(v, "s_gp")
                p.inc("s_ev", v.engine_nop())
        LS3.close()
        LSwo.close()




    if STAGE < 4:
        LSwo.close()
    if STAGE < 5:
        LS3h.close()
        LSattc.close()
    if STAGE >= 5:
        LS4s = ExitStack()
        sb4s = lambda name, shape, dt: LS4s.enter_context(nc.sbuf_tensor(name, shape, dt))
        witmp = sb4s("witmp", [128, HID], I32)
        wacc = sb4s("wacc", [128, 40], F32)
        qtr = sb4s("qtr", [128, 4 * C], F32)
        cand = sb4s("cand", [128, 128], F32)
        negc = sb4s("negc", [128, 128], F32)
        ttmp = sb4s("ttmp", [128, 128], F32)
        LS4 = ExitStack()
        sb4 = lambda name, shape, dt: LS4.enter_context(nc.sbuf_tensor(name, shape, dt))
        t1_sb = sb4("t1_sb", [128, 8 * HID], BF16)
        gsA = sb4("gsA", [128, 16384], BF16)
        gt32 = sb4("gt32", [128, 2 * 512], F32)
        pstg = sb4("pstg", [128, 64], F32)
        n1T = sb4("n1T", [128, 4096], BF16)
        n1stage = sb4("n1stage", [128, C], BF16)

        hbuf = sb4("hbuf", [128, 2 * C], F32)
        pool1 = sb4("pool1", [128, 2048], F32)
        QS = 20

        def v_ss(v, dst, a, s1, s2, o0, o1=None):
            if o1 is None:
                p.inc("s_ev", v.tensor_scalar(dst, a, s1, None, op0=o0))
            else:
                p.inc("s_ev", v.tensor_scalar(dst, a, s1, s2, op0=o0, op1=o1))
            fence(v)

        def v_tt(v, dst, a, b, op):
            p.inc("s_ev", v.tensor_tensor(dst, a, b, op=op))
            fence(v)

        def bcast_row_to_cols(row_ap, col_ap, n):
            if True:
                blk = BLK
                @blk.tensor
                def _(t):
                    p.wait(t, "s_ev")
                    p.wait(t, "s_sc")
                    p.inc("s_mm", t.matmul(ps[5][:, 0:n], ones32[0:1, :], row_ap,
                                           start=True, stop=True))

                @blk.vector
                def _(v):
                    p.wait(v, "s_mm")
                    p.inc("s_ev", v.tensor_copy(col_ap, ps[5][:, 0:n]))

        def cross_sum_cols(col_ap, out_row, n):
            if True:
                blk = BLK
                @blk.tensor
                def _(t):
                    p.wait(t, "s_ev")
                    p.inc("s_mm", t.matmul(ps[5][0:1, 0:n], ones32[:, 0:1], col_ap,
                                           start=True, stop=True))

                @blk.vector
                def _(v):
                    p.wait(v, "s_mm")
                    p.inc("s_ev", v.tensor_copy(out_row, ps[5][0:1, 0:n]))

        def cross_max_col(col_ap, out_11):
            if True:
                blk = BLK
                @blk.vector
                def _(v):
                    p.wait(v, "s_ev")
                    p.inc("s_ev", v.memset(ttmp[:, :], 0.0))
                    fence(v)
                    p.inc("s_ev", v.tensor_copy(ttmp[:, 0:1], col_ap))

                @blk.tensor
                def _(t):
                    p.wait(t, "s_ev")
                    p.inc("s_mm", t.transpose(ps[5][:, 0:128], ttmp[:, :], ident32[:, :]))

                @blk.vector
                def _(v):
                    p.wait(v, "s_mm")
                    p.inc("s_ev", v.reduce_max(out_11, ps[5][0:1, 0:128], axis=AX.X))

        def count_gt(segs, thr_col, out_11):
            pieces = []
            for ap, fd in segs:
                off = 0
                while off < fd:
                    w = min(4096, fd - off)
                    pieces.append((ap[:, off:off + w] if (off or w < fd) else ap, w))
                    off += w
            if True:
                blk = BLK
                @blk.vector
                def _(v):
                    p.wait(v, "s_ev")
                    for i, (ap, w) in enumerate(pieces):
                        if i:
                            fence(v)
                        p.inc("s_ev", v.tensor_scalar(qtr[:, 0:w], ap, thr_col, 0.0,
                                                      op0=ALU.is_gt, op1=ALU.add,
                                                      accum_out=wacc[:, 32 + i:33 + i]))

                @blk.tensor
                def _(t):
                    p.wait(t, "s_ev")
                    p.inc("s_mm", t.matmul(ps[5][0:1, 0:len(pieces)], ones32[:, 0:1],
                                           wacc[:, 32:32 + len(pieces)],
                                           start=True, stop=True))

                @blk.vector
                def _(v):
                    p.wait(v, "s_mm")
                    p.inc("s_ev", v.reduce_sum(out_11, ps[5][0:1, 0:len(pieces)],
                                               axis=AX.X))

        def emit_bisect(segs, rounds, lo, hi, target, out_ap):
            for r in range(rounds):
                if True:
                    blk = BLK
                    @blk.vector
                    def _(v):
                        p.wait(v, "s_ev")
                        v_tt(v, qrow[0:1, QS:QS + 1], lo, hi, ALU.add)
                        v_ss(v, qrow[0:1, QS:QS + 1], qrow[0:1, QS:QS + 1], 0.5, None,
                             ALU.mult)
                bcast_row_to_cols(qrow[0:1, QS:QS + 1], qcol[:, 0:1], 1)
                count_gt(segs, qcol[:, 0:1], qrow[0:1, QS + 1:QS + 2])
                if True:
                    blk = BLK
                    @blk.vector
                    def _(v):
                        p.wait(v, "s_ev")
                        v_tt(v, qrow[0:1, QS + 2:QS + 3], qrow[0:1, QS + 1:QS + 2],
                             target, ALU.is_gt)
                        v_tt(v, qrow[0:1, QS + 3:QS + 4], qrow[0:1, QS:QS + 1], lo,
                             ALU.subtract)
                        v_tt(v, qrow[0:1, QS + 3:QS + 4], qrow[0:1, QS + 3:QS + 4],
                             qrow[0:1, QS + 2:QS + 3], ALU.mult)
                        v_tt(v, lo, lo, qrow[0:1, QS + 3:QS + 4], ALU.add)
                        v_tt(v, qrow[0:1, QS + 4:QS + 5], qrow[0:1, QS:QS + 1], hi,
                             ALU.subtract)
                        v_ss(v, qrow[0:1, QS + 5:QS + 6], qrow[0:1, QS + 2:QS + 3],
                             -1.0, 1.0, ALU.mult, ALU.add)
                        v_tt(v, qrow[0:1, QS + 4:QS + 5], qrow[0:1, QS + 4:QS + 5],
                             qrow[0:1, QS + 5:QS + 6], ALU.mult)
                        v_tt(v, hi, hi, qrow[0:1, QS + 4:QS + 5], ALU.add)
            if out_ap is not None:
                if True:
                    blk = BLK
                    @blk.vector
                    def _(v):
                        p.wait(v, "s_ev")
                        v_tt(v, out_ap, lo, hi, ALU.add)
                        v_ss(v, out_ap, out_ap, 0.5, None, ALU.mult)

        def allreduce_row(pair, width, scale=None, cols=8):
            cin, cout = pair
            if True:
                blk = BLK
                @blk.sync
                def _(sy):
                    p.wait(sy, "s_ev")
                    p.inc("s_sp", sy.dma_start(out=cin[0:1, 0:cols],
                                               in_=qrow[0:1, 40:40 + cols]), 16)

                @blk.gpsimd
                def _(g):
                    p.wait(g, "s_sp")
                    p.inc("s_cc", g.collective_compute(
                        "AllReduce", ALU.add, replica_groups=RG_ALL,
                        ins=[cin[:, :]], outs=[cout[:, :]]), 1)

                @blk.sync
                def _(sy):
                    p.wait(sy, "s_cc")
                    p.inc("s_cr", sy.dma_start(out=qrow[0:1, 40:40 + width],
                                               in_=cout[0:1, 0:width]), 16)

                @blk.vector
                def _(v):
                    p.wait(v, "s_cr")
                    if scale is not None:
                        v_ss(v, qrow[0:1, 40:40 + width], qrow[0:1, 40:40 + width],
                             scale, None, ALU.mult)
                    else:
                        p.inc("s_ev", v.engine_nop())

        def emit_quantile_tail(qidx, pool_ap, pool_fd, gather_pair, ka_slot,
                               frac_slot, s_out_col):
            if True:
                blk = BLK
                @blk.vector
                def _(v):
                    p.wait(v, "s_ev")
                    p.inc("s_ev", v.tensor_copy(qrow[0:1, 40:46], qrow[0:1, 72:78]))
            allreduce_row(cc_ct[qidx], 6)
            if True:
                blk = BLK
                @blk.vector
                def _(v):
                    p.wait(v, "s_ev")
                    p.inc("s_ev", v.tensor_copy(qrow[0:1, 80:86], qrow[0:1, 40:46]))
                    fence(v)
                    p.inc("s_ev", v.tensor_copy(qrow[0:1, 35:36], qrow[0:1, 85:86]))
                    fence(v)
                    v_tt(v, qrow[0:1, 36:37], qrow[0:1, ka_slot:ka_slot + 1],
                         qrow[0:1, 35:36], ALU.subtract)
                    v_ss(v, qrow[0:1, 20:21], qrow[0:1, ka_slot:ka_slot + 1],
                         1.0, None, ALU.add)
                    p.inc("s_ev", v.tensor_scalar(qrow[0:1, 96:102], qrow[0:1, 80:86],
                                                  qrow[0:1, 20:21], None, op0=ALU.is_ge))
                    fence(v)
                    v_tt(v, qrow[0:1, 104:110], qrow[0:1, 64:70], qrow[0:1, 96:102],
                         ALU.mult)
                    p.inc("s_ev", v.tensor_reduce(qrow[0:1, 30:31], qrow[0:1, 104:110],
                                                  axis=AX.X, op=ALU.max))
                    fence(v)
                    v_ss(v, qrow[0:1, 20:21], qrow[0:1, ka_slot:ka_slot + 1],
                         -1.0, None, ALU.add)
                    p.inc("s_ev", v.tensor_scalar(qrow[0:1, 96:102], qrow[0:1, 80:86],
                                                  qrow[0:1, 20:21], None, op0=ALU.is_le))
                    fence(v)
                    v_tt(v, qrow[0:1, 104:110], qrow[0:1, 64:70], qrow[0:1, 96:102],
                         ALU.mult)
                    v_ss(v, qrow[0:1, 96:102], qrow[0:1, 96:102], -1e9, 1e9,
                         ALU.mult, ALU.add)
                    v_tt(v, qrow[0:1, 104:110], qrow[0:1, 104:110], qrow[0:1, 96:102],
                         ALU.add)
                    p.inc("s_ev", v.tensor_reduce(qrow[0:1, 31:32], qrow[0:1, 104:110],
                                                  axis=AX.X, op=ALU.min))
                    fence(v)
            for er in range(3):
                if True:
                    blk = BLK
                    @blk.vector
                    def _(v):
                        p.wait(v, "s_ev")
                        v_tt(v, qrow[0:1, 20:21], qrow[0:1, 31:32], qrow[0:1, 30:31],
                             ALU.subtract)
                        for j in range(8):
                            v_ss(v, qrow[0:1, 88 + j:89 + j], qrow[0:1, 20:21],
                                 (j + 1) / 9.0, None, ALU.mult)
                            v_tt(v, qrow[0:1, 88 + j:89 + j], qrow[0:1, 88 + j:89 + j],
                                 qrow[0:1, 30:31], ALU.add)
                for j in range(8):
                    bcast_row_to_cols(qrow[0:1, 88 + j:89 + j], qcol[:, 1:2], 1)
                    count_gt([(pool_ap, pool_fd)], qcol[:, 1:2], qrow[0:1, 40 + j:41 + j])
                allreduce_row(cc_e[qidx][er], 8, cols=16)
                if True:
                    blk = BLK
                    @blk.vector
                    def _(v):
                        p.wait(v, "s_ev")
                        p.inc("s_ev", v.tensor_copy(qrow[0:1, 96:104], qrow[0:1, 40:48]))
                        fence(v)
                        v_ss(v, qrow[0:1, 20:21], qrow[0:1, 36:37], 1.0, None, ALU.add)
                        p.inc("s_ev", v.tensor_scalar(qrow[0:1, 104:112],
                                                      qrow[0:1, 96:104],
                                                      qrow[0:1, 20:21], None,
                                                      op0=ALU.is_ge))
                        fence(v)
                        v_tt(v, qrow[0:1, 112:120], qrow[0:1, 88:96], qrow[0:1, 104:112],
                             ALU.mult)
                        p.inc("s_ev", v.tensor_reduce(qrow[0:1, 21:22],
                                                      qrow[0:1, 112:120], axis=AX.X,
                                                      op=ALU.max))
                        fence(v)
                        v_tt(v, qrow[0:1, 30:31], qrow[0:1, 30:31], qrow[0:1, 21:22],
                             ALU.max)
                        v_ss(v, qrow[0:1, 20:21], qrow[0:1, 36:37], -65.0, None,
                             ALU.add)
                        p.inc("s_ev", v.tensor_scalar(qrow[0:1, 104:112],
                                                      qrow[0:1, 96:104],
                                                      qrow[0:1, 20:21], None,
                                                      op0=ALU.is_le))
                        fence(v)
                        v_tt(v, qrow[0:1, 112:120], qrow[0:1, 88:96], qrow[0:1, 104:112],
                             ALU.mult)
                        v_ss(v, qrow[0:1, 104:112], qrow[0:1, 104:112], -1e9, 1e9,
                             ALU.mult, ALU.add)
                        v_tt(v, qrow[0:1, 112:120], qrow[0:1, 112:120],
                             qrow[0:1, 104:112], ALU.add)
                        p.inc("s_ev", v.tensor_reduce(qrow[0:1, 21:22],
                                                      qrow[0:1, 112:120], axis=AX.X,
                                                      op=ALU.min))
                        fence(v)
                        v_tt(v, qrow[0:1, 31:32], qrow[0:1, 31:32], qrow[0:1, 21:22],
                             ALU.min)
            bcast_row_to_cols(qrow[0:1, 31:32], qcol[:, 2:3], 1)
            count_gt([(pool_ap, pool_fd)], qcol[:, 2:3], qrow[0:1, 40:41])
            allreduce_row(cc_e[qidx][3], 8, cols=16)
            if True:
                blk = BLK
                @blk.vector
                def _(v):
                    p.wait(v, "s_ev")
                    p.inc("s_ev", v.tensor_copy(qrow[0:1, 37:38], qrow[0:1, 40:41]))
                    fence(v)
                    v_tt(v, qrow[0:1, 38:39], qrow[0:1, 36:37], qrow[0:1, 37:38],
                         ALU.subtract)
            npc = max(1, pool_fd // 4096)
            bpp = 16 // npc
            if True:
                blk = BLK
                @blk.vector
                def _(v):
                    p.wait(v, "s_ev")
                    for h in range(npc):
                        w = pool_fd // npc
                        p.inc("s_ev", v.scalar_tensor_tensor(
                            qtr[:, 0:w], pool_ap[:, h * w:(h + 1) * w],
                            qcol[:, 2:3], pool_ap[:, h * w:(h + 1) * w],
                            op0=ALU.is_le, op1=ALU.mult))
                        fence(v)
                        p.inc("s_ev", v.tensor_reduce(
                            wacc[:, h * bpp:(h + 1) * bpp],
                            qtr[:, 0:w].rearrange("p (b f) -> p b f", b=bpp),
                            axis=AX.X, op=ALU.max))
                        fence(v)

                @blk.sync
                def _(sy):
                    p.wait(sy, "s_ev")
                    p.inc("s_sp", sy.dma_start(out=gather_pair[0][:, :],
                                               in_=wacc[:, 0:16]), 16)

                @blk.gpsimd
                def _(g):
                    p.wait(g, "s_sp")
                    p.inc("s_cc", g.collective_compute(
                        "AllGather", ALU.bypass, replica_groups=RG_ALL,
                        ins=[gather_pair[0][:, :]], outs=[gather_pair[1][:, :]]), 1)

                @blk.sync
                def _(sy):
                    p.wait(sy, "s_cc")
                    p.inc("s_cr", sy.dma_start(
                        out=cand.rearrange("p (r c) -> p r c", r=8),
                        in_=gather_pair[1].rearrange("(r p) c -> p r c", p=128)), 16)

                @blk.vector
                def _(v):
                    p.wait(v, "s_cr")
                    p.inc("s_ev", v.tensor_scalar(negc[:, :], cand[:, :], -1.0, None,
                                                  op0=ALU.mult))
            emit_bisect([(cand[:, :], 128)], 20, qrow[0:1, 30:31], qrow[0:1, 31:32],
                        qrow[0:1, 38:39], None)
            bcast_row_to_cols(qrow[0:1, 31:32], qcol[:, 3:4], 1)
            if True:
                blk = BLK
                @blk.vector
                def _(v):
                    p.wait(v, "s_ev")
                    p.inc("s_ev", v.scalar_tensor_tensor(
                        qtr[:, 0:128], cand[:, :], qcol[:, 3:4], cand[:, :],
                        op0=ALU.is_le, op1=ALU.mult))
                    fence(v)
                    p.inc("s_ev", v.tensor_reduce(wacc[:, 20:21], qtr[:, 0:128],
                                                  axis=AX.X, op=ALU.max))
            cross_max_col(wacc[:, 20:21], qrow[0:1, 26:27])
            if True:
                blk = BLK
                @blk.vector
                def _(v):
                    p.wait(v, "s_ev")
                    p.inc("s_ev", v.scalar_tensor_tensor(
                        qtr[:, 0:128], cand[:, :], qcol[:, 3:4], negc[:, :],
                        op0=ALU.is_gt, op1=ALU.mult))
                    fence(v)
                    p.inc("s_ev", v.tensor_reduce(wacc[:, 21:22], qtr[:, 0:128],
                                                  axis=AX.X, op=ALU.max))
            cross_max_col(wacc[:, 21:22], qrow[0:1, 27:28])
            if True:
                blk = BLK
                @blk.vector
                def _(v):
                    p.wait(v, "s_ev")
                    v_ss(v, qrow[0:1, 27:28], qrow[0:1, 27:28], -1.0, None, ALU.mult)
                    v_tt(v, qrow[0:1, 27:28], qrow[0:1, 27:28], qrow[0:1, 26:27],
                         ALU.max)
                    v_tt(v, qrow[0:1, 28:29], qrow[0:1, 27:28], qrow[0:1, 26:27],
                         ALU.subtract)
                    p.inc("s_ev", v.tensor_scalar(
                        qrow[0:1, 28:29], qrow[0:1, 28:29],
                        qrow[0:1, frac_slot:frac_slot + 1], None, op0=ALU.mult))
                    fence(v)
                    v_tt(v, qrow[0:1, 33:34], qrow[0:1, 26:27], qrow[0:1, 28:29],
                         ALU.add)
                    v_ss(v, qrow[0:1, 33:34], qrow[0:1, 33:34], 1e-5, None, ALU.max)
                    p.inc("s_ev", v.reciprocal(qrow[0:1, 34:35], qrow[0:1, 33:34]))
                    fence(v)
                    v_ss(v, qrow[0:1, 34:35], qrow[0:1, 34:35], 127.0, None, ALU.mult)
            bcast_row_to_cols(qrow[0:1, 34:35], s_out_col, 1)

        for kk in range(8):
            if True:
                blk = BLK
                @blk.sync
                def _(sy, kk=kk):
                    p.wait(sy, "s_ev")
                    p.inc("s_ld", sy.dma_start(
                        out=hbuf[:, 0:C] if False else qtr[:, 0:4096],
                        in_=w1t_d[kk * 128:(kk + 1) * 128, :]), 16)

                @blk.vector
                def _(v, kk=kk):
                    p.wait(v, "s_ld")
                    p.inc("s_ev", v.reduce_sum(wacc[:, kk:kk + 1], qtr[:, 0:4096],
                                               axis=AX.X, apply_absolute_value=True))
        cross_sum_cols(wacc[:, 0:8], qrow[0:1, 40:48], 8)
        if True:
            blk = BLK
            @blk.vector
            def _(v):
                p.wait(v, "s_ev")
                p.inc("s_ev", v.reduce_sum(qrow[0:1, 39:40], qrow[0:1, 40:48], axis=AX.X))
                fence(v)
                v_ss(v, qrow[0:1, 10:11], qrow[0:1, 39:40], 1.0 / (C * HID), 1e-5,
                     ALU.mult, ALU.add)
                p.inc("s_ev", v.reciprocal(qrow[0:1, 11:12], qrow[0:1, 10:11]))
        bcast_row_to_cols(qrow[0:1, 11:12], qcol[:, 4:5], 1)
        for kk in range(8):
            if True:
                blk = BLK
                @blk.sync
                def _(sy, kk=kk):
                    p.wait(sy, "s_ev")
                    p.inc("s_ld", sy.dma_start(out=qtr[:, 0:4096],
                                               in_=w1t_d[kk * 128:(kk + 1) * 128, :]), 16)

                @blk.vector
                def _(v, kk=kk):
                    p.wait(v, "s_ld")
                    p.inc("s_ev", v.tensor_scalar(qtr[:, 0:4096], qtr[:, 0:4096],
                                                  qcol[:, 4:5], 1.0, op0=ALU.mult,
                                                  op1=ALU.min))
                    fence(v)
                    p.inc("s_ev", v.tensor_scalar(qtr[:, 0:4096], qtr[:, 0:4096],
                                                  -1.0, None, op0=ALU.max))
                    fence(v)
                    p.inc("s_ev", v.tensor_scalar(witmp[:, :], qtr[:, 0:4096],
                                                  0.0, None, op0=ALU.add))
                    fence(v)
                    p.inc("s_ev", v.tensor_scalar(t1_sb[:, kk * HID:(kk + 1) * HID],
                                                  witmp[:, :], 0.0, None, op0=ALU.add))
        T1_DONE = p.c["s_ev"]

        for half in range(8):
            if True:
                blk = BLK
                @blk.sync
                def _(sy, half=half):
                    p.wait(sy, "s_ev")
                    p.inc("s_ld", sy.dma_start(
                        out=qtr[:, 0:4096].rearrange("p (a c) -> p a c", a=4),
                        in_=w2t_d[half * 512:(half + 1) * 512, :].rearrange(
                            "(a p) c -> p a c", p=128)), 16)

                @blk.vector
                def _(v, half=half):
                    p.wait(v, "s_ld")
                    p.inc("s_ev", v.reduce_sum(wacc[:, 16 + half:17 + half],
                                               qtr[:, 0:4096], axis=AX.X,
                                               apply_absolute_value=True))
        cross_sum_cols(wacc[:, 16:24], qrow[0:1, 40:48], 8)
        if True:
            blk = BLK
            @blk.vector
            def _(v):
                p.wait(v, "s_ev")
                p.inc("s_ev", v.reduce_sum(qrow[0:1, 44:45], qrow[0:1, 40:48], axis=AX.X))
                fence(v)
                v_ss(v, qrow[0:1, 12:13], qrow[0:1, 44:45], 1.0 / (C * HID), 1e-5,
                     ALU.mult, ALU.add)
                p.inc("s_ev", v.reciprocal(qrow[0:1, 13:14], qrow[0:1, 12:13]))
        bcast_row_to_cols(qrow[0:1, 13:14], qcol[:, 14:15], 1)

        if True:
            blk = BLK
            @blk.vector
            def _(v):
                p.wait(v, "s_ev")
                p.inc("s_ev", v.reduce_max(wacc[:, 24:25], habsA[:, :], axis=AX.X))
        cross_max_col(wacc[:, 24:25], qrow[0:1, 31:32])
        if True:
            blk = BLK
            @blk.vector
            def _(v):
                p.wait(v, "s_ev")
                p.inc("s_ev", v.memset(qrow[0:1, 30:31], 0.0))
                fence(v)
                p.inc("s_ev", v.memset(qrow[0:1, QS + 9:QS + 10], 0.005 * 4 * 128 * C))
        emit_bisect([(habsA[:, :], 4096)], 14, qrow[0:1, 30:31], qrow[0:1, 31:32],
                    qrow[0:1, QS + 9:QS + 10], qrow[0:1, 32:33])
        if True:
            blk = BLK
            @blk.vector
            def _(v):
                p.wait(v, "s_ev")
                p.inc("s_ev", v.tensor_copy(qrow[0:1, 40:41], qrow[0:1, 32:33]))
        allreduce_row(cc_th[0], 1, scale=1.0 / 8.0)
        if True:
            blk = BLK
            @blk.vector
            def _(v):
                p.wait(v, "s_ev")
                p.inc("s_ev", v.tensor_copy(qrow[0:1, 32:33], qrow[0:1, 40:41]))

        DLT = 0.012
        if True:
            blk = BLK
            @blk.vector
            def _(v):
                p.wait(v, "s_ev")
                for j in range(6):
                    f = -DLT + (2 * DLT) * j / 5.0
                    v_ss(v, qrow[0:1, 40 + j:41 + j], qrow[0:1, 32:33], 1.0 + f,
                         None, ALU.mult)
                p.inc("s_ev", v.memset(wacc[:, 8:14], 0.0))
        bcast_row_to_cols(qrow[0:1, 40:46], qcol[:, 8:14], 6)
        for cc in range(NCH):
            cur = cc % 2
            if True:
                blk = BLK
                @blk.sync
                def _(sy, cc=cc, cur=cur):
                    p.wait(sy, "s_ev")
                    p.wait(sy, "s_sp", h1sp_sn[cc])
                    p.inc("s_ld", sy.dma_start(out=hbuf[:, cur * C:(cur + 1) * C],
                                               in_=h1_sp[cc * 128:(cc + 1) * 128, :]), 16)

                @blk.vector
                def _(v, cc=cc, cur=cur):
                    p.wait(v, "s_ld")
                    hb0 = hbuf[:, cur * C:(cur + 1) * C]
                    hb = qtr[:, 0:C]
                    p.inc("s_ev", v.tensor_scalar(qtr[:, 1024:2048], hb0, -1.0, None,
                                                  op0=ALU.mult))
                    fence(v)
                    p.inc("s_ev", v.tensor_tensor(hb, hb0, qtr[:, 1024:2048],
                                                  op=ALU.max))
                    fence(v)
                    for j in range(6):
                        sco = 1024 + (j % 3) * 1024
                        if j == 3:
                            fence(v)
                        p.inc("s_ev", v.tensor_scalar(qtr[:, sco:sco + 1024], hb,
                                                      qcol[:, 8 + j:9 + j], 0.0,
                                                      op0=ALU.is_gt, op1=ALU.add,
                                                      accum_out=wacc[:, 30 + j:31 + j]))
                    fence(v)
                    p.inc("s_ev", v.tensor_tensor(wacc[:, 8:14], wacc[:, 8:14],
                                                  wacc[:, 30:36], op=ALU.add))
                    fence(v)
                    p.inc("s_ev", v.scalar_tensor_tensor(qtr[:, 0:C], hb,
                                                         qcol[:, 13:14], hb,
                                                         op0=ALU.is_le, op1=ALU.mult))
                    fence(v)
                    p.inc("s_ev", v.tensor_tensor(qtr[:, 0:512], qtr[:, 0:512],
                                                  qtr[:, 512:1024], op=ALU.max))
                    fence(v)
                    p.inc("s_ev", v.tensor_tensor(qtr[:, 0:256], qtr[:, 0:256],
                                                  qtr[:, 256:512], op=ALU.max))
                    fence(v)
                    p.inc("s_ev", v.tensor_tensor(qtr[:, 0:128], qtr[:, 0:128],
                                                  qtr[:, 128:256], op=ALU.max))
                    fence(v)
                    p.inc("s_ev", v.tensor_tensor(pool1[:, cc * 64:(cc + 1) * 64],
                                                  qtr[:, 0:64], qtr[:, 64:128],
                                                  op=ALU.max))
                    fence(v)
        cross_sum_cols(wacc[:, 8:14], qrow[0:1, 72:78], 6)
        if True:
            blk = BLK
            @blk.vector
            def _(v):
                p.wait(v, "s_ev")
                v_ss(v, qrow[0:1, 14:15], qrow[0:1, 0:1], -1.0, None, ALU.add)
                p.inc("s_ev", v.tensor_copy(qrow[0:1, 64:70], qrow[0:1, 40:46]))
        emit_quantile_tail(0, pool1[:, :], 2048, cc_gc[0], 14, 1, qcol[:, 5:6])
        if True:
            blk = BLK
            @blk.vector
            def _(v):
                p.wait(v, "s_ev")
                v_tt(v, qrow[0:1, 16:17], qrow[0:1, 10:11], qrow[0:1, 33:34], ALU.mult)
                v_ss(v, qrow[0:1, 16:17], qrow[0:1, 16:17], 1.0 / 127.0, None, ALU.mult)
        bcast_row_to_cols(qrow[0:1, 16:17], qcol[:, 6:7], 1)
        if "d_qrow1" in dbg_d:
            if True:
                blk = BLK
                @blk.gpsimd
                def _(g):
                    p.wait(g, "s_ev")
                    p.inc("s_gp", g.dma_start(out=dbg_d["d_qrow1"][:, :],
                                              in_=qrow[0:1, :]), 16)
                    p.inc("s_gp", g.dma_start(out=dbg_d["d_qcol"][:, :],
                                              in_=qcol[:, :]), 16)
                    p.inc("s_gp", g.dma_start(out=dbg_d["d_pool1"][:, :],
                                              in_=pool1[:, :]), 16)
                    p.inc("s_gp", g.dma_start(out=dbg_d["d_habsA"][:, :],
                                              in_=habsA[:, :]), 16)
        Q1_DONE = p.c["s_ev"]

        def q2_tile_counts(v, gb, ti):
            for j in range(6):
                sco = j * 512
                p.inc("s_ev", v.tensor_scalar(qtr[:, sco:sco + 512], gb,
                                              qcol[:, 8 + j:9 + j], 0.0,
                                              op0=ALU.is_gt, op1=ALU.add,
                                              accum_out=wacc[:, 30 + j:31 + j]))
            fence(v)
            p.inc("s_ev", v.tensor_tensor(wacc[:, 8:14], wacc[:, 8:14],
                                          wacc[:, 30:36], op=ALU.add))
            fence(v)
            p.inc("s_ev", v.scalar_tensor_tensor(qtr[:, 0:512], gb,
                                                 qcol[:, 13:14], gb,
                                                 op0=ALU.is_le, op1=ALU.mult))
            fence(v)
            p.inc("s_ev", v.tensor_tensor(qtr[:, 0:256], qtr[:, 0:256],
                                          qtr[:, 256:512], op=ALU.max))
            fence(v)
            p.inc("s_ev", v.tensor_tensor(qtr[:, 0:128], qtr[:, 0:128],
                                          qtr[:, 128:256], op=ALU.max))
            fence(v)
            p.inc("s_ev", v.tensor_tensor(qtr[:, 0:64], qtr[:, 0:64],
                                          qtr[:, 64:128], op=ALU.max))
            fence(v)
            slot = ti % 2
            if ti >= 2:
                p.wait(v, "s_ps%d" % slot, pspill_sn[ti - 2])
            p.inc("s_ev", v.tensor_tensor(pstg[:, slot * 32:(slot + 1) * 32],
                                          qtr[:, 0:32], qtr[:, 32:64],
                                          op=ALU.max))
            ptree_sn[ti] = p.c["s_ev"]

        def q2_tile_spill(ti):
            blk = BLK
            slot = ti % 2

            @blk.sync
            def _(sy, ti=ti, slot=slot):
                p.wait(sy, "s_ev", ptree_sn[ti])
                p.inc("s_ps%d" % slot, sy.dma_start(
                    out=pool2_d[:, ti * 32:(ti + 1) * 32],
                    in_=pstg[:, slot * 32:(slot + 1) * 32]), 16)
            pspill_sn[ti] = p.c["s_ps%d" % slot]

        scale_snap = {}
        pspill_sn = {}
        ptree_sn = {}
        gel_snap = {}
        gspill_sn = {}
        cnt_snap = {}
        for s in range(NSLAB):
            for ccq in range(4):
                cc = s * 4 + ccq
                cur = cc % 2
                if True:
                    blk = BLK
                    @blk.sync
                    def _(sy, cc=cc, cur=cur):
                        p.wait(sy, "s_ev")
                        p.wait(sy, "s_sp", h1sp_sn[NCH - 1])
                        p.inc("s_ld", sy.dma_start(
                            out=hbuf[:, cur * C:(cur + 1) * C],
                            in_=h1_sp[cc * 128:(cc + 1) * 128, :]), 16)

                    @blk.vector
                    def _(v, ccq=ccq, cur=cur):
                        p.wait(v, "s_ld")
                        p.wait(v, "s_mm")
                        hb = hbuf[:, cur * C:(cur + 1) * C]
                        p.inc("s_ev", v.tensor_scalar(qtr[:, 0:C], hb,
                                                      qcol[:, 5:6], 127.0,
                                                      op0=ALU.mult, op1=ALU.min))
                        fence(v)
                        p.inc("s_ev", v.tensor_scalar(qtr[:, 0:C], qtr[:, 0:C],
                                                      -128.0, None, op0=ALU.max))
                        fence(v)
                        p.inc("s_ev", v.tensor_scalar(witmp[:, 0:C], qtr[:, 0:C],
                                                      0.0, None, op0=ALU.add))
                        fence(v)
                        p.inc("s_ev", v.tensor_scalar(n1stage[:, :], witmp[:, 0:C],
                                                      0.0, None, op0=ALU.add))
                for half in range(2):
                    if True:
                        blk = BLK
                        @blk.tensor
                        def _(t, half=half, ccq=ccq):
                            p.wait(t, "s_ev")
                            p.wait(t, "s_sc")
                            last = None
                            for j in range(4):
                                kk = half * 4 + j
                                last = t.transpose(
                                    pt[:, j * 256:j * 256 + 128],
                                    n1stage[:, kk * 128:(kk + 1) * 128],
                                    ident[:, :])
                            p.inc("s_mm", last)

                        @blk.vector
                        def _(v, half=half, ccq=ccq):
                            p.wait(v, "s_mm")
                            last = None
                            for j in range(4):
                                kk = half * 4 + j
                                last = v.tensor_copy(n1T[:, kk * SLAB + ccq * 128:
                                                   kk * SLAB + (ccq + 1) * 128],
                                               pt[:, j * 256:j * 256 + 128])
                            p.inc("s_sc", last)
            N1T_DONE = p.c["s_sc"]
            for mh in range(32):
                ti = s * 32 + mh
                slot = ti % 2
                bank = ps[slot]
                ssl = gt32[:, slot * 512:(slot + 1) * 512]
                gsl = qtr[:, 3072 + slot * 512:3072 + (slot + 1) * 512]
                if True:
                    blk = BLK
                    @blk.tensor
                    def _(t, mh=mh, bank=bank, s=s,
                          bsn=gel_snap.get(ti - 2)):
                        if bsn is None:
                            p.wait(t, "s_sc", N1T_DONE)
                            p.wait(t, "s_ev")
                        else:
                            p.wait(t, "s_sc", max(N1T_DONE, bsn))
                        last = None
                        for kk in range(8):
                            last = t.matmul(
                                bank[:, :],
                                t1_sb[:, kk * HID + mh * 128:kk * HID + (mh + 1) * 128],
                                n1T[:, kk * SLAB:(kk + 1) * SLAB],
                                start=(kk == 0), stop=(kk == 7))
                        p.inc("s_mm", last)
                    mm_sn = p.c["s_mm"]

                    @blk.scalar
                    def _(sc, mh=mh, s=s, bank=bank, gsl=gsl, mm_sn=mm_sn,
                          csn=cnt_snap.get(ti - 2),
                          dsn=gspill_sn.get(ti - 2), slot=slot):
                        p.wait(sc, "s_mm", mm_sn)
                        if dsn is not None:
                            p.wait(sc, "s_gs%d" % slot, dsn)
                        if csn is not None:
                            p.wait(sc, "s_ev", csn)
                        p.inc("s_sc", sc.activation(gsl, bank[:, :], AF.Gelu,
                                                    scale=qcol[:, 6:7]))
                        if s == 0:
                            p.wait(sc, "s_sc")
                            p.inc("s_sc", sc.activation(gsA[:, mh * 512:(mh + 1) * 512],
                                                        gsl, AF.Abs))
                    gel_snap[ti] = p.c["s_sc"]

                    if s > 0:
                        @blk.vector
                        def _(v, gsl=gsl, ti=ti, gel_sn=gel_snap[ti]):
                            p.wait(v, "s_sc", gel_sn)
                            q2_tile_counts(v, gsl, ti)
                        cnt_snap[ti] = p.c["s_ev"]
                        q2_tile_spill(ti)

                    @blk.sync
                    def _(sy, mh=mh, s=s, gsl=gsl, slot=slot):
                        p.wait(sy, "s_sc")
                        p.inc("s_gs%d" % slot, sy.dma_start(
                            out=g_sp[s, mh * 128:(mh + 1) * 128, :],
                            in_=gsl), 16)
                    gspill_sn[ti] = p.c["s_gs%d" % slot]

            if s == 0:
                if True:
                    blk = BLK
                    @blk.vector
                    def _(v):
                        p.wait(v, "s_ev")
                        p.wait(v, "s_sc")
                        p.inc("s_ev", v.reduce_max(wacc[:, 24:25], gsA[:, :], axis=AX.X))
                cross_max_col(wacc[:, 24:25], qrow[0:1, 31:32])
                if True:
                    blk = BLK
                    @blk.vector
                    def _(v):
                        p.wait(v, "s_ev")
                        p.inc("s_ev", v.memset(qrow[0:1, 30:31], 0.0))
                        fence(v)
                        p.inc("s_ev", v.memset(qrow[0:1, 29:30], 0.005 * 128 * 16384))
                emit_bisect([(gsA[:, k * 2048:(k + 1) * 2048], 2048)
                             for k in range(8)],
                            14, qrow[0:1, 30:31], qrow[0:1, 31:32], qrow[0:1, 29:30],
                            qrow[0:1, 32:33])
                if True:
                    blk = BLK
                    @blk.vector
                    def _(v):
                        p.wait(v, "s_ev")
                        p.inc("s_ev", v.tensor_copy(qrow[0:1, 40:41], qrow[0:1, 32:33]))
                allreduce_row(cc_th[1], 1, scale=1.0 / 8.0)
                if True:
                    blk = BLK
                    @blk.vector
                    def _(v):
                        p.wait(v, "s_ev")
                        p.inc("s_ev", v.tensor_copy(qrow[0:1, 32:33], qrow[0:1, 40:41]))
                        fence(v)
                        for j in range(6):
                            f = -0.012 + 0.024 * j / 5.0
                            v_ss(v, qrow[0:1, 40 + j:41 + j], qrow[0:1, 32:33], 1.0 + f,
                                 None, ALU.mult)
                        p.inc("s_ev", v.memset(wacc[:, 8:14], 0.0))
                bcast_row_to_cols(qrow[0:1, 40:46], qcol[:, 8:14], 6)
                s0cnt_sn = {}
                for mh0 in range(32):
                    sl4 = mh0 % 4
                    if True:
                        blk = BLK
                        @blk.sync
                        def _(sy, mh0=mh0, sl4=sl4):
                            if mh0 < 4:
                                p.wait(sy, "s_gs0")
                                p.wait(sy, "s_gs1")
                            else:
                                p.wait(sy, "s_ev", s0cnt_sn[mh0 - 4])
                            p.inc("s_ld", sy.dma_start(
                                out=pool1[:, sl4 * 512:(sl4 + 1) * 512],
                                in_=g_sp[0, mh0 * 128:(mh0 + 1) * 128, :]), 16)
                        ld_sn = p.c["s_ld"]

                        @blk.vector
                        def _(v, mh0=mh0, sl4=sl4, ld_sn=ld_sn):
                            p.wait(v, "s_ld", ld_sn)
                            if mh0 == 0:
                                fence(v)
                            q2_tile_counts(v, pool1[:, sl4 * 512:(sl4 + 1) * 512], mh0)
                        q2_tile_spill(mh0)
                        s0cnt_sn[mh0] = p.c["s_ev"]
        MM1_DONE = p.c["s_sp"]
        if "d_t1" in dbg_d:
            if True:
                blk = BLK
                @blk.gpsimd
                def _(g):
                    p.wait(g, "s_ev")
                    p.wait(g, "s_sc")
                    p.wait(g, "s_sp")
                    p.inc("s_gp", g.dma_start(out=dbg_d["d_t1"][:, :],
                                              in_=t1_sb[:, 0:4096]), 16)
                    p.inc("s_gp", g.dma_start(out=dbg_d["d_n1T"][:, :],
                                              in_=n1T[:, :]), 16)
                    p.inc("s_gp", g.dma_start(out=dbg_d["d_gsA"][:, :],
                                              in_=gsA[:, 0:4096]), 16)


        cross_sum_cols(wacc[:, 8:14], qrow[0:1, 72:78], 6)
        if True:
            blk = BLK
            @blk.vector
            def _(v):
                p.wait(v, "s_ev")
                v_ss(v, qrow[0:1, 15:16], qrow[0:1, 2:3], -1.0, None, ALU.add)
                p.inc("s_ev", v.tensor_copy(qrow[0:1, 64:70], qrow[0:1, 40:46]))
        LS4.close()
        LS4t = ExitStack()
        pool2 = LS4t.enter_context(nc.sbuf_tensor("pool2", [128, 8192], F32))
        if True:
            blk = BLK
            @blk.sync
            def _(sy):
                p.wait(sy, "s_ps0")
                p.wait(sy, "s_ps1")
                p.wait(sy, "s_gs0")
                p.wait(sy, "s_gs1")
                p.inc("s_ld", sy.dma_start(out=pool2[:, 0:4096],
                                           in_=pool2_d[:, 0:4096]), 16)
                p.inc("s_ld", sy.dma_start(out=pool2[:, 4096:8192],
                                           in_=pool2_d[:, 4096:8192]), 16)

            @blk.vector
            def _(v):
                p.wait(v, "s_ld")
                p.inc("s_ev", v.engine_nop())
        emit_quantile_tail(1, pool2[:, :], 8192, cc_gc[1], 15, 3, qcol[:, 7:8])
        if True:
            blk = BLK
            @blk.vector
            def _(v):
                p.wait(v, "s_ev")
                p.inc("s_ev", v.reciprocal(qrow[0:1, 21:22], qrow[0:1, 34:35]))
                fence(v)
                v_tt(v, qrow[0:1, 22:23], qrow[0:1, 12:13], qrow[0:1, 21:22], ALU.mult)
        bcast_row_to_cols(qrow[0:1, 22:23], qcol[:, 9:10], 1)
        LS4t.close()

        LS4c = ExitStack()
        sb4c = lambda name, shape, dt: LS4c.enter_context(nc.sbuf_tensor(name, shape, dt))
        t2_sb = sb4c("t2_sb", [128, 32 * C], BF16)
        gbuf2 = sb4c("gbuf2", [128, 2 * 512], F32)
        n2s = sb4c("n2s", [128, 32 * SLAB], BF16)
        p4s = sb4c("p4s", [128, 2 * C], F32)
        x2b = sb4c("x2b", [128, C], F32)
        for ti in range(32):
            if True:
                blk = BLK
                @blk.sync
                def _(sy, ti=ti):
                    p.wait(sy, "s_ev")
                    p.inc("s_ld", sy.dma_start(out=qtr[:, 0:C],
                                               in_=w2t_d[ti * 128:(ti + 1) * 128, :]), 16)

                @blk.vector
                def _(v, ti=ti):
                    p.wait(v, "s_ld")
                    p.inc("s_ev", v.tensor_scalar(qtr[:, 0:C], qtr[:, 0:C],
                                                  qcol[:, 14:15], 1.0, op0=ALU.mult,
                                                  op1=ALU.min))
                    fence(v)
                    p.inc("s_ev", v.tensor_scalar(qtr[:, 0:C], qtr[:, 0:C],
                                                  -1.0, None, op0=ALU.max))
                    fence(v)
                    p.inc("s_ev", v.tensor_scalar(witmp[:, 0:C], qtr[:, 0:C],
                                                  0.0, None, op0=ALU.add))
                    fence(v)
                    p.inc("s_ev", v.tensor_scalar(t2_sb[:, ti * C:(ti + 1) * C],
                                                  witmp[:, 0:C], 0.0, None, op0=ALU.add))
        for s in range(NSLAB):
            for mh in range(32):
                cur = mh % 2
                if True:
                    blk = BLK
                    @blk.sync
                    def _(sy, s=s, mh=mh, cur=cur):
                        p.wait(sy, "s_ev")
                        p.wait(sy, "s_gs%d" % ((s * 32 + mh) % 2), gspill_sn[s * 32 + mh])
                        p.inc("s_ld", sy.dma_start(
                            out=gbuf2[:, cur * 512:(cur + 1) * 512],
                            in_=g_sp[s, mh * 128:(mh + 1) * 128, :]), 16)

                    @blk.vector
                    def _(v, mh=mh, cur=cur):
                        p.wait(v, "s_ld")
                        gb = gbuf2[:, cur * 512:(cur + 1) * 512]
                        p.inc("s_ev", v.tensor_scalar(qtr[:, 0:512], gb,
                                                      qcol[:, 7:8], 127.0,
                                                      op0=ALU.mult, op1=ALU.min))
                        fence(v)
                        p.inc("s_ev", v.tensor_scalar(qtr[:, 0:512], qtr[:, 0:512],
                                                      -128.0, None, op0=ALU.max))
                        fence(v)
                        p.inc("s_ev", v.tensor_scalar(witmp[:, 0:512], qtr[:, 0:512],
                                                      0.0, None, op0=ALU.add))
                        fence(v)
                        p.inc("s_ev", v.tensor_scalar(n2s[:, mh * SLAB:(mh + 1) * SLAB],
                                                      witmp[:, 0:512], 0.0, None,
                                                      op0=ALU.add))
            N2_DONE = p.c["s_ev"]
            for ccq in range(4):
                cc = s * 4 + ccq
                if True:
                    blk = BLK
                    @blk.tensor
                    def _(t, ccq=ccq):
                        p.wait(t, "s_ev")
                        p.wait(t, "s_sc")
                        last = None
                        for ob in range(2):
                            for hid in range(32):
                                last = t.matmul(
                                    ps[2 + ob][:, :],
                                    n2s[:, hid * SLAB + ccq * 128:
                                        hid * SLAB + (ccq + 1) * 128],
                                    t2_sb[:, hid * C + ob * 512:hid * C + (ob + 1) * 512],
                                    start=(hid == 0), stop=(hid == 31))
                        p.inc("s_mm", last)

                    @blk.sync
                    def _(sy, cc=cc):
                        p.wait(sy, "s_ev")
                        p.wait(sy, "s_sp", h1sp_sn[cc])
                        p.inc("s_ld", sy.dma_start(out=x2b[:, :],
                                                   in_=x2_sp[cc * 128:(cc + 1) * 128, :]), 16)

                    @blk.vector
                    def _(v):
                        p.wait(v, "s_mm")
                        p.wait(v, "s_ld")
                        p.inc("s_ev", v.tensor_scalar(p4s[:, 0:512], ps[2][:, :],
                                                      qcol[:, 9:10], None, op0=ALU.mult))
                        fence(v)
                        p.inc("s_ev", v.tensor_scalar(p4s[:, 512:1024], ps[3][:, :],
                                                      qcol[:, 9:10], None, op0=ALU.mult))
                        fence(v)

                    @blk.gpsimd
                    def _(g, cc=cc):
                        if cc == NCH - 1 and "d_raw" in dbg_d:
                            p.wait(g, "s_ev")
                            p.inc("s_gp", g.dma_start(out=dbg_d["d_raw"][:, :],
                                                      in_=p4s[:, 0:1024]), 16)

                    @blk.vector
                    def _(v):
                        p.wait(v, "s_gp")
                        p.inc("s_ev", v.engine_nop())
                        p.inc("s_ev", v.reduce_sum(stats[:, 0:1], p4s[:, 0:1024],
                                                   axis=AX.X))
                        fence(v)
                        p.inc("s_ev", v.tensor_scalar(stats[:, 0:1], stats[:, 0:1],
                                                      1.0 / C, None, op0=ALU.mult))
                        fence(v)
                        p.inc("s_ev", v.tensor_scalar(p4s[:, 0:1024], p4s[:, 0:1024],
                                                      stats[:, 0:1], None,
                                                      op0=ALU.subtract))

                    @blk.vector
                    def _(v):
                        p.wait(v, "s_ev")
                        p.wait(v, "s_out")
                        p.inc("s_ev", v.tensor_tensor(p4s[:, 1024:2048], p4s[:, 0:1024],
                                                      p4s[:, 0:1024], op=ALU.mult))
                        fence(v)
                        p.inc("s_ev", v.reduce_sum(stats[:, 1:2], p4s[:, 1024:2048],
                                                   axis=AX.X))
                        fence(v)
                        p.inc("s_ev", v.tensor_scalar(stats[:, 1:2], stats[:, 1:2],
                                                      1.0 / C, 1e-5, op0=ALU.mult,
                                                      op1=ALU.add))

                    @blk.vector
                    def _(v):
                        p.wait(v, "s_ev")
                        p.inc("s_ev", v.reciprocal(stats[:, 1:2], stats[:, 1:2]))

                    @blk.scalar
                    def _(sc):
                        p.wait(sc, "s_ev")
                        p.inc("s_sc", sc.activation(stats[:, 1:2], stats[:, 1:2], AF.Sqrt))

                    @blk.vector
                    def _(v):
                        p.wait(v, "s_sc")
                        p.wait(v, "s_out")
                        p.inc("s_ev", v.tensor_scalar(p4s[:, 0:1024], p4s[:, 0:1024],
                                                      stats[:, 1:2], None, op0=ALU.mult))
                        fence(v)
                        p.inc("s_ev", v.tensor_tensor(p4s[:, 1024:2048], x2b[:, :],
                                                      p4s[:, 0:1024], op=ALU.add))

                    @blk.sync
                    def _(sy, cc=cc):
                        p.wait(sy, "s_ev")
                        p.inc("s_out", sy.dma_start(out=out_d[cc * 128:(cc + 1) * 128, :],
                                                    in_=p4s[:, 1024:2048]), 16)
        if "d_qrow" in dbg_d:
            if True:
                blk = BLK
                @blk.gpsimd
                def _(g):
                    p.wait(g, "s_ev")
                    p.wait(g, "s_out")
                    p.inc("s_gp", g.dma_start(out=dbg_d["d_qrow"][:, :],
                                              in_=qrow[0:1, :]), 16)
                    p.inc("s_gp", g.dma_start(out=dbg_d["d_gt"][:, :],
                                              in_=g_sp[0, 0:128, :]), 16)
                    p.inc("s_gp", g.dma_start(out=dbg_d["d_n2"][:, :],
                                              in_=n2s[:, 0:512]), 16)
                    p.inc("s_gp", g.dma_start(out=dbg_d["d_t2"][:, :],
                                              in_=t2_sb[:, 0:1024]), 16)
                    p.inc("s_gp", g.dma_start(out=dbg_d["d_fout"][:, :],
                                              in_=p4s[:, 0:2048]), 16)

        LS4c.close()
        LS4s.close()
        LS3h.close()
        LSattc.close()


    if STAGE < 5:
        if True:
            blk = BLK
            @blk.vector
            def _(v):
                p.inc("s_ev", v.memset(zbuf[:, :], 0.0))

            @blk.sync
            def _(sy):
                p.wait(sy, "s_sp")
                p.wait(sy, "s_gp")
                p.wait(sy, "s_ev")
                for i in range(NT // 128):
                    p.inc("s_out", sy.dma_start(out=out_d[i * 128:(i + 1) * 128, :],
                                                in_=zbuf[:, 0:C]), 16)
                p.wait(sy, "s_out")
    else:
        if True:
            blk = BLK
            @blk.sync
            def _(sy):
                p.wait(sy, "s_sp")
                p.wait(sy, "s_gp")
                p.wait(sy, "s_out")

    es.close()
    return nc, [nm for nm, _, _ in dbg]


_cache = {}


def host_inputs(inputs, core):
    b, half = core // 2, core % 2
    x = inputs["L0_attn_x"]
    ut = np.triu(np.ones((128, 128), np.float32), 1)
    hselT = np.zeros((16, C), np.float32)
    for hh in range(H):
        hselT[hh, hh * D:(hh + 1) * D] = 1.0
    n1 = B * T * C
    n2 = B * T * HID
    r1, r2 = 0.995 * (n1 - 1), 0.995 * (n2 - 1)
    qtarg = np.zeros((1, 16), np.float32)
    qtarg[0, 0] = n1 - float(np.floor(r1))
    qtarg[0, 1] = r1 - np.floor(r1)
    qtarg[0, 2] = n2 - float(np.floor(r2))
    qtarg[0, 3] = r2 - np.floor(r2)
    qtarg[0, 4] = float(n1)
    qtarg[0, 5] = float(n2)
    return {
        "xin": np.ascontiguousarray(x[b, half * NT:(half + 1) * NT, :]),
        "wq": inputs["L0_attn_Wq"], "wk": inputs["L0_attn_Wk"],
        "wv": inputs["L0_attn_Wv"], "wo": inputs["L0_attn_Wo"],
        "w1t": np.ascontiguousarray(inputs["L0_ffn_w1"].T),
        "w2t": np.ascontiguousarray(inputs["L0_ffn_w2"].T),
        "alpha": np.ascontiguousarray(inputs["L0_attn_alpha_raw"].reshape(1, H)),
        "beta": np.ascontiguousarray(inputs["L0_attn_beta_raw"].reshape(1, H)),
        "lng": np.ascontiguousarray(inputs["L0_ffn_ln_g"].reshape(1, C)),
        "lnb": np.ascontiguousarray(inputs["L0_ffn_ln_b"].reshape(1, C)),
        "identin": np.eye(128, dtype=np.float32),
        "ltmin": np.tril(np.ones((128, 128), np.float32), -1),
        "utm4in": np.ascontiguousarray(np.tile(ut, (1, 4))),
        "hselTin": hselT,
        "hselin": np.ascontiguousarray(hselT.T),
        "onesin": np.ones((128, 128), np.float32),
        "selcin": np.full((64, 1), float(half), np.float32),
        "eoin": np.ascontiguousarray(np.concatenate([
            np.repeat([[1.0], [0.0]], 64, 0), np.repeat([[0.0], [1.0]], 64, 0)],
            1).astype(np.float32)),
        "qtargin": qtarg,
    }


def kernel(**inputs):
    global _dbg_out
    key = tuple(nm for nm, _, _ in DEBUG)
    if key not in _cache:
        _cache[key] = build(DEBUG)
    nc, dbg_names = _cache[key]
    in_maps = [host_inputs(inputs, c) for c in range(8)]
    res = run_bass_kernel_spmd(nc, in_maps, core_ids=list(range(8)))
    if dbg_names:
        _dbg_out = [{nm: r[nm] for nm in dbg_names} for r in res.results]
    out = np.empty((B, T, C), np.float32)
    for c in range(8):
        b, half = c // 2, c % 2
        out[b, half * NT:(half + 1) * NT, :] = res.results[c]["out"]
    return out
